# version 12
# baseline (speedup 1.0000x reference)
import numpy as np
import concourse.bass as bass
import concourse.mybir as mybir
from concourse.bass_utils import run_bass_kernel_spmd

F32 = mybir.dt.float32
BF16 = mybir.dt.bfloat16
AF = mybir.ActivationFunctionType
ALU = mybir.AluOpType

NCORES = 8
B, T, C = 32, 2048, 1024
NSEQ = B // NCORES
NCH = C // 128
NT = 4
TW = 512
NG = 8
NPAIR = 8
EPS = 1e-5
LN_DVE_T = 25
STAB = 1e-6

LN_XB = [c * 1024 for c in range(8)]
LN_SQ = [8192 + c * 1024 for c in range(8)]
LN_ROW0 = 16384
LN_ROW1 = 18432
LN_TS = [20480, 22528]
AA_QSC = [0, 1024, 2048, 3072]
AA_KSC = [4096, 5120, 6144, 7168]
AA_PHIK = [8192, 8704]
AA_VTK = [9216, 9728]
AA_KVBD = 24576
AA_KSUM = 26624
AB_ATT = 0
AB_NRM = 8192
AB_TSB = [12288, 14336]
AB_TSB2 = [16384, 18432]
FF_XR = 0
FF_XK = 8192
FF_KACT = 16384
FF_DD = [24576, 25600]
FF_RSB = [26624, 28672]
FF_TMP = [30720, 24576]
BP_COS = 0
BP_SIN = 4096
BP_STAGE = 8192
BP_LQ = 8192
BP_LKV = 12288
BP_KREP = 16384
BS_VTOK = 0
BS_PA0 = 8320
BS_PB0 = 11392
BS_OHSBA = 14464
BW_TSB2 = [0, 2048]
ST_F32 = [0, 4096, 8192, 12288]
ST_BF = [16384, 18432, 20480, 22528]

WA_WQKV = [0, 6144]
WA_WOUT = 0
WF_K = [0, 2048]
WF_R = [4096, 6144]
WF_V = [8192, 10240]
WB_WQA = 0
WB_WQB = 1024
WB_WKB = 3072
WB_WVB = 4096
WB_WKVA = 4608
WB_MASKD = 5120
WB_QSBT = [5632, 6656]
WB_RT1 = 7680
WB_RT2 = 9728
WB_RECA = 11776
WB_RECB = 12800
WB_DROW = 13824


class EngP:

    def __init__(self, e):
        object.__setattr__(self, "e", e)
        object.__setattr__(self, "last", None)

    def __getattr__(self, name):
        f = getattr(object.__getattribute__(self, "e"), name)
        if not callable(f):
            return f

        def wrap(*a, **k):
            r = f(*a, **k)
            if hasattr(r, "then_inc"):
                object.__setattr__(self, "last", r)
            return r
        return wrap


class BW:

    def __init__(self, blk):
        self._blk = blk

    def __getattr__(self, name):
        dec = getattr(self._blk, name)

        def mydec(f):
            return dec(lambda e: f(EngP(e)))
        return mydec


class Ctr:
    def __init__(self, h, step=1):
        self.h = h
        self.v = 0
        self.step = step

    def inc(self, x, k=None):
        k = self.step if k is None else k
        if isinstance(x, EngP):
            last = object.__getattribute__(x, "last")
            if last is not None:
                last.then_inc(self.h, k)
            else:
                object.__getattribute__(x, "e").sem_inc(self.h, k)
        elif hasattr(x, "then_inc"):
            x.then_inc(self.h, k)
        else:
            x.sem_inc(self.h, k)
        self.v += k
        return self.v

    def pad(self, engp, k=None):
        k = self.step if k is None else k
        e = object.__getattribute__(engp, "e") if isinstance(engp, EngP) else engp
        e.sem_inc(self.h, k)
        self.v += k
        return self.v


class MK:
    def __init__(self, tap=None):
        self.tap = tap
        nc = bass.Bass("TRN2", target_bir_lowering=False,
                       detect_race_conditions=False)
        self.nc = nc
        self.X = nc.dram_tensor("x", [NSEQ, C, T], F32, kind="ExternalInput").ap()
        self.Y = nc.dram_tensor("y", [NSEQ, C, T], F32, kind="ExternalOutput").ap()
        self.WQKV = nc.dram_tensor("wqkv", [C, 3 * C], F32, kind="ExternalInput").ap()
        self.RF = nc.dram_tensor("rf", [NG, 128, 128], F32, kind="ExternalInput").ap()
        self.WOUTA = nc.dram_tensor("wouta", [C, C], F32, kind="ExternalInput").ap()
        self.WQA = nc.dram_tensor("wqa", [C, 64], F32, kind="ExternalInput").ap()
        self.WQB = nc.dram_tensor("wqb", [64, C], F32, kind="ExternalInput").ap()
        self.WKVA = nc.dram_tensor("wkva", [C, 32], F32, kind="ExternalInput").ap()
        self.WKB = nc.dram_tensor("wkb", [32, 512], F32, kind="ExternalInput").ap()
        self.WVB = nc.dram_tensor("wvb", [32, 256], F32, kind="ExternalInput").ap()
        self.WOUTB = nc.dram_tensor("woutb", [C, C], F32, kind="ExternalInput").ap()
        self.WRT = nc.dram_tensor("wrt", [C, C], F32, kind="ExternalInput").ap()
        self.WKT = nc.dram_tensor("wkt", [C, C], F32, kind="ExternalInput").ap()
        self.WVT = nc.dram_tensor("wvt", [C, C], F32, kind="ExternalInput").ap()
        self.BMU = nc.dram_tensor("bmu", [128, 32], F32, kind="ExternalInput").ap()
        self.COSM = nc.dram_tensor("cosm", [128, T], F32, kind="ExternalInput").ap()
        self.SINM = nc.dram_tensor("sinm", [128, T], F32, kind="ExternalInput").ap()
        self.S128T = nc.dram_tensor("s128t", [128, 128], F32, kind="ExternalInput").ap()
        self.MASKD = nc.dram_tensor("maskd", [128, 128], F32, kind="ExternalInput").ap()
        self.SEL2 = nc.dram_tensor("sel2", [2, 128], F32, kind="ExternalInput").ap()
        self.WQKV_S = nc.dram_tensor("wqkv_s", [NG, NCH, 128, 384], BF16).ap()
        self.WOUTA_S = nc.dram_tensor("wouta_s", [C, C], BF16).ap()
        self.WOUTB_S = nc.dram_tensor("woutb_s", [C, C], BF16).ap()
        self.WRT_S = nc.dram_tensor("wrt_s", [C, C], BF16).ap()
        self.WKT_S = nc.dram_tensor("wkt_s", [C, C], BF16).ap()
        self.WVT_S = nc.dram_tensor("wvt_s", [C, C], BF16).ap()
        self.TAPX = None
        if tap is not None:
            self.TAPX = nc.dram_tensor("tapx", [C, T], F32, kind="ExternalOutput").ap()

    def scf(self, off, p0, np_, w):
        assert off % 4 == 0 and off + 0 <= 32768
        return self.scr[p0:p0 + np_, off // 4: off // 4 + w]

    def scb(self, off, p0, np_, w):
        assert off % 2 == 0
        return self.scr.ap().bitcast(BF16)[p0:p0 + np_, off // 2: off // 2 + w]

    def wab(self, off, p0, np_, w):
        assert off % 2 == 0
        return self.warena[p0:p0 + np_, off // 2: off // 2 + w]

    def waf(self, off, p0, np_, w):
        assert off % 4 == 0
        return self.warena.ap().bitcast(F32)[p0:p0 + np_, off // 4: off // 4 + w]

    def xch(self, c, t0=0, w=T):
        return self.x_sb[:, c * T + t0: c * T + t0 + w]

    def lnch(self, c, t0=0, w=T, shift=0, p0=0, np_=128):
        base = c * (T + 1) + 1 - shift + t0
        return self.lnx[p0:p0 + np_, base: base + w]

    def bgch(self, c, t0=0, w=T, p0=0, np_=128):
        return self.biga[p0:p0 + np_, c * T + t0: c * T + t0 + w]

    def build(self):
        nc = self.nc
        from contextlib import ExitStack
        with ExitStack() as st:
            x_sb = st.enter_context(nc.sbuf_tensor("x_sb", [128, NCH * T], F32))
            lnx = st.enter_context(nc.sbuf_tensor("lnx", [128, NCH * (T + 1)], BF16))
            biga = st.enter_context(nc.sbuf_tensor("biga", [128, NCH * T], BF16))
            warena = st.enter_context(nc.sbuf_tensor("warena", [128, 8192], BF16))
            scr = st.enter_context(nc.sbuf_tensor("scr", [128, 8192], F32))
            rf_bf = st.enter_context(nc.sbuf_tensor("rf_bf", [128, NG * 128], BF16))
            s128t_bf = st.enter_context(nc.sbuf_tensor("s128t_bf", [128, 128], BF16))
            sel2_f = st.enter_context(nc.sbuf_tensor("sel2_f", [2, 128], F32))
            ones1f = st.enter_context(nc.sbuf_tensor("ones1f", [1, 128], F32))
            ones1b = st.enter_context(nc.sbuf_tensor("ones1b", [1, 64], BF16))
            onesc_bf = st.enter_context(nc.sbuf_tensor("onesc_bf", [128, 8], BF16))
            bmu_sb = st.enter_context(nc.sbuf_tensor("bmu_sb", [128, 32], F32))
            cst = st.enter_context(nc.sbuf_tensor("cst", [128, 2], F32))
            maskd_bf = st.enter_context(nc.sbuf_tensor("maskd_bf", [128, 128], BF16))
            recs_bf = st.enter_context(nc.sbuf_tensor("recs_bf", [1, 1024], BF16))
            drow_f = st.enter_context(nc.sbuf_tensor("drow_f", [1, 512], F32))
            self.maskd_bf, self.recs_bf, self.drow_f = maskd_bf, recs_bf, drow_f
            ohsb_t = st.enter_context(nc.sbuf_tensor("ohsb_t", [64, 512], BF16))
            self.ohsb_t = ohsb_t
            nc.const_aps.register(F32, EPS, cst[:, 0:1])
            nc.const_aps.register(F32, STAB, cst[:, 1:2])
            self.cst = cst
            psA = st.enter_context(nc.psum_tensor("psA", [128, 512], F32))
            psB = st.enter_context(nc.psum_tensor("psB", [128, 512], F32))
            psC = st.enter_context(nc.psum_tensor("psC", [128, 512], F32))
            psD = st.enter_context(nc.psum_tensor("psD", [128, 512], F32))
            psW = st.enter_context(nc.psum_tensor("psW", [128, 2048], F32))
            h_dmx = st.enter_context(nc.semaphore("dmx"))
            h_dmk0 = st.enter_context(nc.semaphore("dmk0"))
            h_dmk1 = st.enter_context(nc.semaphore("dmk1"))
            h_dmw2 = st.enter_context(nc.semaphore("dmw2"))
            h_dmf0 = st.enter_context(nc.semaphore("dmf0"))
            h_dmf1 = st.enter_context(nc.semaphore("dmf1"))
            h_dmf2 = st.enter_context(nc.semaphore("dmf2"))
            h_dmf3 = st.enter_context(nc.semaphore("dmf3"))
            h_dms0 = st.enter_context(nc.semaphore("dms0"))
            h_dms1 = st.enter_context(nc.semaphore("dms1"))
            h_dmso0 = st.enter_context(nc.semaphore("dmso0"))
            h_dmso1 = st.enter_context(nc.semaphore("dmso1"))
            h_dmc = st.enter_context(nc.semaphore("dmc"))
            h_dmo = st.enter_context(nc.semaphore("dmo"))
            h_spe = st.enter_context(nc.semaphore("spe"))
            h_sact = st.enter_context(nc.semaphore("sact"))
            h_sdve = st.enter_context(nc.semaphore("sdve"))
            h_spl = st.enter_context(nc.semaphore("spl"))
            h_spool = st.enter_context(nc.semaphore("spool"))
            self.x_sb, self.lnx, self.biga = x_sb, lnx, biga
            self.warena, self.scr = warena, scr
            self.rf_bf, self.s128t_bf, self.sel2_f = rf_bf, s128t_bf, sel2_f
            self.ones1f, self.ones1b, self.onesc_bf = ones1f, ones1b, onesc_bf
            self.bmu_sb = bmu_sb
            self.psA, self.psB, self.psC, self.psD, self.psW = psA, psB, psC, psD, psW
            self.h_dmx, self.h_dmw2, self.h_dmc, self.h_dmo = h_dmx, h_dmw2, h_dmc, h_dmo
            self.h_dmk = [h_dmk0, h_dmk1]
            self.h_dmf = [h_dmf0, h_dmf1, h_dmf2, h_dmf3]
            self.h_dms = [h_dms0, h_dms1]
            self.h_dmso = [h_dmso0, h_dmso1]
            self.h_spe, self.h_sact, self.h_sdve, self.h_spl = h_spe, h_sact, h_sdve, h_spl
            self.dmx, self.dmw2, self.dmc, self.dmo = (Ctr(h_dmx, 16), Ctr(h_dmw2, 16),
                                                       Ctr(h_dmc, 16), Ctr(h_dmo, 16))
            self.dmk = [Ctr(h_dmk0, 16), Ctr(h_dmk1, 16)]
            self.dmf = [Ctr(h_dmf0, 16), Ctr(h_dmf1, 16), Ctr(h_dmf2, 16), Ctr(h_dmf3, 16)]
            self.dms = [Ctr(h_dms0, 16), Ctr(h_dms1, 16)]
            self.dmso = [Ctr(h_dmso0, 16), Ctr(h_dmso1, 16)]
            self.spe, self.sact, self.sdve, self.spl = (Ctr(h_spe), Ctr(h_sact),
                                                        Ctr(h_sdve), Ctr(h_spl))
            self.h_spool = h_spool
            self.spool = Ctr(h_spool)
            self.emit_staging()
            self.emit_consts()
            for s in range(NSEQ):
                self.emit_seq(s)
            with nc.Block("fin") as _b:
                blk = BW(_b)
                @blk.gpsimd
                def _(g):
                    g.wait_ge(self.h_dmo, self.dmo.v)

                @blk.sync
                def _(sync):
                    sync.wait_ge(self.h_dmo, self.dmo.v)
        nc.finalize()
        return nc

    def emit_staging(self):
        nc = self.nc
        jobs = []
        for c in range(NCH):
            for sub in range(3):
                jobs.append((self.WQKV[c * 128:(c + 1) * 128, sub * 1024:(sub + 1) * 1024],
                             self.WQKV_S[:, c, :, sub * 128:(sub + 1) * 128], "qkv"))
        for W, WS in ((self.WOUTA, self.WOUTA_S), (self.WOUTB, self.WOUTB_S),
                      (self.WRT, self.WRT_S), (self.WKT, self.WKT_S),
                      (self.WVT, self.WVT_S)):
            for c in range(NCH):
                jobs.append((W[c * 128:(c + 1) * 128, :],
                             WS[c * 128:(c + 1) * 128, :], "row"))
        v0 = self.sdve.v
        h_sdve, sdve = self.h_sdve, self.sdve
        ld_sem, ld_h = self.dmf, self.h_dmf
        st_sem = [self.dms[0], self.dms[1], self.dmso[0], self.dmso[1]]
        st_h = [self.h_dms[0], self.h_dms[1], self.h_dmso[0], self.h_dmso[1]]
        sl0 = [c.v for c in ld_sem]
        so0 = [c.v for c in st_sem]
        scf, scb = self.scf, self.scb
        with nc.Block("staging") as _b:
            blk = BW(_b)
            @blk.sync
            def _(sync):
                for i, (src, dst, kind) in enumerate(jobs):
                    if i >= 4:
                        sync.wait_ge(h_sdve, v0 + (i - 4) + 1)
                    ld_sem[i % 4].inc(sync.dma_start(scf(ST_F32[i % 4], 0, 128, 1024), src))

            @blk.vector
            def _(v):
                for i, (src, dst, kind) in enumerate(jobs):
                    v.wait_ge(ld_h[i % 4], sl0[i % 4] + 16 * (i // 4 + 1))
                    if i >= 4:
                        v.wait_ge(st_h[i % 4], so0[i % 4] + 16 * ((i - 4) // 4 + 1))
                    v.tensor_copy(scb(ST_BF[i % 4], 0, 128, 1024),
                                  scf(ST_F32[i % 4], 0, 128, 1024))
                    sdve.inc(v)

            @blk.gpsimd
            def _(g):
                for i, (src, dst, kind) in enumerate(jobs):
                    g.wait_ge(h_sdve, v0 + i + 1)
                    sb = scb(ST_BF[i % 4], 0, 128, 1024)
                    if kind == "qkv":
                        st_sem[i % 4].inc(g.dma_start(dst.transpose([1, 0, 2]),
                                                      sb.rearrange("p (g w) -> p g w", w=128)))
                    else:
                        st_sem[i % 4].inc(g.dma_start(dst, sb))

    def emit_consts(self):
        nc = self.nc
        h_dmc, dmc, sdve = self.h_dmc, self.dmc, self.sdve
        scf, scb = self.scf, self.scb
        with nc.Block("consts") as _b:
            blk = BW(_b)
            @blk.sync
            def _(sync):
                sync.wait_ge(self.h_sdve, self.sdve.v)
                dmc.inc(sync.dma_start(
                    scf(0, 0, 128, 1024).rearrange("p (g w) -> p g w", w=128),
                    self.RF.transpose([1, 0, 2])))
                dmc.inc(sync.dma_start(self.sel2_f[:, :], self.SEL2))
                dmc.inc(sync.dma_start(scf(4096, 0, 128, 128), self.S128T))
                dmc.inc(sync.dma_start(self.bmu_sb[:, :], self.BMU))
                dmc.inc(sync.dma_start(scf(5120, 0, 128, 128), self.MASKD))

            @blk.vector
            def _(v):
                v.wait_ge(h_dmc, dmc.v)
                v.wait_ge(self.h_dmso[0], self.dmso[0].v)
                v.wait_ge(self.h_dmso[1], self.dmso[1].v)
                v.memset(self.cst[:, 0:1], EPS)
                v.memset(self.cst[:, 1:2], STAB)
                v.tensor_copy(self.rf_bf[:, :], scf(0, 0, 128, 1024))
                v.tensor_copy(self.s128t_bf[:, :], scf(4096, 0, 128, 128))
                v.tensor_copy(self.maskd_bf[:, :], scf(5120, 0, 128, 128))
                v.memset(self.ones1f[:, :], 1.0)
                v.memset(self.ones1b[:, :], 1.0)
                v.memset(self.onesc_bf[:, :], 1.0)
                v.memset(self.lnx.ap().rearrange("p (c t) -> p c t", t=T + 1)[:, :, 0:1], 0.0)
                sdve.inc(v)

    def emit_seq(self, s):
        nc = self.nc
        dmx, dmo = self.dmx, self.dmo
        h_dmo = self.h_dmo
        with nc.Block(f"s{s}_loadx") as _b:
            blk = BW(_b)
            @blk.sync
            def _(sync):
                if s > 0:
                    sync.wait_ge(h_dmo, dmo.v)
                for c in range(NCH):
                    dmx.inc(sync.dma_start(self.xch(c),
                                           self.X[s, c * 128:(c + 1) * 128, :]))
                self.dmk[0].inc(sync.dma_start(
                    self.wab(WA_WQKV[0], 0, 128, 3072).rearrange("p (c w) -> p c w", w=384),
                    self.WQKV_S[0].transpose([1, 0, 2])))
        self._xload = dmx.v
        self._aa_g0 = self.dmk[0].v

        if self.tap == 0:
            self.emit_tap(s)
            return
        self.emit_ln(s, 1)
        if self.tap == 5:
            self.emit_tap_lnx(s)
            return
        self.emit_attn_a(s)
        if self.tap == 1:
            self.emit_tap(s)
            return
        self.emit_ln(s, 2)
        self.emit_ffn(s, 1)
        if self.tap == 2:
            self.emit_tap(s)
            return
        self.emit_ln(s, 3)
        self.emit_attn_b(s)
        if self.tap == 3:
            self.emit_tap(s)
            return
        self.emit_ln(s, 4)
        self.emit_ffn(s, 2, store=True)
        if self.tap == 4:
            self.emit_tap(s)

    def emit_tap(self, s):
        if s != 0:
            return
        nc, dmo, h_dmo = self.nc, self.dmo, self.h_dmo
        h_dmx = self.h_dmx
        xload = self._xload
        with nc.Block("tap") as _b:
            blk = BW(_b)
            @blk.gpsimd
            def _(g):
                g.wait_ge(h_dmx, xload)
                for c in range(NCH):
                    dmo.inc(g.dma_start(self.TAPX[c * 128:(c + 1) * 128, :],
                                        self.xch(c)))
                g.wait_ge(h_dmo, dmo.v)

    def emit_tap_lnx(self, s):
        if s != 0:
            return
        nc, dmo, h_dmo, sdve, h_sdve = (self.nc, self.dmo, self.h_dmo,
                                        self.sdve, self.h_sdve)
        v0 = sdve.v
        o0 = self.dmo.v
        with nc.Block("taplnx") as _b:
            blk = BW(_b)
            @blk.vector
            def _(v):
                for c in range(NCH):
                    if c >= 2:
                        v.wait_ge(h_dmo, o0 + 16 * (c - 1))
                    v.tensor_copy(self.scf([0, 8192][c % 2], 0, 128, 2048),
                                  self.lnch(c))
                    sdve.inc(v)

            @blk.gpsimd
            def _(g):
                for c in range(NCH):
                    g.wait_ge(h_sdve, v0 + c + 1)
                    g.wait_ge(h_dmo, o0 + 16 * c)
                    dmo.inc(g.dma_start(self.TAPX[c * 128:(c + 1) * 128, :],
                                        self.scf([0, 8192][c % 2], 0, 128, 2048)))
                g.wait_ge(h_dmo, dmo.v)

    def emit_ln(self, s, which):
        nc = self.nc
        h_spool, spool = self.h_spool, self.spool
        LN_MEV, LN_REV = 24576, 26624
        h_spe, h_sact, h_sdve = self.h_spe, self.h_sact, self.h_sdve
        spe, sact, sdve = self.spe, self.sact, self.sdve
        scf, scb, xch, lnch = self.scf, self.scb, self.xch, self.lnch
        psA, psB, psC, psD, psW = self.psA, self.psB, self.psC, self.psD, self.psW
        onesc, ones1f = self.onesc_bf, self.ones1f
        h_dmx = self.h_dmx
        xload = self._xload
        a0, v0, p0 = sact.v, sdve.v, spe.v
        g0 = spool.v
        first = (which == 1)

        def bmean(t):
            return [psC[:, 0:512], psW[:, 0:512]][t % 2]

        def brstd(t):
            return [psD[:, 0:512], psW[:, 512:1024]][t % 2]

        with nc.Block(f"s{s}_ln{which}") as _b:
            blk = BW(_b)
            @blk.scalar
            def _(act):
                for t in range(NT):
                    for c in range(NCH):
                        bi = c
                        if first and t == 0 and c == 0:
                            act.wait_ge(h_dmx, xload)
                        act.copy(scb(LN_XB[bi], 0, 128, 512), xch(c, t * TW, TW))
                        sact.inc(act)
                    act.wait_ge(h_spe, p0 + t * 18 + 16)
                    act.activation(scf(LN_ROW0, 0, 1, 512), psA[0:1, 0:512],
                                   AF.Copy, scale=1.0 / 1024.0)
                    sact.inc(act)
                    act.wait_ge(h_sdve, v0 + t * LN_DVE_T + 9)
                    act.activation(scf(LN_ROW1, 0, 1, 512), scf(LN_ROW1, 0, 1, 512),
                                   AF.Ln, bias=EPS)
                    act.activation(scf(LN_ROW1, 0, 1, 512), scf(LN_ROW1, 0, 1, 512),
                                   AF.Exp, scale=-0.5)
                    sact.inc(act, 2)

            @blk.vector
            def _(v):
                for t in range(NT):
                    for c in range(NCH):
                        bi = c
                        v.wait_ge(h_sact, a0 + t * 11 + c + 1)
                        v.tensor_mul(scb(LN_SQ[bi], 0, 128, 512),
                                     scb(LN_XB[bi], 0, 128, 512),
                                     scb(LN_XB[bi], 0, 128, 512))
                        sdve.inc(v)
                    v.wait_ge(h_sact, a0 + t * 11 + 9)
                    v.scalar_tensor_tensor(scf(LN_ROW1, 0, 1, 512),
                                           scf(LN_ROW0, 0, 1, 512), 1.0,
                                           scf(LN_ROW0, 0, 1, 512), ALU.mult, ALU.mult)
                    v.wait_ge(h_spe, p0 + t * 18 + 16)
                    v.scalar_tensor_tensor(scf(LN_ROW1, 0, 1, 512), psB[0:1, 0:512],
                                           1.0 / 1024.0, scf(LN_ROW1, 0, 1, 512),
                                           ALU.mult, ALU.subtract)
                    sdve.inc(v)
                    for c in range(NCH):
                        ts = scf(LN_TS[c % 2], 0, 128, 512)
                        if c == 0:
                            v.wait_ge(h_spe, p0 + t * 18 + 17)
                        v.tensor_tensor(ts, xch(c, t * TW, TW), bmean(t),
                                        ALU.subtract)
                        if c == 0:
                            v.wait_ge(h_spe, p0 + t * 18 + 18)
                        v.tensor_tensor(lnch(c, t * TW, TW), ts, brstd(t),
                                        ALU.mult)
                        sdve.inc(v, 2)

            @blk.tensor
            def _(pe):
                for t in range(NT):
                    for c in range(NCH):
                        bi = c
                        pe.wait_ge(h_sact, a0 + t * 11 + c + 1)
                        pe.wait_ge(h_sdve, v0 + t * LN_DVE_T + c + 1)
                        if c == 0 and t > 0:
                            pe.wait_ge(h_sact, a0 + (t - 1) * 11 + 9)
                            pe.wait_ge(h_sdve, v0 + (t - 1) * 19 + 9)
                        pe.matmul(psA[0:1, 0:512], onesc[:, 0:1],
                                  scb(LN_XB[bi], 0, 128, 512),
                                  start=(c == 0), stop=(c == NCH - 1))
                        pe.matmul(psB[0:1, 0:512], onesc[:, 0:1],
                                  scb(LN_SQ[bi], 0, 128, 512),
                                  start=(c == 0), stop=(c == NCH - 1))
                        spe.inc(pe, 2)
                    pe.wait_ge(h_sact, a0 + t * 11 + 9)
                    if t >= 2:
                        pe.wait_ge(h_sdve, v0 + (t - 2) * LN_DVE_T + LN_DVE_T)
                        pass
                    pe.matmul(bmean(t), ones1f[:, :], scf(LN_ROW0, 0, 1, 512),
                              start=True, stop=True)
                    spe.inc(pe)
                    pe.wait_ge(h_sact, a0 + t * 11 + 11)
                    pe.matmul(brstd(t), ones1f[:, :], scf(LN_ROW1, 0, 1, 512),
                              start=True, stop=True)
                    spe.inc(pe)

        assert sact.v == a0 + NT * 11
        assert sdve.v == v0 + NT * LN_DVE_T
        assert spe.v == p0 + NT * 18
        
    def emit_attn_a(self, s):
        nc = self.nc
        h_spe, h_sact, h_sdve = self.h_spe, self.h_sact, self.h_sdve
        h_dmk, h_dmw2 = self.h_dmk, self.h_dmw2
        dmk, dmw2 = self.dmk, self.dmw2
        spe, sact, sdve = self.spe, self.sact, self.sdve
        scf, scb, wab, xch, lnch, bgch = (self.scf, self.scb, self.wab, self.xch,
                                          self.lnch, self.bgch)
        psA, psB, psC, psD, psW = self.psA, self.psB, self.psC, self.psD, self.psW
        rf_bf, sel2_f, bmu_sb = self.rf_bf, self.sel2_f, self.bmu_sb
        loads = {0: self._aa_g0}
        wout_cnt = [None]

        for gi in range(NG):
            wb = WA_WQKV[gi % 2]

            def wq(c, seg, wb=wb):
                return wab(wb, 0, 128, 3072).rearrange(
                    "p (c w) -> p c w", w=384)[:, c, seg * 128:(seg + 1) * 128]

            a0, v0, p0 = sact.v, sdve.v, spe.v
            dvo = 1 if gi == 0 else 0
            load_cnt = loads[gi]
            with nc.Block(f"s{s}_aa_g{gi}") as _b:
                blk = BW(_b)
                @blk.sync
                def _(sync):
                    if gi + 1 < NG:
                        dmk[(gi + 1) % 2].inc(sync.dma_start(
                            wab(WA_WQKV[(gi + 1) % 2], 0, 128, 3072)
                            .rearrange("p (c w) -> p c w", w=384),
                            self.WQKV_S[gi + 1].transpose([1, 0, 2])))
                        loads[gi + 1] = dmk[(gi + 1) % 2].v
                    else:
                        sync.wait_ge(h_spe, p0 + 60)
                        dmw2.inc(sync.dma_start(
                            wab(WA_WOUT, 0, 128, 8192).rearrange("p (c w) -> p c w", w=1024),
                            self.WOUTA_S.rearrange("(c p) w -> c p w", p=128)
                            .transpose([1, 0, 2])))
                        wout_cnt[0] = dmw2.v

                @blk.tensor
                def _(pe):
                    pe.wait_ge(h_dmk[gi % 2], load_cnt)
                    for t in range(NT):
                        pq = [psA, psC][t % 2]
                        pk = [psB, psD][t % 2]
                        if t >= 2:
                            pe.wait_ge(h_sact, a0 + (t - 2) * 2 + 2)
                        for c in range(NCH):
                            pe.matmul(pq[:, 0:512], wq(c, 0), lnch(c, t * TW, TW),
                                      start=(c == 0), stop=(c == NCH - 1))
                        spe.inc(pe)
                        for c in range(NCH):
                            pe.matmul(pk[:, 0:512], wq(c, 1), lnch(c, t * TW, TW),
                                      start=(c == 0), stop=(c == NCH - 1))
                        spe.inc(pe)
                    for t in range(NT):
                        pq = [psA, psC][t % 2]
                        wv = (t + 2) * 2 + 1 if t + 2 < NT else t * 2 + 1
                        pe.wait_ge(h_sact, a0 + wv)
                        if t >= 2:
                            pe.wait_ge(h_sdve, v0 + dvo + (t - 2) + 1)
                        pe.matmul(pq[:, 0:512], rf_bf[:, gi * 128:(gi + 1) * 128],
                                  scb(AA_QSC[t], 0, 128, 512), start=True, stop=True)
                        spe.inc(pe)
                    def kv_mm(j):
                        pe.wait_ge(h_sact, a0 + 8 + j * 2 + 2)
                        if gi == 0 and j < 2:
                            pe.wait_ge(h_sdve, v0 + 1)
                        pe.matmul(psW[:, 0:130], scb(AA_PHIK[j % 2], 0, 128, 128),
                                  scb(AA_VTK[j % 2], 0, 128, 130),
                                  start=(j == 0), stop=(j == 15))
                        spe.inc(pe)
                    for j in range(16):
                        t = j // 4
                        jj = j % 4
                        pe.wait_ge(h_sact, a0 + t * 2 + 2)
                        pv = [psA, psB][j % 2][:, 0:128]
                        pf = [psC, psD][j % 2][:, 0:128]
                        if j >= 2:
                            pe.wait_ge(h_sact, a0 + 8 + (j - 2) * 2 + 2)
                        elif j == 0:
                            pe.wait_ge(h_sdve, v0 + dvo + 3)
                        elif j == 1:
                            pe.wait_ge(h_sact, a0 + 6)
                        for c in range(NCH):
                            pe.matmul(pv, lnch(c, j * 128, 128), wq(c, 2),
                                      start=(c == 0), stop=(c == NCH - 1))
                        spe.inc(pe)
                        if j == 0:
                            pe.wait_ge(h_sdve, v0 + dvo + 4)
                        pe.matmul(pf, scb(AA_KSC[t], 0, 128, 512)[:, jj * 128:(jj + 1) * 128],
                                  rf_bf[:, gi * 128:(gi + 1) * 128], start=True, stop=True)
                        spe.inc(pe)
                        if j >= 1:
                            kv_mm(j - 1)
                    kv_mm(15)

                @blk.scalar
                def _(act):
                    for t in range(NT):
                        pq = [psA, psC][t % 2]
                        pk = [psB, psD][t % 2]
                        act.wait_ge(h_spe, p0 + t * 2 + 1)
                        act.copy(scb(AA_QSC[t], 0, 128, 512), pq[:, 0:512])
                        sact.inc(act)
                        act.wait_ge(h_spe, p0 + t * 2 + 2)
                        act.copy(scb(AA_KSC[t], 0, 128, 512), pk[:, 0:512])
                        sact.inc(act)
                    for j in range(16):
                        pv = [psA, psB][j % 2][:, 0:128]
                        pf = [psC, psD][j % 2][:, 0:128]
                        vj = 13 if j == 0 else 3 * j + 12
                        act.wait_ge(h_spe, p0 + vj)
                        act.copy(scb(AA_VTK[j % 2], 0, 128, 130)[:, 0:128], pv)
                        sact.inc(act)
                        act.wait_ge(h_spe, p0 + vj + 1)
                        act.activation(scb(AA_PHIK[j % 2], 0, 128, 128), pf, AF.Relu)
                        sact.inc(act)
                    act.wait_ge(h_spe, p0 + 12 + 48)
                    if gi == 0:
                        act.wait_ge(h_sdve, v0 + 1)
                    kvbd = scb(AA_KVBD, 0, 128, 1024)
                    ksum = scb(AA_KSUM, 0, 128, 16)
                    act.copy(kvbd[0:64, gi * 128: gi * 128 + 64], psW[0:64, 0:64])
                    act.copy(kvbd[64:128, gi * 128 + 64: gi * 128 + 128],
                             psW[64:128, 64:128])
                    act.copy(ksum[0:64, 2 * gi: 2 * gi + 1], psW[0:64, 128:129])
                    act.copy(ksum[64:128, 2 * gi + 1: 2 * gi + 2], psW[64:128, 128:129])
                    sact.inc(act, 4)

                @blk.vector
                def _(v):
                    if gi == 0:
                        v.memset(scb(AA_KVBD, 0, 128, 1024), 0.0)
                        v.memset(scb(AA_KSUM, 0, 128, 16), 0.0)
                        v.memset(scb(AA_VTK[0], 0, 128, 130)[:, 128:129], 1.0)
                        v.memset(scb(AA_VTK[1], 0, 128, 130)[:, 128:129], 1.0)
                        sdve.inc(v)
                    for t in range(NT):
                        pq = [psA, psC][t % 2]
                        v.wait_ge(h_spe, p0 + 8 + t + 1)
                        v.tensor_scalar_max(bgch(gi, t * TW, TW), pq[:, 0:512], 0.0)
                        sdve.inc(v)

            assert spe.v == p0 + 60
            assert sact.v == a0 + 44
            assert sdve.v == v0 + 4 + (1 if gi == 0 else 0)

        a0, v0, p0 = sact.v, sdve.v, spe.v
        with nc.Block(f"s{s}_aa_B") as _b:
            blk = BW(_b)
            def peV(k):
                t, gi = divmod(k, 8)
                return p0 + t * 32 + gi * 3

            def actV(k):
                t, gi = divmod(k, 8)
                return a0 + t * 32 + gi * 3

            @blk.tensor
            def _(pe):
                kvbd = scb(AA_KVBD, 0, 128, 1024)
                ksum = scb(AA_KSUM, 0, 128, 16)
                for t in range(NT):
                    for gi in range(NG):
                        k = t * 8 + gi
                        po = [psA, psC][gi % 2]
                        if k >= 2:
                            kk = k - 2
                            pe.wait_ge(h_sdve, v0 + (kk // 8) * 16 + (kk % 8) + 1)
                        pe.matmul(po[:, 0:512], kvbd[:, gi * 128:(gi + 1) * 128],
                                  bgch(gi, t * TW, TW), start=True, stop=True)
                        spe.inc(pe)
                        if k >= 1:
                            pe.wait_ge(h_sact, actV(k - 1) + 2)
                        pe.matmul(psB[0:2, 0:512], ksum[:, 2 * gi:2 * gi + 2],
                                  bgch(gi, t * TW, TW), start=True, stop=True)
                        spe.inc(pe)
                        pe.wait_ge(h_sact, actV(k) + 2)
                        pe.matmul(psD[:, 0:512], sel2_f[:, :], scf(AB_NRM, 0, 2, 512),
                                  start=True, stop=True)
                        spe.inc(pe)
                    pe.wait_ge(h_sdve, v0 + t * 16 + 8)
                    pe.wait_ge(h_dmw2, wout_cnt[0])
                    for d in range(NCH):
                        pw = psW[:, (d % 4) * 512:(d % 4) * 512 + 512]
                        k = t * 8 + d
                        if k >= 4:
                            kk = k - 4
                            pe.wait_ge(h_sdve, v0 + (kk // 8) * 16 + 8 + (kk % 8) + 1)
                        for c in range(NCH):
                            pe.matmul(pw, wab(WA_WOUT, 0, 128, 8192).rearrange(
                                "p (c w) -> p c w", w=1024)[:, c, d * 128:(d + 1) * 128],
                                scb(AB_ATT + c * 1024, 0, 128, 512),
                                start=(c == 0), stop=(c == NCH - 1))
                        spe.inc(pe)

            @blk.scalar
            def _(act):
                for t in range(NT):
                    for gi in range(NG):
                        k = t * 8 + gi
                        act.wait_ge(h_spe, peV(k) + 2)
                        act.activation(scf(AB_NRM, 0, 2, 512), psB[0:2, 0:512],
                                       AF.Ln, bias=STAB)
                        act.activation(scf(AB_NRM, 0, 2, 512), scf(AB_NRM, 0, 2, 512),
                                       AF.Exp, scale=-1.0)
                        sact.inc(act, 2)
                        act.wait_ge(h_spe, peV(k) + 3)
                        if k >= 2:
                            kk = k - 2
                            act.wait_ge(h_sdve, v0 + (kk // 8) * 16 + (kk % 8) + 1)
                        act.copy(scf(AB_TSB[gi % 2], 0, 128, 512), psD[:, 0:512])
                        sact.inc(act)
                    for d in range(NCH):
                        k = t * 8 + d
                        pw = psW[:, (d % 4) * 512:(d % 4) * 512 + 512]
                        act.wait_ge(h_spe, p0 + t * 32 + 24 + d + 1)
                        if k >= 2:
                            kk = k - 2
                            act.wait_ge(h_sdve, v0 + (kk // 8) * 16 + 8 + (kk % 8) + 1)
                        act.activation(scf(AB_TSB2[d % 2], 0, 128, 512), pw, AF.Identity,
                                       bias=bmu_sb[:, d:d + 1])
                        sact.inc(act)

            @blk.vector
            def _(v):
                for t in range(NT):
                    for gi in range(NG):
                        k = t * 8 + gi
                        po = [psA, psC][gi % 2]
                        v.wait_ge(h_sact, actV(k) + 3)
                        if t >= 1 and gi == 0:
                            v.wait_ge(h_spe, p0 + t * 32)
                        v.tensor_tensor(scb(AB_ATT + gi * 1024, 0, 128, 512),
                                        po[:, 0:512], scf(AB_TSB[gi % 2], 0, 128, 512),
                                        ALU.mult)
                        sdve.inc(v)
                    for d in range(NCH):
                        v.wait_ge(h_sact, a0 + t * 32 + 24 + d + 1)
                        v.tensor_tensor(xch(d, t * TW, TW), xch(d, t * TW, TW),
                                        scf(AB_TSB2[d % 2], 0, 128, 512), ALU.add)
                        sdve.inc(v)

        assert spe.v == p0 + NT * 32
        assert sact.v == a0 + NT * 32
        assert sdve.v == v0 + NT * 16

    def emit_ffn(self, s, which, store=False):
        nc = self.nc
        h_spe, h_sact, h_sdve = self.h_spe, self.h_sact, self.h_sdve
        h_dmf, dmf = self.h_dmf, self.dmf
        dmo, spe, sact, sdve = self.dmo, self.spe, self.sact, self.sdve
        scf, scb, wab, xch, lnch = self.scf, self.scb, self.wab, self.xch, self.lnch
        psA, psB, psC, psD = self.psA, self.psB, self.psC, self.psD
        bmu_sb, Y = self.bmu_sb, self.Y
        h_dmo = self.h_dmo
        MU_R = bmu_sb[:, 16:24]
        MU_K = bmu_sb[:, 24:32]

        def wsl(WS, d):
            return WS.rearrange("(c p) w -> c p w", p=128)[:, :, d * 128:(d + 1) * 128]

        for t in range(NT):
            a0, v0, p0 = sact.v, sdve.v, spe.v
            f0 = [dmf[b].v for b in range(4)]
            with nc.Block(f"s{s}_ffn{which}_t{t}") as _b:
                blk = BW(_b)
                def wk(b, c):
                    return wab(WF_K[b], 0, 128, 1024).rearrange("p (c w) -> p c w", w=128)[:, c, :]

                def wr(b, c):
                    return wab(WF_R[b], 0, 128, 1024).rearrange("p (c w) -> p c w", w=128)[:, c, :]

                def wv(b, c):
                    return wab(WF_V[b], 0, 128, 1024).rearrange("p (c w) -> p c w", w=128)[:, c, :]

                @blk.sync
                def _(sync):
                    for d in range(NCH):
                        if d >= 2:
                            sync.wait_ge(h_spe, p0 + (d - 2) + 1)
                        dmf[d % 2].inc(sync.dma_start(
                            wab(WF_K[d % 2], 0, 128, 1024)
                            .rearrange("p (c w) -> p c w", w=128),
                            wsl(self.WKT_S, d).transpose([1, 0, 2])))
                    for d in range(NCH):
                        if d >= 2:
                            sync.wait_ge(h_spe, p0 + 8 + (d - 2) * 2 + 2)
                        dmf[2 + d % 2].inc(sync.dma_start(
                            wab(WF_R[d % 2], 0, 128, 1024)
                            .rearrange("p (c w) -> p c w", w=128),
                            wsl(self.WRT_S, d).transpose([1, 0, 2])))
                        dmf[2 + d % 2].inc(sync.dma_start(
                            wab(WF_V[d % 2], 0, 128, 1024)
                            .rearrange("p (c w) -> p c w", w=128),
                            wsl(self.WVT_S, d).transpose([1, 0, 2])))

                @blk.vector
                def _(v):
                    for c in range(NCH):
                        dd = scb(FF_DD[c % 2], 0, 128, 512)
                        v.tensor_sub(dd, lnch(c, t * TW, TW), lnch(c, t * TW, TW, shift=1))
                        v.scalar_tensor_tensor(scb(FF_XK + c * 1024, 0, 128, 512), dd,
                                               MU_K[:, c:c + 1],
                                               lnch(c, t * TW, TW, shift=1),
                                               ALU.mult, ALU.add)
                        sdve.inc(v, 2)
                    for c in range(NCH):
                        dd = scb(FF_DD[c % 2], 0, 128, 512)
                        v.tensor_sub(dd, lnch(c, t * TW, TW), lnch(c, t * TW, TW, shift=1))
                        v.scalar_tensor_tensor(scb(FF_XR + c * 1024, 0, 128, 512), dd,
                                               MU_R[:, c:c + 1],
                                               lnch(c, t * TW, TW, shift=1),
                                               ALU.mult, ALU.add)
                        sdve.inc(v)
                    for d in range(NCH):
                        pv = [psC, psD][d % 2]
                        v.wait_ge(h_spe, p0 + 8 + d * 2 + 2)
                        v.wait_ge(h_sact, a0 + 16 + d + 1)
                        tmp = scf(FF_TMP[d % 2], 0, 128, 512)
                        v.tensor_tensor(tmp, scf(FF_RSB[d % 2], 0, 128, 512),
                                        pv[:, 0:512], ALU.mult)
                        v.tensor_tensor(xch(d, t * TW, TW), xch(d, t * TW, TW),
                                        tmp, ALU.add)
                        sdve.inc(v, 2)

                @blk.tensor
                def _(pe):
                    for d in range(NCH):
                        pk = [psA, psB][d % 2]
                        pe.wait_ge(h_dmf[d % 2], f0[d % 2] + 16 * (d // 2 + 1))
                        if d >= 2:
                            pe.wait_ge(h_sact, a0 + 2 * (d - 2) + 1)
                        for c in range(NCH):
                            if d == 0:
                                pe.wait_ge(h_sdve, v0 + 2 * (c + 1))
                            pe.matmul(pk[:, 0:512], wk(d % 2, c),
                                      scb(FF_XK + c * 1024, 0, 128, 512),
                                      start=(c == 0), stop=(c == NCH - 1))
                        spe.inc(pe)
                    for d in range(NCH):
                        pr = [psA, psB][d % 2]
                        pv = [psC, psD][d % 2]
                        pe.wait_ge(h_dmf[2 + d % 2], f0[2 + d % 2] + 32 * (d // 2 + 1))
                        if d >= 2:
                            pe.wait_ge(h_sact, a0 + 16 + (d - 2) + 1)
                        else:
                            pe.wait_ge(h_sact, a0 + 2 * (6 + d) + 2)
                        for c in range(NCH):
                            if d == 0:
                                pe.wait_ge(h_sdve, v0 + 16 + c + 1)
                            pe.matmul(pr[:, 0:512], wr(d % 2, c),
                                      scb(FF_XR + c * 1024, 0, 128, 512),
                                      start=(c == 0), stop=(c == NCH - 1))
                        spe.inc(pe)
                        pe.wait_ge(h_sact, a0 + 16)
                        if d >= 2:
                            pe.wait_ge(h_sdve, v0 + 24 + (d - 2) * 2 + 1)
                        for c in range(NCH):
                            pe.matmul(pv[:, 0:512], wv(d % 2, c),
                                      scb(FF_KACT + c * 1024, 0, 128, 512),
                                      start=(c == 0), stop=(c == NCH - 1))
                        spe.inc(pe)

                @blk.scalar
                def _(act):
                    for d in range(NCH):
                        pk = [psA, psB][d % 2]
                        act.wait_ge(h_spe, p0 + d + 1)
                        ka = scb(FF_KACT + d * 1024, 0, 128, 512)
                        act.activation(ka, pk[:, 0:512], AF.Relu)
                        sact.inc(act)
                        act.activation(ka, ka, AF.Square)
                        sact.inc(act)
                    for d in range(NCH):
                        pr = [psA, psB][d % 2]
                        act.wait_ge(h_spe, p0 + 8 + d * 2 + 1)
                        if d >= 2:
                            act.wait_ge(h_sdve, v0 + 24 + (d - 2) * 2 + 1)
                        act.activation(scf(FF_RSB[d % 2], 0, 128, 512), pr[:, 0:512],
                                       AF.Sigmoid)
                        sact.inc(act)

                if store:
                    @blk.gpsimd
                    def _(gp):
                        for c in range(NCH):
                            gp.wait_ge(h_sdve, v0 + 24 + c * 2 + 2)
                            dmo.inc(gp.dma_start(
                                Y[s, c * 128:(c + 1) * 128, t * TW:(t + 1) * TW],
                                xch(c, t * TW, TW)))

            assert spe.v == p0 + 24
            assert sact.v == a0 + 24
            assert sdve.v == v0 + 40

    def emit_attn_b(self, s):
        nc = self.nc
        h_dmo, h_spe, h_sact, h_sdve, h_spl = (self.h_dmo, self.h_spe,
                                               self.h_sact, self.h_sdve,
                                               self.h_spl)
        h_dmc, h_dmw2 = self.h_dmc, self.h_dmw2
        dmc, dmw2 = self.dmc, self.dmw2
        dmo, spe, sact, sdve, spl = (self.dmo, self.spe, self.sact,
                                     self.sdve, self.spl)
        scf, scb, wab, waf, xch, lnch, bgch = (self.scf, self.scb, self.wab,
                                               self.waf, self.xch, self.lnch,
                                               self.bgch)
        psA, psB, psC, psD, psW = self.psA, self.psB, self.psC, self.psD, self.psW
        s128t_bf, ones1b, bmu_sb = self.s128t_bf, self.ones1b, self.bmu_sb

        with nc.Block(f"s{s}_bp_w") as _b:
            blk = BW(_b)
            @blk.sync
            def _(sync):
                dmc.inc(sync.dma_start(
                    scf(0, 0, 128, 512).rearrange("p (c w) -> p c w", w=64),
                    self.WQA.rearrange("(c p) w -> p c w", p=128)))
                dmc.inc(sync.dma_start(scf(2048, 0, 64, 1024), self.WQB))
                dmc.inc(sync.dma_start(scf(6144, 0, 32, 512), self.WKB))
                dmc.inc(sync.dma_start(scf(8192, 0, 32, 256), self.WVB))
                dmc.inc(sync.dma_start(
                    scf(9216, 0, 128, 256).rearrange("p (c w) -> p c w", w=32),
                    self.WKVA.rearrange("(c p) w -> p c w", p=128)))

            @blk.vector
            def _(v):
                v.wait_ge(h_dmc, dmc.v)
                v.tensor_copy(wab(WB_WQA, 0, 128, 512), scf(0, 0, 128, 512))
                v.tensor_copy(wab(WB_WQB, 0, 64, 1024), scf(2048, 0, 64, 1024))
                v.tensor_copy(wab(WB_WKB, 0, 32, 512), scf(6144, 0, 32, 512))
                v.tensor_copy(wab(WB_WVB, 0, 32, 256), scf(8192, 0, 32, 256))
                v.tensor_copy(wab(WB_WKVA, 0, 128, 256), scf(9216, 0, 128, 256))
                sdve.inc(v)

        for NAME, DST in ((self.COSM, BP_COS), (self.SINM, BP_SIN)):
            with nc.Block(f"s{s}_bp_rope{DST}") as _b:
                blk = BW(_b)
                @blk.sync
                def _(sync):
                    sync.wait_ge(h_sdve, sdve.v)
                    dmc.inc(sync.dma_start(scf(BP_STAGE, 0, 128, 2048), NAME))

                @blk.vector
                def _(v):
                    v.wait_ge(h_dmc, dmc.v)
                    v.tensor_copy(scb(DST, 0, 128, 2048), scf(BP_STAGE, 0, 128, 2048))
                    sdve.inc(v)

        a0, p0 = sact.v, spe.v
        with nc.Block(f"s{s}_bp_lat") as _b:
            blk = BW(_b)
            @blk.tensor
            def _(pe):
                for t in range(NT):
                    pq = [psA, psB][t % 2]
                    pk = [psC, psD][t % 2]
                    if t >= 2:
                        pe.wait_ge(h_sact, a0 + (t - 2) * 2 + 2)
                    for c in range(NCH):
                        pe.matmul(pq[0:64, 0:512],
                                  wab(WB_WQA, 0, 128, 512)
                                  .rearrange("p (c w) -> p c w", w=64)[:, c, :],
                                  lnch(c, t * TW, TW), start=(c == 0), stop=(c == NCH - 1))
                    spe.inc(pe)
                    for c in range(NCH):
                        pe.matmul(pk[0:32, 0:512],
                                  wab(WB_WKVA, 0, 128, 256)
                                  .rearrange("p (c w) -> p c w", w=32)[:, c, :],
                                  lnch(c, t * TW, TW), start=(c == 0), stop=(c == NCH - 1))
                    spe.inc(pe)

            @blk.scalar
            def _(act):
                for t in range(NT):
                    pq = [psA, psB][t % 2]
                    pk = [psC, psD][t % 2]
                    act.wait_ge(h_spe, p0 + t * 2 + 1)
                    act.copy(scb(BP_LQ, 0, 64, 2048)[:, t * TW:(t + 1) * TW],
                             pq[0:64, 0:512])
                    sact.inc(act)
                    act.wait_ge(h_spe, p0 + t * 2 + 2)
                    act.copy(scb(BP_LKV, 0, 32, 2048)[:, t * TW:(t + 1) * TW],
                             pk[0:32, 0:512])
                    sact.inc(act)

        assert spe.v == p0 + 8 and sact.v == a0 + 8

        for phase, nchunks in (("q", NCH), ("k", 4)):
            a0, v0, p0 = sact.v, sdve.v, spe.v
            with nc.Block(f"s{s}_bp_{phase}") as _b:
                blk = BW(_b)
                @blk.tensor
                def _(pe):
                    n_i = nchunks * NT

                    def qs_mm(i):
                        ps_ = [psC, psD][i % 2]
                        pe.wait_ge(h_sact, a0 + i + 1)
                        if i >= 2:
                            pe.wait_ge(h_sdve, v0 + (i - 2) * 2 + 2)
                        pe.matmul(ps_[:, 0:512], s128t_bf[:, :],
                                  wab(WB_QSBT[i % 2], 0, 128, 512),
                                  start=True, stop=True)
                        spe.inc(pe)
                    for c in range(nchunks):
                        for t in range(NT):
                            i = c * NT + t
                            pq = [psA, psB][i % 2]
                            if i >= 2:
                                pe.wait_ge(h_sact, a0 + (i - 2) + 1)
                            if phase == "q":
                                pe.matmul(pq[:, 0:512],
                                          wab(WB_WQB, 0, 64, 1024)[:, c * 128:(c + 1) * 128],
                                          scb(BP_LQ, 0, 64, 2048)[:, t * TW:(t + 1) * TW],
                                          start=True, stop=True)
                            else:
                                pe.matmul(pq[:, 0:512],
                                          wab(WB_WKB, 0, 32, 512)[:, c * 128:(c + 1) * 128],
                                          scb(BP_LKV, 0, 32, 2048)[:, t * TW:(t + 1) * TW],
                                          start=True, stop=True)
                            spe.inc(pe)
                            if i >= 1:
                                qs_mm(i - 1)
                    qs_mm(n_i - 1)

                @blk.scalar
                def _(act):
                    for c in range(nchunks):
                        for t in range(NT):
                            i = c * NT + t
                            pq = [psA, psB][i % 2]
                            act.wait_ge(h_spe, p0 + (1 if i == 0 else 2 * i))
                            if i >= 2:
                                act.wait_ge(h_sdve, v0 + (i - 2) * 2 + 2)
                            act.copy(wab(WB_QSBT[i % 2], 0, 128, 512), pq[:, 0:512])
                            sact.inc(act)

                @blk.vector
                def _(v):
                    for c in range(nchunks):
                        for t in range(NT):
                            i = c * NT + t
                            n_i = nchunks * NT
                            ps_ = [psC, psD][i % 2]
                            v.wait_ge(h_spe, p0 + (2 * i + 3 if i < n_i - 1 else 2 * n_i))
                            t1 = wab(WB_RT1, 0, 128, 512)
                            t2 = wab(WB_RT2, 0, 128, 512)
                            v.tensor_tensor(t1, ps_[:, 0:512],
                                            scb(BP_SIN, 0, 128, 2048)[:, t * TW:(t + 1) * TW],
                                            ALU.mult)
                            v.tensor_tensor(t2, wab(WB_QSBT[i % 2], 0, 128, 512),
                                            scb(BP_COS, 0, 128, 2048)[:, t * TW:(t + 1) * TW],
                                            ALU.mult)
                            if phase == "q":
                                dst = bgch(c, t * TW, TW)
                            else:
                                dst = scb(BP_KREP + c * 4096, 0, 128, 2048)[:, t * TW:(t + 1) * TW]
                            v.tensor_tensor(dst, t2, t1, ALU.add)
                            sdve.inc(v, 2)

            assert spe.v == p0 + nchunks * NT * 2
            assert sact.v == a0 + nchunks * NT
            assert sdve.v == v0 + nchunks * NT * 2

        a0, v0, p0 = sact.v, sdve.v, spe.v
        with nc.Block(f"s{s}_bp_v") as _b:
            blk = BW(_b)
            @blk.vector
            def _(v):
                vt = scb(BS_VTOK, 0, 128, 16 * 260)
                v.memset(vt.rearrange("p (j g w) -> p j g w", g=4, w=65)[:, :, :, 64:65],
                         1.0)
                sdve.inc(v)

            @blk.tensor
            def _(pe):
                for j in range(16):
                    pv = [psA, psB][j % 2]
                    if j >= 2:
                        pe.wait_ge(h_sact, a0 + (j - 2) + 1)
                    pe.matmul(pv[:, 0:256],
                              scb(BP_LKV, 0, 32, 2048)[:, j * 128:(j + 1) * 128],
                              wab(WB_WVB, 0, 32, 256), start=True, stop=True)
                    spe.inc(pe)

            @blk.scalar
            def _(act):
                vt = scb(BS_VTOK, 0, 128, 16 * 260)
                for j in range(16):
                    pv = [psA, psB][j % 2]
                    act.wait_ge(h_spe, p0 + j + 1)
                    act.copy(vt[:, j * 260: j * 260 + 260]
                             .rearrange("p (g w) -> p g w", w=65)[:, :, 0:64],
                             pv[:, 0:256].rearrange("p (g w) -> p g w", w=64))
                    sact.inc(act)

        assert spe.v == p0 + 16 and sact.v == a0 + 16

        njs = [4 * i + 4 for i in range(NT)]
        a0, v0, p0, l0 = sact.v, sdve.v, spe.v, spl.v

        ops = []
        pend_pv = None
        pend_bc = None
        for pi in range(NPAIR):
            for i in range(NT):
                for j in range(njs[i]):
                    ops.append(("scA", pi, i, j))
                    ops.append(("scB", pi, i, j))
                    if pend_bc is not None and j == 1:
                        ops.append(("bcA",) + pend_bc)
                        ops.append(("bcB",) + pend_bc)
                        pend_bc = None
                    if pend_pv is not None:
                        ops.append(("pvA",) + pend_pv)
                        ops.append(("pvB",) + pend_pv)
                    pend_pv = (pi, i, j)
                ops.append(("pvA",) + pend_pv)
                ops.append(("pvB",) + pend_pv)
                pend_pv = None
                assert pend_bc is None
                pend_bc = (pi, i)
        ops.append(("bcA",) + pend_bc)
        ops.append(("bcB",) + pend_bc)
        val = {op: k + 1 for k, op in enumerate(ops)}

        def actb(p, i):
            return a0 + p * sum(2 * n + 4 for n in njs) + sum(2 * njs[k] + 4 for k in range(i))

        def dveb(p, i):
            return v0 + p * NT * 20 + 20 * i

        def splb(p, i):
            return l0 + 16 * (p * NT + i)

        def pv_done(pi, i, j):
            return p0 + val[("pvB", pi, i, j)]

        def prev_pv_all(pi, i):
            if i > 0:
                return pv_done(pi, i - 1, njs[i - 1] - 1)
            if pi > 0:
                return pv_done(pi - 1, NT - 1, njs[NT - 1] - 1)
            return 0

        PAb = [BS_PA0, BS_PA0 + 1024, BS_PA0 + 2048]
        PBb = [BS_PB0, BS_PB0 + 1024, BS_PB0 + 2048]
        ohsbB = self.ohsb_t[:, :]
        bw_ld = [None]

        with nc.Block(f"s{s}_bs") as _b:
            blk = BW(_b)

            @blk.sync
            def _(sync):
                dmw2.inc(sync.dma_start(
                    wab(WA_WOUT, 0, 128, 8192).rearrange("p (c w) -> p c w", w=1024),
                    self.WOUTB_S.rearrange("(c p) w -> c p w", p=128).transpose([1, 0, 2])))
                bw_ld[0] = dmw2.v

            @blk.tensor
            def _(pe):
                for op in ops:
                    kind, pi = op[0], op[1]
                    gkv = pi // 2
                    if kind in ("scA", "scB"):
                        _, pi, i, j = op
                        q0 = max(128 * j, TW * i)
                        w = TW * (i + 1) - q0
                        ps_ = [[psA, psC], [psB, psD]][kind == "scB"][j % 2]
                        if kind == "scA":
                            if j >= 2:
                                pe.wait_ge(h_sact, actb(pi, i) + 2 * (j - 2) + 2)
                            elif i > 0:
                                pe.wait_ge(h_sact, actb(pi, i - 1) + 2 * njs[i - 1])
                            elif pi > 0:
                                pe.wait_ge(h_sact, actb(pi - 1, NT - 1) + 2 * njs[NT - 1])
                        base = 0 if kind == "scA" else 64
                        pe.matmul(ps_[:, 0:w],
                                  scb(BP_KREP + gkv * 4096, 0, 128, 2048)
                                  [base:base + 64, j * 128:(j + 1) * 128],
                                  bgch(pi, q0, w, p0=base, np_=64),
                                  start=True, stop=True)
                        spe.inc(pe)
                    elif kind in ("pvA", "pvB"):
                        _, pi, i, j = op
                        if kind == "pvA":
                            pe.wait_ge(h_sact, actb(pi, i) + 2 * j + 2)
                            if j >= 4 * i:
                                jj = j - 4 * i
                                pe.wait_ge(h_sdve, dveb(pi, i) + 4 * jj + 4)
                            if j == 0:
                                if i > 0 or pi > 0:
                                    pe.wait_ge(h_sdve, dveb(pi, i))
                                    pe.wait_ge(h_spl, splb(pi, i))
                        vsl = scb(BS_VTOK, 0, 128, 16 * 260)[
                            :, j * 260 + gkv * 65: j * 260 + gkv * 65 + 65]
                        pbuf = scb((PAb if kind == "pvA" else PBb)[j % 3], 0, 128, 512)
                        col = 0 if kind == "pvA" else 512
                        pe.matmul(psW[0:65, col:col + 512], vsl, pbuf,
                                  start=(j == 0), stop=(j == njs[i] - 1))
                        spe.inc(pe)
                    elif kind == "bcA":
                        _, pi, i = op
                        pe.wait_ge(h_sact, actb(pi, i) + 2 * njs[i] + 2)
                        pe.wait_ge(h_sdve, dveb(pi, i))
                        pe.matmul(psW[0:64, 1024:1536], ones1b[:, :],
                                  self.recs_bf[:, 0:512], start=True, stop=True)
                        spe.inc(pe)
                    else:
                        _, pi, i = op
                        pe.wait_ge(h_sact, actb(pi, i) + 2 * njs[i] + 4)
                        pe.matmul(psW[64:128, 1536:2048], ones1b[:, :],
                                  self.recs_bf[:, 512:1024], start=True, stop=True,
                                  tile_position=(0, 64))
                        spe.inc(pe)

            @blk.scalar
            def _(act):
                for pi in range(NPAIR):
                    for i in range(NT):
                        for j in range(njs[i]):
                            q0 = max(128 * j, TW * i)
                            w = TW * (i + 1) - q0
                            poff = q0 - TW * i
                            psSA = [psA, psC][j % 2]
                            psSB = [psB, psD][j % 2]
                            pA = scb(PAb[j % 3], 0, 128, 512)
                            pB = scb(PBb[j % 3], 0, 128, 512)
                            act.wait_ge(h_spe, p0 + val[("scB", pi, i, j)])
                            if j >= 3:
                                act.wait_ge(h_spe, pv_done(pi, i, j - 3))
                            else:
                                pw = prev_pv_all(pi, i)
                                if pw:
                                    act.wait_ge(h_spe, pw)
                            act.activation(pA[:, poff:poff + w], psSA[:, 0:w],
                                           AF.Exp, scale=0.125)
                            act.activation(pB[:, poff:poff + w], psSB[:, 0:w],
                                           AF.Exp, scale=0.125)
                            sact.inc(act, 2)
                        act.wait_ge(h_spe, pv_done(pi, i, njs[i] - 1))
                        act.activation(self.drow_f[:, :], psW[64:65, 0:512], AF.Ln)
                        act.activation(self.recs_bf[:, 0:512], self.drow_f[:, :],
                                       AF.Exp, scale=-1.0)
                        sact.inc(act, 2)
                        act.activation(self.drow_f[:, :], psW[64:65, 512:1024], AF.Ln)
                        act.activation(self.recs_bf[:, 512:1024], self.drow_f[:, :],
                                       AF.Exp, scale=-1.0)
                        sact.inc(act, 2)

            @blk.gpsimd
            def _(gp):
                for pi in range(NPAIR):
                    for i in range(NT):
                        gp.wait_ge(h_sdve, dveb(pi, i) + 18)
                        spl.inc(gp.dma_start(
                            self.lnx[64:128,
                                     pi * (T + 1) + 1 + i * TW: pi * (T + 1) + 1 + (i + 1) * TW],
                            ohsbB), 16)

            @blk.vector
            def _(v):
                for pi in range(NPAIR):
                    for i in range(NT):
                        for jj in range(4):
                            j = 4 * i + jj
                            q0 = 128 * j
                            poff = q0 - TW * i
                            pA = scb(PAb[j % 3], 0, 128, 512)
                            pB = scb(PBb[j % 3], 0, 128, 512)
                            if poff > 0:
                                if j >= 3:
                                    v.wait_ge(h_spe, pv_done(pi, i, j - 3))
                                v.memset(pA[:, 0:poff], 0.0)
                                v.memset(pB[:, 0:poff], 0.0)
                                sdve.inc(v, 2)
                            else:
                                sdve.pad(v, 2)
                            v.wait_ge(h_sact, actb(pi, i) + 2 * j + 2)
                            v.tensor_mul(pA[:, poff:poff + 128],
                                         pA[:, poff:poff + 128],
                                         self.maskd_bf[:, :])
                            v.tensor_mul(pB[:, poff:poff + 128],
                                         pB[:, poff:poff + 128],
                                         self.maskd_bf[:, :])
                            sdve.inc(v, 2)
                        v.wait_ge(h_spe, pv_done(pi, i, njs[i] - 1))
                        v.tensor_copy(scb(BS_OHSBA, 0, 128, 512)[0:64, :],
                                      psW[0:64, 0:512])
                        sdve.inc(v)
                        v.wait_ge(h_spl, splb(pi, i))
                        v.tensor_copy(ohsbB, psW[0:64, 512:1024])
                        sdve.inc(v)
                        v.wait_ge(h_spe, p0 + val[("bcA", pi, i)])
                        v.tensor_tensor(
                            self.lnx[0:64, pi * (T + 1) + 1 + i * TW:
                                     pi * (T + 1) + 1 + (i + 1) * TW],
                            scb(BS_OHSBA, 0, 128, 512)[0:64, :],
                            psW[0:64, 1024:1536], ALU.mult)
                        sdve.inc(v)
                        v.wait_ge(h_spe, p0 + val[("bcB", pi, i)])
                        v.wait_ge(h_spl, splb(pi, i) + 16)
                        ln_b = self.lnx[64:128, pi * (T + 1) + 1 + i * TW:
                                        pi * (T + 1) + 1 + (i + 1) * TW]
                        v.tensor_tensor(ln_b, ln_b, psW[64:128, 1536:2048], ALU.mult)
                        sdve.inc(v)

        assert spe.v == p0 + len(ops)
        assert sact.v == a0 + NPAIR * sum(2 * n + 4 for n in njs)
        assert sdve.v == v0 + NPAIR * NT * 20
        assert spl.v == l0 + 16 * NPAIR * NT

        a0, v0, p0 = sact.v, sdve.v, spe.v
        with nc.Block(f"s{s}_bw") as _b:
            blk = BW(_b)

            @blk.tensor
            def _(pe):
                pe.wait_ge(h_dmw2, bw_ld[0])
                for t in range(NT):
                    for d in range(NCH):
                        i = t * NCH + d
                        pw = [psA, psB, psC, psD][i % 4]
                        if i >= 4:
                            pe.wait_ge(h_sdve, v0 + (i - 4) + 1)
                        for c in range(NCH):
                            pe.matmul(pw[:, 0:512],
                                      wab(WA_WOUT, 0, 128, 8192).rearrange(
                                          "p (c w) -> p c w", w=1024)[:, c, d * 128:(d + 1) * 128],
                                      lnch(c, t * TW, TW), start=(c == 0),
                                      stop=(c == NCH - 1))
                        spe.inc(pe)

            @blk.scalar
            def _(act):
                for t in range(NT):
                    for d in range(NCH):
                        i = t * NCH + d
                        pw = [psA, psB, psC, psD][i % 4]
                        act.wait_ge(h_spe, p0 + i + 1)
                        if i >= 2:
                            act.wait_ge(h_sdve, v0 + (i - 2) + 1)
                        act.activation(scf(BW_TSB2[i % 2], 0, 128, 512), pw[:, 0:512],
                                       AF.Identity, bias=bmu_sb[:, 8 + d: 8 + d + 1])
                        sact.inc(act)

            @blk.vector
            def _(v):
                for t in range(NT):
                    for d in range(NCH):
                        i = t * NCH + d
                        v.wait_ge(h_sact, a0 + i + 1)
                        v.tensor_tensor(xch(d, t * TW, TW), xch(d, t * TW, TW),
                                        scf(BW_TSB2[i % 2], 0, 128, 512), ALU.add)
                        sdve.inc(v)

        assert spe.v == p0 + 32 and sact.v == a0 + 32 and sdve.v == v0 + 32



_CACHE = {}


def _host_prep(inputs):
    x = np.ascontiguousarray(np.asarray(inputs["L0_attn_a_x"]).transpose(0, 2, 1))
    wqkv = np.ascontiguousarray(np.asarray(inputs["L0_attn_a_W_qkv"], np.float32))
    rf = np.asarray(inputs["L0_attn_a_RF"], np.float32)
    rfp = np.zeros((8, 128, 128), np.float32)
    for g in range(8):
        rfp[g, 0:64, 0:64] = rf[2 * g]
        rfp[g, 64:128, 64:128] = rf[2 * g + 1]
    wouta = np.ascontiguousarray(np.asarray(inputs["L0_attn_a_W_out"], np.float32))
    b_out_a = np.asarray(inputs["L0_attn_a_b_out"], np.float32)
    wqa = np.ascontiguousarray(np.asarray(inputs["L0_attn_b_Wqa"], np.float32))
    wqb = np.ascontiguousarray(np.asarray(inputs["L0_attn_b_Wqb"], np.float32))
    wkva = np.ascontiguousarray(np.asarray(inputs["L0_attn_b_Wkva"], np.float32))
    wkvb = np.asarray(inputs["L0_attn_b_Wkvb"], np.float32).reshape(32, 4, 64, 2)
    wkb = np.zeros((32, 512), np.float32)
    for g in range(4):
        wkb[:, g * 128: g * 128 + 64] = wkvb[:, g, :, 0]
        wkb[:, g * 128 + 64: g * 128 + 128] = wkvb[:, g, :, 0]
    wvb = np.ascontiguousarray(wkvb[..., 1].reshape(32, 256))
    woutb = np.ascontiguousarray(np.asarray(inputs["L0_attn_b_Wout"], np.float32))
    b_out_b = np.asarray(inputs["L0_attn_b_bout"], np.float32)
    wrt = np.ascontiguousarray(np.asarray(inputs["L0_ffn_W_r"], np.float32).T)
    wkt = np.ascontiguousarray(np.asarray(inputs["L0_ffn_W_k"], np.float32).T)
    wvt = np.ascontiguousarray(np.asarray(inputs["L0_ffn_W_v"], np.float32).T)
    mu_r = np.asarray(inputs["L0_ffn_mu_r"], np.float32)
    mu_k = np.asarray(inputs["L0_ffn_mu_k"], np.float32)
    bmu = np.zeros((128, 32), np.float32)
    bmu[:, 0:8] = b_out_a.reshape(8, 128).T
    bmu[:, 8:16] = b_out_b.reshape(8, 128).T
    bmu[:, 16:24] = mu_r.reshape(8, 128).T
    bmu[:, 24:32] = mu_k.reshape(8, 128).T

    D = 64
    inv = (1.0 / (10000.0 ** (np.arange(0, D, 2, dtype=np.float32) / np.float32(D)))).astype(np.float32)
    tpos = np.arange(T, dtype=np.float32)
    fr = np.outer(tpos, inv).astype(np.float32)
    cos64 = np.cos(fr).astype(np.float32).T
    sin64 = np.sin(fr).astype(np.float32).T
    cosm = np.repeat(cos64, 2, axis=0)
    sinm = np.repeat(sin64, 2, axis=0)
    cosm = np.tile(cosm, (2, 1))
    sinm = np.tile(sinm, (2, 1))
    s64 = np.zeros((64, 64), np.float32)
    for i2 in range(32):
        s64[2 * i2, 2 * i2 + 1] = 1.0
        s64[2 * i2 + 1, 2 * i2] = -1.0
    s128t = np.zeros((128, 128), np.float32)
    s128t[0:64, 0:64] = s64
    s128t[64:128, 64:128] = s64
    maskd = (np.arange(128)[:, None] <= np.arange(128)[None, :]).astype(np.float32)
    sel2 = np.zeros((2, 128), np.float32)
    sel2[0, 0:64] = 1.0
    sel2[1, 64:128] = 1.0

    com = dict(wqkv=wqkv, rf=rfp, wouta=wouta, wqa=wqa, wqb=wqb, wkva=wkva,
               wkb=wkb, wvb=wvb, woutb=woutb, wrt=wrt, wkt=wkt, wvt=wvt,
               bmu=bmu, cosm=cosm, sinm=sinm, s128t=s128t, maskd=maskd,
               sel2=sel2)
    return x, com


def get_nc(tap=None):
    key = ("nc", tap)
    if key not in _CACHE:
        _CACHE[key] = MK(tap=tap).build()
    return _CACHE[key]


def kernel(**inputs):
    x, com = _host_prep(inputs)
    nc = get_nc()
    in_maps = []
    for k in range(NCORES):
        m = {"x": np.ascontiguousarray(x[k * NSEQ:(k + 1) * NSEQ])}
        m.update(com)
        in_maps.append(m)
    res = run_bass_kernel_spmd(nc, in_maps, core_ids=list(range(NCORES)))
    y = np.stack([np.asarray(r["y"]) for r in res.results])
    y = y.reshape(B, C, T).transpose(0, 2, 1)
    return np.ascontiguousarray(y.astype(np.float32))


# version 13
# speedup vs baseline: 1.1802x; 1.1802x over previous
import numpy as np
import concourse.bass as bass
import concourse.mybir as mybir
from concourse.bass_utils import run_bass_kernel_spmd

F32 = mybir.dt.float32
BF16 = mybir.dt.bfloat16
AF = mybir.ActivationFunctionType
ALU = mybir.AluOpType

NCORES = 8
B, T, C = 32, 2048, 1024
NSEQ = B // NCORES
NCH = C // 128
NT = 4
TW = 512
NG = 8
NPAIR = 8
EPS = 1e-5
LN_DVE_T = 25
STAB = 1e-6

LN_XB = [c * 1024 for c in range(8)]
LN_SQ = [8192 + c * 1024 for c in range(8)]
LN_ROW0 = 16384
LN_ROW1 = 18432
LN_TS = [20480, 22528]
AA_QSC = [0, 1024, 2048, 3072]
AA_KSC = [4096, 5120, 6144, 7168]
AA_PHIK = [8192, 8704]
AA_VTK = [9216, 9728]
AA_KVBD = 24576
AA_KSUM = 26624
AB_ATT = 0
AB_NRM = 8192
AB_TSB = [12288, 14336]
AB_TSB2 = [16384, 18432]
FF_XR = 0
FF_XK = 8192
FF_KACT = 16384
FF_DD = [24576, 25600]
FF_RSB = [26624, 28672]
FF_TMP = [30720, 24576]
BP_COS = 0
BP_SIN = 4096
BP_STAGE = 8192
BP_LQ = 8192
BP_LKV = 12288
BP_KREP = 16384
BS_VTOK = 0
BS_PA0 = 8320
BS_PB0 = 11392
BS_OHSBA = 14464
BW_TSB2 = [0, 2048]
ST_F32 = [0, 4096, 8192, 12288]
ST_BF = [16384, 18432, 20480, 22528]

WA_WQKV = [0, 6144]
WA_WOUT = 0
WF_K = [0, 2048]
WF_R = [4096, 6144]
WF_V = [8192, 10240]
WB_WQA = 0
WB_WQB = 1024
WB_WKB = 3072
WB_WVB = 4096
WB_WKVA = 4608
WB_MASKD = 5120
WB_QSBT = [5632, 6656]
WB_RT1 = 7680
WB_RT2 = 9728
WB_RECA = 11776
WB_RECB = 12800
WB_DROW = 13824


class EngP:

    def __init__(self, e):
        object.__setattr__(self, "e", e)
        object.__setattr__(self, "last", None)

    def __getattr__(self, name):
        f = getattr(object.__getattribute__(self, "e"), name)
        if not callable(f):
            return f

        def wrap(*a, **k):
            r = f(*a, **k)
            if hasattr(r, "then_inc"):
                object.__setattr__(self, "last", r)
            return r
        return wrap


class BW:

    def __init__(self, blk):
        self._blk = blk

    def __getattr__(self, name):
        dec = getattr(self._blk, name)

        def mydec(f):
            return dec(lambda e: f(EngP(e)))
        return mydec


class Ctr:
    def __init__(self, h, step=1):
        self.h = h
        self.v = 0
        self.step = step

    def inc(self, x, k=None):
        k = self.step if k is None else k
        if isinstance(x, EngP):
            last = object.__getattribute__(x, "last")
            if last is not None:
                last.then_inc(self.h, k)
            else:
                object.__getattribute__(x, "e").sem_inc(self.h, k)
        elif hasattr(x, "then_inc"):
            x.then_inc(self.h, k)
        else:
            x.sem_inc(self.h, k)
        self.v += k
        return self.v

    def pad(self, engp, k=None):
        k = self.step if k is None else k
        e = object.__getattribute__(engp, "e") if isinstance(engp, EngP) else engp
        e.sem_inc(self.h, k)
        self.v += k
        return self.v


class MK:
    def __init__(self, tap=None):
        self.tap = tap
        nc = bass.Bass("TRN2", target_bir_lowering=False,
                       detect_race_conditions=False)
        self.nc = nc
        self.X = nc.dram_tensor("x", [NSEQ, C, T], F32, kind="ExternalInput").ap()
        self.Y = nc.dram_tensor("y", [NSEQ, C, T], F32, kind="ExternalOutput").ap()
        self.WQKV = nc.dram_tensor("wqkv", [C, 3 * C], F32, kind="ExternalInput").ap()
        self.RF = nc.dram_tensor("rf", [NG, 128, 128], F32, kind="ExternalInput").ap()
        self.WOUTA = nc.dram_tensor("wouta", [C, C], F32, kind="ExternalInput").ap()
        self.WQA = nc.dram_tensor("wqa", [C, 64], F32, kind="ExternalInput").ap()
        self.WQB = nc.dram_tensor("wqb", [64, C], F32, kind="ExternalInput").ap()
        self.WKVA = nc.dram_tensor("wkva", [C, 32], F32, kind="ExternalInput").ap()
        self.WKB = nc.dram_tensor("wkb", [32, 512], F32, kind="ExternalInput").ap()
        self.WVB = nc.dram_tensor("wvb", [32, 256], F32, kind="ExternalInput").ap()
        self.WOUTB = nc.dram_tensor("woutb", [C, C], F32, kind="ExternalInput").ap()
        self.WRT = nc.dram_tensor("wrt", [C, C], F32, kind="ExternalInput").ap()
        self.WKT = nc.dram_tensor("wkt", [C, C], F32, kind="ExternalInput").ap()
        self.WVT = nc.dram_tensor("wvt", [C, C], F32, kind="ExternalInput").ap()
        self.BMU = nc.dram_tensor("bmu", [128, 32], F32, kind="ExternalInput").ap()
        self.COSM = nc.dram_tensor("cosm", [128, T], F32, kind="ExternalInput").ap()
        self.SINM = nc.dram_tensor("sinm", [128, T], F32, kind="ExternalInput").ap()
        self.S128T = nc.dram_tensor("s128t", [128, 128], F32, kind="ExternalInput").ap()
        self.MASKD = nc.dram_tensor("maskd", [128, 128], F32, kind="ExternalInput").ap()
        self.SEL2 = nc.dram_tensor("sel2", [2, 128], F32, kind="ExternalInput").ap()
        self.WQKV_S = nc.dram_tensor("wqkv_s", [NG, NCH, 128, 384], BF16).ap()
        self.WOUTA_S = nc.dram_tensor("wouta_s", [C, C], BF16).ap()
        self.WOUTB_S = nc.dram_tensor("woutb_s", [C, C], BF16).ap()
        self.WRT_S = nc.dram_tensor("wrt_s", [C, C], BF16).ap()
        self.WKT_S = nc.dram_tensor("wkt_s", [C, C], BF16).ap()
        self.WVT_S = nc.dram_tensor("wvt_s", [C, C], BF16).ap()
        self.TAPX = None
        if tap is not None:
            self.TAPX = nc.dram_tensor("tapx", [C, T], F32, kind="ExternalOutput").ap()

    def scf(self, off, p0, np_, w):
        assert off % 4 == 0 and off + 0 <= 32768
        return self.scr[p0:p0 + np_, off // 4: off // 4 + w]

    def scb(self, off, p0, np_, w):
        assert off % 2 == 0
        return self.scr.ap().bitcast(BF16)[p0:p0 + np_, off // 2: off // 2 + w]

    def wab(self, off, p0, np_, w):
        assert off % 2 == 0
        return self.warena[p0:p0 + np_, off // 2: off // 2 + w]

    def waf(self, off, p0, np_, w):
        assert off % 4 == 0
        return self.warena.ap().bitcast(F32)[p0:p0 + np_, off // 4: off // 4 + w]

    def xch(self, c, t0=0, w=T):
        return self.x_sb[:, c * T + t0: c * T + t0 + w]

    def lnch(self, c, t0=0, w=T, shift=0, p0=0, np_=128):
        base = c * (T + 1) + 1 - shift + t0
        return self.lnx[p0:p0 + np_, base: base + w]

    def bgch(self, c, t0=0, w=T, p0=0, np_=128):
        return self.biga[p0:p0 + np_, c * T + t0: c * T + t0 + w]

    def build(self):
        nc = self.nc
        from contextlib import ExitStack
        with ExitStack() as st:
            x_sb = st.enter_context(nc.sbuf_tensor("x_sb", [128, NCH * T], F32))
            lnx = st.enter_context(nc.sbuf_tensor("lnx", [128, NCH * (T + 1)], BF16))
            biga = st.enter_context(nc.sbuf_tensor("biga", [128, NCH * T], BF16))
            warena = st.enter_context(nc.sbuf_tensor("warena", [128, 8192], BF16))
            scr = st.enter_context(nc.sbuf_tensor("scr", [128, 8192], F32))
            rf_bf = st.enter_context(nc.sbuf_tensor("rf_bf", [128, NG * 128], BF16))
            s128t_bf = st.enter_context(nc.sbuf_tensor("s128t_bf", [128, 128], BF16))
            sel2_f = st.enter_context(nc.sbuf_tensor("sel2_f", [2, 128], F32))
            ones1f = st.enter_context(nc.sbuf_tensor("ones1f", [1, 128], F32))
            ones1b = st.enter_context(nc.sbuf_tensor("ones1b", [1, 64], BF16))
            onesc_bf = st.enter_context(nc.sbuf_tensor("onesc_bf", [128, 8], BF16))
            bmu_sb = st.enter_context(nc.sbuf_tensor("bmu_sb", [128, 32], F32))
            cst = st.enter_context(nc.sbuf_tensor("cst", [128, 2], F32))
            maskd_bf = st.enter_context(nc.sbuf_tensor("maskd_bf", [128, 128], BF16))
            recs_bf = st.enter_context(nc.sbuf_tensor("recs_bf", [1, 1024], BF16))
            drow_f = st.enter_context(nc.sbuf_tensor("drow_f", [1, 512], F32))
            self.maskd_bf, self.recs_bf, self.drow_f = maskd_bf, recs_bf, drow_f
            ohsb_t = st.enter_context(nc.sbuf_tensor("ohsb_t", [64, 512], BF16))
            self.ohsb_t = ohsb_t
            nc.const_aps.register(F32, EPS, cst[:, 0:1])
            nc.const_aps.register(F32, STAB, cst[:, 1:2])
            self.cst = cst
            psA = st.enter_context(nc.psum_tensor("psA", [128, 512], F32))
            psB = st.enter_context(nc.psum_tensor("psB", [128, 512], F32))
            psC = st.enter_context(nc.psum_tensor("psC", [128, 512], F32))
            psD = st.enter_context(nc.psum_tensor("psD", [128, 512], F32))
            psW = st.enter_context(nc.psum_tensor("psW", [128, 2048], F32))
            h_dmx = st.enter_context(nc.semaphore("dmx"))
            h_dmk0 = st.enter_context(nc.semaphore("dmk0"))
            h_dmk1 = st.enter_context(nc.semaphore("dmk1"))
            h_dmw2 = st.enter_context(nc.semaphore("dmw2"))
            h_dmf0 = st.enter_context(nc.semaphore("dmf0"))
            h_dmf1 = st.enter_context(nc.semaphore("dmf1"))
            h_dmf2 = st.enter_context(nc.semaphore("dmf2"))
            h_dmf3 = st.enter_context(nc.semaphore("dmf3"))
            h_dms0 = st.enter_context(nc.semaphore("dms0"))
            h_dms1 = st.enter_context(nc.semaphore("dms1"))
            h_dmso0 = st.enter_context(nc.semaphore("dmso0"))
            h_dmso1 = st.enter_context(nc.semaphore("dmso1"))
            h_dmc = st.enter_context(nc.semaphore("dmc"))
            h_dmo = st.enter_context(nc.semaphore("dmo"))
            h_spe = st.enter_context(nc.semaphore("spe"))
            h_sact = st.enter_context(nc.semaphore("sact"))
            h_sdve = st.enter_context(nc.semaphore("sdve"))
            h_spl = st.enter_context(nc.semaphore("spl"))
            h_spool = st.enter_context(nc.semaphore("spool"))
            self.x_sb, self.lnx, self.biga = x_sb, lnx, biga
            self.warena, self.scr = warena, scr
            self.rf_bf, self.s128t_bf, self.sel2_f = rf_bf, s128t_bf, sel2_f
            self.ones1f, self.ones1b, self.onesc_bf = ones1f, ones1b, onesc_bf
            self.bmu_sb = bmu_sb
            self.psA, self.psB, self.psC, self.psD, self.psW = psA, psB, psC, psD, psW
            self.h_dmx, self.h_dmw2, self.h_dmc, self.h_dmo = h_dmx, h_dmw2, h_dmc, h_dmo
            self.h_dmk = [h_dmk0, h_dmk1]
            self.h_dmf = [h_dmf0, h_dmf1, h_dmf2, h_dmf3]
            self.h_dms = [h_dms0, h_dms1]
            self.h_dmso = [h_dmso0, h_dmso1]
            self.h_spe, self.h_sact, self.h_sdve, self.h_spl = h_spe, h_sact, h_sdve, h_spl
            self.dmx, self.dmw2, self.dmc, self.dmo = (Ctr(h_dmx, 16), Ctr(h_dmw2, 16),
                                                       Ctr(h_dmc, 16), Ctr(h_dmo, 16))
            self.dmk = [Ctr(h_dmk0, 16), Ctr(h_dmk1, 16)]
            self.dmf = [Ctr(h_dmf0, 16), Ctr(h_dmf1, 16), Ctr(h_dmf2, 16), Ctr(h_dmf3, 16)]
            self.dms = [Ctr(h_dms0, 16), Ctr(h_dms1, 16)]
            self.dmso = [Ctr(h_dmso0, 16), Ctr(h_dmso1, 16)]
            self.spe, self.sact, self.sdve, self.spl = (Ctr(h_spe), Ctr(h_sact),
                                                        Ctr(h_sdve), Ctr(h_spl))
            self.h_spool = h_spool
            self.spool = Ctr(h_spool)
            self.emit_staging()
            self.emit_consts()
            for s in range(NSEQ):
                self.emit_seq(s)
            with nc.Block("fin") as _b:
                blk = BW(_b)
                @blk.gpsimd
                def _(g):
                    g.wait_ge(self.h_dmo, self.dmo.v)

                @blk.sync
                def _(sync):
                    sync.wait_ge(self.h_dmo, self.dmo.v)
        nc.finalize()
        return nc

    def emit_staging(self):
        nc = self.nc
        jobs = []
        for c in range(NCH):
            for sub in range(3):
                jobs.append((self.WQKV[c * 128:(c + 1) * 128, sub * 1024:(sub + 1) * 1024],
                             self.WQKV_S[:, c, :, sub * 128:(sub + 1) * 128], "qkv"))
        for W, WS in ((self.WOUTA, self.WOUTA_S), (self.WOUTB, self.WOUTB_S),
                      (self.WRT, self.WRT_S), (self.WKT, self.WKT_S),
                      (self.WVT, self.WVT_S)):
            for c in range(NCH):
                jobs.append((W[c * 128:(c + 1) * 128, :],
                             WS[c * 128:(c + 1) * 128, :], "row"))
        v0 = self.sdve.v
        h_sdve, sdve = self.h_sdve, self.sdve
        ld_sem, ld_h = self.dmf, self.h_dmf
        st_sem = [self.dms[0], self.dms[1], self.dmso[0], self.dmso[1]]
        st_h = [self.h_dms[0], self.h_dms[1], self.h_dmso[0], self.h_dmso[1]]
        sl0 = [c.v for c in ld_sem]
        so0 = [c.v for c in st_sem]
        scf, scb = self.scf, self.scb
        with nc.Block("staging") as _b:
            blk = BW(_b)
            @blk.sync
            def _(sync):
                for i, (src, dst, kind) in enumerate(jobs):
                    if i >= 4:
                        sync.wait_ge(h_sdve, v0 + (i - 4) + 1)
                    ld_sem[i % 4].inc(sync.dma_start(scf(ST_F32[i % 4], 0, 128, 1024), src))

            @blk.vector
            def _(v):
                for i, (src, dst, kind) in enumerate(jobs):
                    v.wait_ge(ld_h[i % 4], sl0[i % 4] + 16 * (i // 4 + 1))
                    if i >= 4:
                        v.wait_ge(st_h[i % 4], so0[i % 4] + 16 * ((i - 4) // 4 + 1))
                    v.tensor_copy(scb(ST_BF[i % 4], 0, 128, 1024),
                                  scf(ST_F32[i % 4], 0, 128, 1024))
                    sdve.inc(v)

            @blk.gpsimd
            def _(g):
                for i, (src, dst, kind) in enumerate(jobs):
                    g.wait_ge(h_sdve, v0 + i + 1)
                    sb = scb(ST_BF[i % 4], 0, 128, 1024)
                    if kind == "qkv":
                        st_sem[i % 4].inc(g.dma_start(dst.transpose([1, 0, 2]),
                                                      sb.rearrange("p (g w) -> p g w", w=128)))
                    else:
                        st_sem[i % 4].inc(g.dma_start(dst, sb))

    def emit_consts(self):
        nc = self.nc
        h_dmc, dmc, sdve = self.h_dmc, self.dmc, self.sdve
        scf, scb = self.scf, self.scb
        with nc.Block("consts") as _b:
            blk = BW(_b)
            @blk.sync
            def _(sync):
                sync.wait_ge(self.h_sdve, self.sdve.v)
                dmc.inc(sync.dma_start(
                    scf(0, 0, 128, 1024).rearrange("p (g w) -> p g w", w=128),
                    self.RF.transpose([1, 0, 2])))
                dmc.inc(sync.dma_start(self.sel2_f[:, :], self.SEL2))
                dmc.inc(sync.dma_start(scf(4096, 0, 128, 128), self.S128T))
                dmc.inc(sync.dma_start(self.bmu_sb[:, :], self.BMU))
                dmc.inc(sync.dma_start(scf(5120, 0, 128, 128), self.MASKD))

            @blk.vector
            def _(v):
                v.wait_ge(h_dmc, dmc.v)
                v.wait_ge(self.h_dmso[0], self.dmso[0].v)
                v.wait_ge(self.h_dmso[1], self.dmso[1].v)
                v.memset(self.cst[:, 0:1], EPS)
                v.memset(self.cst[:, 1:2], STAB)
                v.tensor_copy(self.rf_bf[:, :], scf(0, 0, 128, 1024))
                v.tensor_copy(self.s128t_bf[:, :], scf(4096, 0, 128, 128))
                v.tensor_copy(self.maskd_bf[:, :], scf(5120, 0, 128, 128))
                v.memset(self.ones1f[:, :], 1.0)
                v.memset(self.ones1b[:, :], 1.0)
                v.memset(self.onesc_bf[:, :], 1.0)
                v.memset(self.lnx.ap().rearrange("p (c t) -> p c t", t=T + 1)[:, :, 0:1], 0.0)
                sdve.inc(v)

    def emit_seq(self, s):
        nc = self.nc
        dmx, dmo = self.dmx, self.dmo
        h_dmo = self.h_dmo
        with nc.Block(f"s{s}_loadx") as _b:
            blk = BW(_b)
            @blk.sync
            def _(sync):
                if s > 0:
                    sync.wait_ge(h_dmo, dmo.v)
                for c in range(NCH):
                    dmx.inc(sync.dma_start(self.xch(c),
                                           self.X[s, c * 128:(c + 1) * 128, :]))
                self.dmk[0].inc(sync.dma_start(
                    self.wab(WA_WQKV[0], 0, 128, 3072).rearrange("p (c w) -> p c w", w=384),
                    self.WQKV_S[0].transpose([1, 0, 2])))
        self._xload = dmx.v
        self._aa_g0 = self.dmk[0].v

        if self.tap == 0:
            self.emit_tap(s)
            return
        self.emit_ln(s, 1)
        if self.tap == 5:
            self.emit_tap_lnx(s)
            return
        self.emit_attn_a(s)
        if self.tap == 1:
            self.emit_tap(s)
            return
        self.emit_ln(s, 2)
        self.emit_ffn(s, 1)
        if self.tap == 2:
            self.emit_tap(s)
            return
        self.emit_ln(s, 3)
        self.emit_attn_b(s)
        if self.tap == 3:
            self.emit_tap(s)
            return
        self.emit_ln(s, 4)
        self.emit_ffn(s, 2, store=True)
        if self.tap == 4:
            self.emit_tap(s)

    def emit_tap(self, s):
        if s != 0:
            return
        nc, dmo, h_dmo = self.nc, self.dmo, self.h_dmo
        h_dmx = self.h_dmx
        xload = self._xload
        with nc.Block("tap") as _b:
            blk = BW(_b)
            @blk.gpsimd
            def _(g):
                g.wait_ge(h_dmx, xload)
                for c in range(NCH):
                    dmo.inc(g.dma_start(self.TAPX[c * 128:(c + 1) * 128, :],
                                        self.xch(c)))
                g.wait_ge(h_dmo, dmo.v)

    def emit_tap_lnx(self, s):
        if s != 0:
            return
        nc, dmo, h_dmo, sdve, h_sdve = (self.nc, self.dmo, self.h_dmo,
                                        self.sdve, self.h_sdve)
        v0 = sdve.v
        o0 = self.dmo.v
        with nc.Block("taplnx") as _b:
            blk = BW(_b)
            @blk.vector
            def _(v):
                for c in range(NCH):
                    if c >= 2:
                        v.wait_ge(h_dmo, o0 + 16 * (c - 1))
                    v.tensor_copy(self.scf([0, 8192][c % 2], 0, 128, 2048),
                                  self.lnch(c))
                    sdve.inc(v)

            @blk.gpsimd
            def _(g):
                for c in range(NCH):
                    g.wait_ge(h_sdve, v0 + c + 1)
                    g.wait_ge(h_dmo, o0 + 16 * c)
                    dmo.inc(g.dma_start(self.TAPX[c * 128:(c + 1) * 128, :],
                                        self.scf([0, 8192][c % 2], 0, 128, 2048)))
                g.wait_ge(h_dmo, dmo.v)

    def emit_ln(self, s, which):
        nc = self.nc
        h_spool, spool = self.h_spool, self.spool
        LN_MEV, LN_REV = 24576, 26624
        h_spe, h_sact, h_sdve = self.h_spe, self.h_sact, self.h_sdve
        spe, sact, sdve = self.spe, self.sact, self.sdve
        scf, scb, xch, lnch = self.scf, self.scb, self.xch, self.lnch
        psA, psB, psC, psD, psW = self.psA, self.psB, self.psC, self.psD, self.psW
        onesc, ones1f = self.onesc_bf, self.ones1f
        h_dmx = self.h_dmx
        xload = self._xload
        a0, v0, p0 = sact.v, sdve.v, spe.v
        g0 = spool.v
        first = (which == 1)

        def bmean(t):
            return [psC[:, 0:512], psW[:, 0:512]][t % 2]

        def brstd(t):
            return [psD[:, 0:512], psW[:, 512:1024]][t % 2]

        with nc.Block(f"s{s}_ln{which}") as _b:
            blk = BW(_b)
            @blk.scalar
            def _(act):
                for t in range(NT):
                    for c in range(NCH):
                        bi = c
                        if first and t == 0 and c == 0:
                            act.wait_ge(h_dmx, xload)
                        act.copy(scb(LN_XB[bi], 0, 128, 512), xch(c, t * TW, TW))
                        sact.inc(act)
                    act.wait_ge(h_spe, p0 + t * 18 + 16)
                    act.activation(scf(LN_ROW0, 0, 1, 512), psA[0:1, 0:512],
                                   AF.Copy, scale=1.0 / 1024.0)
                    sact.inc(act)
                    act.wait_ge(h_sdve, v0 + t * LN_DVE_T + 9)
                    act.activation(scf(LN_ROW1, 0, 1, 512), scf(LN_ROW1, 0, 1, 512),
                                   AF.Ln, bias=EPS)
                    act.activation(scf(LN_ROW1, 0, 1, 512), scf(LN_ROW1, 0, 1, 512),
                                   AF.Exp, scale=-0.5)
                    sact.inc(act, 2)

            @blk.vector
            def _(v):
                for t in range(NT):
                    for c in range(NCH):
                        bi = c
                        v.wait_ge(h_sact, a0 + t * 11 + c + 1)
                        v.tensor_mul(scb(LN_SQ[bi], 0, 128, 512),
                                     scb(LN_XB[bi], 0, 128, 512),
                                     scb(LN_XB[bi], 0, 128, 512))
                        sdve.inc(v)
                    v.wait_ge(h_sact, a0 + t * 11 + 9)
                    v.scalar_tensor_tensor(scf(LN_ROW1, 0, 1, 512),
                                           scf(LN_ROW0, 0, 1, 512), 1.0,
                                           scf(LN_ROW0, 0, 1, 512), ALU.mult, ALU.mult)
                    v.wait_ge(h_spe, p0 + t * 18 + 16)
                    v.scalar_tensor_tensor(scf(LN_ROW1, 0, 1, 512), psB[0:1, 0:512],
                                           1.0 / 1024.0, scf(LN_ROW1, 0, 1, 512),
                                           ALU.mult, ALU.subtract)
                    sdve.inc(v)
                    for c in range(NCH):
                        ts = scf(LN_TS[c % 2], 0, 128, 512)
                        if c == 0:
                            v.wait_ge(h_spe, p0 + t * 18 + 17)
                        v.tensor_tensor(ts, xch(c, t * TW, TW), bmean(t),
                                        ALU.subtract)
                        if c == 0:
                            v.wait_ge(h_spe, p0 + t * 18 + 18)
                        v.tensor_tensor(lnch(c, t * TW, TW), ts, brstd(t),
                                        ALU.mult)
                        sdve.inc(v, 2)

            @blk.tensor
            def _(pe):
                for t in range(NT):
                    for c in range(NCH):
                        bi = c
                        pe.wait_ge(h_sact, a0 + t * 11 + c + 1)
                        pe.wait_ge(h_sdve, v0 + t * LN_DVE_T + c + 1)
                        if c == 0 and t > 0:
                            pe.wait_ge(h_sact, a0 + (t - 1) * 11 + 9)
                            pe.wait_ge(h_sdve, v0 + (t - 1) * 19 + 9)
                        pe.matmul(psA[0:1, 0:512], onesc[:, 0:1],
                                  scb(LN_XB[bi], 0, 128, 512),
                                  start=(c == 0), stop=(c == NCH - 1))
                        pe.matmul(psB[0:1, 0:512], onesc[:, 0:1],
                                  scb(LN_SQ[bi], 0, 128, 512),
                                  start=(c == 0), stop=(c == NCH - 1))
                        spe.inc(pe, 2)
                    pe.wait_ge(h_sact, a0 + t * 11 + 9)
                    if t >= 2:
                        pe.wait_ge(h_sdve, v0 + (t - 2) * LN_DVE_T + LN_DVE_T)
                        pass
                    pe.matmul(bmean(t), ones1f[:, :], scf(LN_ROW0, 0, 1, 512),
                              start=True, stop=True)
                    spe.inc(pe)
                    pe.wait_ge(h_sact, a0 + t * 11 + 11)
                    pe.matmul(brstd(t), ones1f[:, :], scf(LN_ROW1, 0, 1, 512),
                              start=True, stop=True)
                    spe.inc(pe)

        assert sact.v == a0 + NT * 11
        assert sdve.v == v0 + NT * LN_DVE_T
        assert spe.v == p0 + NT * 18
        
    def emit_attn_a(self, s):
        nc = self.nc
        h_spe, h_sact, h_sdve = self.h_spe, self.h_sact, self.h_sdve
        h_dmk, h_dmw2 = self.h_dmk, self.h_dmw2
        dmk, dmw2 = self.dmk, self.dmw2
        spe, sact, sdve = self.spe, self.sact, self.sdve
        scf, scb, wab, xch, lnch, bgch = (self.scf, self.scb, self.wab, self.xch,
                                          self.lnch, self.bgch)
        psA, psB, psC, psD, psW = self.psA, self.psB, self.psC, self.psD, self.psW
        rf_bf, sel2_f, bmu_sb = self.rf_bf, self.sel2_f, self.bmu_sb
        loads = {0: self._aa_g0}
        wout_cnt = [None]

        for gi in range(NG):
            wb = WA_WQKV[gi % 2]

            def wq(c, seg, wb=wb):
                return wab(wb, 0, 128, 3072).rearrange(
                    "p (c w) -> p c w", w=384)[:, c, seg * 128:(seg + 1) * 128]

            a0, v0, p0 = sact.v, sdve.v, spe.v
            dvo = 1 if gi == 0 else 0
            load_cnt = loads[gi]
            with nc.Block(f"s{s}_aa_g{gi}") as _b:
                blk = BW(_b)
                @blk.sync
                def _(sync):
                    if gi + 1 < NG:
                        dmk[(gi + 1) % 2].inc(sync.dma_start(
                            wab(WA_WQKV[(gi + 1) % 2], 0, 128, 3072)
                            .rearrange("p (c w) -> p c w", w=384),
                            self.WQKV_S[gi + 1].transpose([1, 0, 2])))
                        loads[gi + 1] = dmk[(gi + 1) % 2].v
                    else:
                        sync.wait_ge(h_spe, p0 + 60)
                        dmw2.inc(sync.dma_start(
                            wab(WA_WOUT, 0, 128, 8192).rearrange("p (c w) -> p c w", w=1024),
                            self.WOUTA_S.rearrange("(c p) w -> c p w", p=128)
                            .transpose([1, 0, 2])))
                        wout_cnt[0] = dmw2.v

                @blk.tensor
                def _(pe):
                    pe.wait_ge(h_dmk[gi % 2], load_cnt)
                    for t in range(NT):
                        pq = [psA, psC][t % 2]
                        pk = [psB, psD][t % 2]
                        if t >= 2:
                            pe.wait_ge(h_sact, a0 + (t - 2) * 2 + 2)
                        for c in range(NCH):
                            pe.matmul(pq[:, 0:512], wq(c, 0), lnch(c, t * TW, TW),
                                      start=(c == 0), stop=(c == NCH - 1))
                        spe.inc(pe)
                        for c in range(NCH):
                            pe.matmul(pk[:, 0:512], wq(c, 1), lnch(c, t * TW, TW),
                                      start=(c == 0), stop=(c == NCH - 1))
                        spe.inc(pe)
                    for t in range(NT):
                        pq = [psA, psC][t % 2]
                        wv = (t + 2) * 2 + 1 if t + 2 < NT else t * 2 + 1
                        pe.wait_ge(h_sact, a0 + wv)
                        if t >= 2:
                            pe.wait_ge(h_sdve, v0 + dvo + (t - 2) + 1)
                        pe.matmul(pq[:, 0:512], rf_bf[:, gi * 128:(gi + 1) * 128],
                                  scb(AA_QSC[t], 0, 128, 512), start=True, stop=True)
                        spe.inc(pe)
                    def kv_mm(j):
                        pe.wait_ge(h_sact, a0 + 8 + j * 2 + 2)
                        if gi == 0 and j < 2:
                            pe.wait_ge(h_sdve, v0 + 1)
                        pe.matmul(psW[:, 0:130], scb(AA_PHIK[j % 2], 0, 128, 128),
                                  scb(AA_VTK[j % 2], 0, 128, 130),
                                  start=(j == 0), stop=(j == 15))
                        spe.inc(pe)
                    for j in range(16):
                        t = j // 4
                        jj = j % 4
                        pe.wait_ge(h_sact, a0 + t * 2 + 2)
                        pv = [psA, psB][j % 2][:, 0:128]
                        pf = [psC, psD][j % 2][:, 0:128]
                        if j >= 2:
                            pe.wait_ge(h_sact, a0 + 8 + (j - 2) * 2 + 2)
                        elif j == 0:
                            pe.wait_ge(h_sdve, v0 + dvo + 3)
                        elif j == 1:
                            pe.wait_ge(h_sact, a0 + 6)
                        for c in range(NCH):
                            pe.matmul(pv, lnch(c, j * 128, 128), wq(c, 2),
                                      start=(c == 0), stop=(c == NCH - 1))
                        spe.inc(pe)
                        if j == 0:
                            pe.wait_ge(h_sdve, v0 + dvo + 4)
                        pe.matmul(pf, scb(AA_KSC[t], 0, 128, 512)[:, jj * 128:(jj + 1) * 128],
                                  rf_bf[:, gi * 128:(gi + 1) * 128], start=True, stop=True)
                        spe.inc(pe)
                        if j >= 1:
                            kv_mm(j - 1)
                    kv_mm(15)

                @blk.scalar
                def _(act):
                    for t in range(NT):
                        pq = [psA, psC][t % 2]
                        pk = [psB, psD][t % 2]
                        act.wait_ge(h_spe, p0 + t * 2 + 1)
                        act.copy(scb(AA_QSC[t], 0, 128, 512), pq[:, 0:512])
                        sact.inc(act)
                        act.wait_ge(h_spe, p0 + t * 2 + 2)
                        act.copy(scb(AA_KSC[t], 0, 128, 512), pk[:, 0:512])
                        sact.inc(act)
                    for j in range(16):
                        pv = [psA, psB][j % 2][:, 0:128]
                        pf = [psC, psD][j % 2][:, 0:128]
                        vj = 13 if j == 0 else 3 * j + 12
                        act.wait_ge(h_spe, p0 + vj)
                        act.copy(scb(AA_VTK[j % 2], 0, 128, 130)[:, 0:128], pv)
                        sact.inc(act)
                        act.wait_ge(h_spe, p0 + vj + 1)
                        act.activation(scb(AA_PHIK[j % 2], 0, 128, 128), pf, AF.Relu)
                        sact.inc(act)
                    act.wait_ge(h_spe, p0 + 12 + 48)
                    if gi == 0:
                        act.wait_ge(h_sdve, v0 + 1)
                    kvbd = scb(AA_KVBD, 0, 128, 1024)
                    ksum = scb(AA_KSUM, 0, 128, 16)
                    act.copy(kvbd[0:64, gi * 128: gi * 128 + 64], psW[0:64, 0:64])
                    act.copy(kvbd[64:128, gi * 128 + 64: gi * 128 + 128],
                             psW[64:128, 64:128])
                    act.copy(ksum[0:64, 2 * gi: 2 * gi + 1], psW[0:64, 128:129])
                    act.copy(ksum[64:128, 2 * gi + 1: 2 * gi + 2], psW[64:128, 128:129])
                    sact.inc(act, 4)

                @blk.vector
                def _(v):
                    if gi == 0:
                        v.memset(scb(AA_KVBD, 0, 128, 1024), 0.0)
                        v.memset(scb(AA_KSUM, 0, 128, 16), 0.0)
                        v.memset(scb(AA_VTK[0], 0, 128, 130)[:, 128:129], 1.0)
                        v.memset(scb(AA_VTK[1], 0, 128, 130)[:, 128:129], 1.0)
                        sdve.inc(v)
                    for t in range(NT):
                        pq = [psA, psC][t % 2]
                        v.wait_ge(h_spe, p0 + 8 + t + 1)
                        v.tensor_scalar_max(bgch(gi, t * TW, TW), pq[:, 0:512], 0.0)
                        sdve.inc(v)

            assert spe.v == p0 + 60
            assert sact.v == a0 + 44
            assert sdve.v == v0 + 4 + (1 if gi == 0 else 0)

        a0, v0, p0 = sact.v, sdve.v, spe.v
        with nc.Block(f"s{s}_aa_B") as _b:
            blk = BW(_b)
            def peV(k):
                t, gi = divmod(k, 8)
                return p0 + t * 32 + gi * 3

            def actV(k):
                t, gi = divmod(k, 8)
                return a0 + t * 32 + gi * 3

            @blk.tensor
            def _(pe):
                kvbd = scb(AA_KVBD, 0, 128, 1024)
                ksum = scb(AA_KSUM, 0, 128, 16)
                for t in range(NT):
                    for gi in range(NG):
                        k = t * 8 + gi
                        po = [psA, psC][gi % 2]
                        if k >= 2:
                            kk = k - 2
                            pe.wait_ge(h_sdve, v0 + (kk // 8) * 16 + (kk % 8) + 1)
                        pe.matmul(po[:, 0:512], kvbd[:, gi * 128:(gi + 1) * 128],
                                  bgch(gi, t * TW, TW), start=True, stop=True)
                        spe.inc(pe)
                        if k >= 1:
                            pe.wait_ge(h_sact, actV(k - 1) + 2)
                        pe.matmul(psB[0:2, 0:512], ksum[:, 2 * gi:2 * gi + 2],
                                  bgch(gi, t * TW, TW), start=True, stop=True)
                        spe.inc(pe)
                        pe.wait_ge(h_sact, actV(k) + 2)
                        pe.matmul(psD[:, 0:512], sel2_f[:, :], scf(AB_NRM, 0, 2, 512),
                                  start=True, stop=True)
                        spe.inc(pe)
                    pe.wait_ge(h_sdve, v0 + t * 16 + 8)
                    pe.wait_ge(h_dmw2, wout_cnt[0])
                    for d in range(NCH):
                        pw = psW[:, (d % 4) * 512:(d % 4) * 512 + 512]
                        k = t * 8 + d
                        if k >= 4:
                            kk = k - 4
                            pe.wait_ge(h_sdve, v0 + (kk // 8) * 16 + 8 + (kk % 8) + 1)
                        for c in range(NCH):
                            pe.matmul(pw, wab(WA_WOUT, 0, 128, 8192).rearrange(
                                "p (c w) -> p c w", w=1024)[:, c, d * 128:(d + 1) * 128],
                                scb(AB_ATT + c * 1024, 0, 128, 512),
                                start=(c == 0), stop=(c == NCH - 1))
                        spe.inc(pe)

            @blk.scalar
            def _(act):
                for t in range(NT):
                    for gi in range(NG):
                        k = t * 8 + gi
                        act.wait_ge(h_spe, peV(k) + 2)
                        act.activation(scf(AB_NRM, 0, 2, 512), psB[0:2, 0:512],
                                       AF.Ln, bias=STAB)
                        act.activation(scf(AB_NRM, 0, 2, 512), scf(AB_NRM, 0, 2, 512),
                                       AF.Exp, scale=-1.0)
                        sact.inc(act, 2)
                        act.wait_ge(h_spe, peV(k) + 3)
                        if k >= 2:
                            kk = k - 2
                            act.wait_ge(h_sdve, v0 + (kk // 8) * 16 + (kk % 8) + 1)
                        act.copy(scf(AB_TSB[gi % 2], 0, 128, 512), psD[:, 0:512])
                        sact.inc(act)
                    for d in range(NCH):
                        k = t * 8 + d
                        pw = psW[:, (d % 4) * 512:(d % 4) * 512 + 512]
                        act.wait_ge(h_spe, p0 + t * 32 + 24 + d + 1)
                        if k >= 2:
                            kk = k - 2
                            act.wait_ge(h_sdve, v0 + (kk // 8) * 16 + 8 + (kk % 8) + 1)
                        act.activation(scf(AB_TSB2[d % 2], 0, 128, 512), pw, AF.Identity,
                                       bias=bmu_sb[:, d:d + 1])
                        sact.inc(act)

            @blk.vector
            def _(v):
                for t in range(NT):
                    for gi in range(NG):
                        k = t * 8 + gi
                        po = [psA, psC][gi % 2]
                        v.wait_ge(h_sact, actV(k) + 3)
                        if t >= 1 and gi == 0:
                            v.wait_ge(h_spe, p0 + t * 32)
                        v.tensor_tensor(scb(AB_ATT + gi * 1024, 0, 128, 512),
                                        po[:, 0:512], scf(AB_TSB[gi % 2], 0, 128, 512),
                                        ALU.mult)
                        sdve.inc(v)
                    for d in range(NCH):
                        v.wait_ge(h_sact, a0 + t * 32 + 24 + d + 1)
                        v.tensor_tensor(xch(d, t * TW, TW), xch(d, t * TW, TW),
                                        scf(AB_TSB2[d % 2], 0, 128, 512), ALU.add)
                        sdve.inc(v)

        assert spe.v == p0 + NT * 32
        assert sact.v == a0 + NT * 32
        assert sdve.v == v0 + NT * 16

    def emit_ffn(self, s, which, store=False):
        nc = self.nc
        h_spe, h_sact, h_sdve = self.h_spe, self.h_sact, self.h_sdve
        h_dmf, dmf = self.h_dmf, self.dmf
        dmo, spe, sact, sdve = self.dmo, self.spe, self.sact, self.sdve
        scf, scb, wab, xch, lnch = self.scf, self.scb, self.wab, self.xch, self.lnch
        psA, psB, psC, psD = self.psA, self.psB, self.psC, self.psD
        bmu_sb, Y = self.bmu_sb, self.Y
        h_dmo = self.h_dmo
        MU_R = bmu_sb[:, 16:24]
        MU_K = bmu_sb[:, 24:32]

        def wsl(WS, d):
            return WS.rearrange("(c p) w -> c p w", p=128)[:, :, d * 128:(d + 1) * 128]

        A0, V0, P0 = sact.v, sdve.v, spe.v
        F0 = [dmf[b].v for b in range(4)]

        def pb(t):
            return P0 + 24 * t

        def ab(t):
            return A0 + 24 * t

        def vb(t):
            return V0 + 40 * t

        def fb(b, t):
            return F0[b] + (4 if b < 2 else 8) * 16 * t

        with nc.Block(f"s{s}_ffn{which}") as _b:
            blk = BW(_b)

            def wk(b, c):
                return wab(WF_K[b], 0, 128, 1024).rearrange("p (c w) -> p c w", w=128)[:, c, :]

            def wr(b, c):
                return wab(WF_R[b], 0, 128, 1024).rearrange("p (c w) -> p c w", w=128)[:, c, :]

            def wv(b, c):
                return wab(WF_V[b], 0, 128, 1024).rearrange("p (c w) -> p c w", w=128)[:, c, :]

            @blk.sync
            def _(sync):
                for t in range(NT):
                    for d in range(NCH):
                        if d >= 2:
                            sync.wait_ge(h_spe, pb(t) + (d - 2) + 1)
                        elif t > 0:
                            sync.wait_ge(h_spe, pb(t - 1) + (d + 6) + 1)
                        dmf[d % 2].inc(sync.dma_start(
                            wab(WF_K[d % 2], 0, 128, 1024)
                            .rearrange("p (c w) -> p c w", w=128),
                            wsl(self.WKT_S, d).transpose([1, 0, 2])))
                    for d in range(NCH):
                        if d >= 2:
                            sync.wait_ge(h_spe, pb(t) + 8 + (d - 2) * 2 + 2)
                        elif t > 0:
                            sync.wait_ge(h_spe, pb(t - 1) + 8 + (d + 6) * 2 + 2)
                        dmf[2 + d % 2].inc(sync.dma_start(
                            wab(WF_R[d % 2], 0, 128, 1024)
                            .rearrange("p (c w) -> p c w", w=128),
                            wsl(self.WRT_S, d).transpose([1, 0, 2])))
                        dmf[2 + d % 2].inc(sync.dma_start(
                            wab(WF_V[d % 2], 0, 128, 1024)
                            .rearrange("p (c w) -> p c w", w=128),
                            wsl(self.WVT_S, d).transpose([1, 0, 2])))

            @blk.vector
            def _(v):
                for t in range(NT):
                    for c in range(NCH):
                        dd = scb(FF_DD[c % 2], 0, 128, 512)
                        if t > 0 and c == 0:
                            v.wait_ge(h_spe, pb(t - 1) + 8)
                        v.tensor_sub(dd, lnch(c, t * TW, TW), lnch(c, t * TW, TW, shift=1))
                        v.scalar_tensor_tensor(scb(FF_XK + c * 1024, 0, 128, 512), dd,
                                               MU_K[:, c:c + 1],
                                               lnch(c, t * TW, TW, shift=1),
                                               ALU.mult, ALU.add)
                        sdve.inc(v, 2)
                    for c in range(NCH):
                        dd = scb(FF_DD[c % 2], 0, 128, 512)
                        if t > 0 and c == 0:
                            v.wait_ge(h_spe, pb(t - 1) + 8 + 2 * 7 + 1)
                        v.tensor_sub(dd, lnch(c, t * TW, TW), lnch(c, t * TW, TW, shift=1))
                        v.scalar_tensor_tensor(scb(FF_XR + c * 1024, 0, 128, 512), dd,
                                               MU_R[:, c:c + 1],
                                               lnch(c, t * TW, TW, shift=1),
                                               ALU.mult, ALU.add)
                        sdve.inc(v)
                    for d in range(NCH):
                        pv = [psC, psD][d % 2]
                        v.wait_ge(h_spe, pb(t) + 8 + d * 2 + 2)
                        v.wait_ge(h_sact, ab(t) + 16 + d + 1)
                        tmp = scf(FF_TMP[d % 2], 0, 128, 512)
                        v.tensor_tensor(tmp, scf(FF_RSB[d % 2], 0, 128, 512),
                                        pv[:, 0:512], ALU.mult)
                        v.tensor_tensor(xch(d, t * TW, TW), xch(d, t * TW, TW),
                                        tmp, ALU.add)
                        sdve.inc(v, 2)

            @blk.tensor
            def _(pe):
                for t in range(NT):
                    for d in range(NCH):
                        pk = [psA, psB][d % 2]
                        pe.wait_ge(h_dmf[d % 2], fb(d % 2, t) + 16 * (d // 2 + 1))
                        if d >= 2:
                            pe.wait_ge(h_sact, ab(t) + 2 * (d - 2) + 1)
                        elif t > 0:
                            pe.wait_ge(h_sact, ab(t - 1) + 16 + (d + 6) + 1)
                        for c in range(NCH):
                            if d == 0:
                                pe.wait_ge(h_sdve, vb(t) + 2 * (c + 1))
                            pe.matmul(pk[:, 0:512], wk(d % 2, c),
                                      scb(FF_XK + c * 1024, 0, 128, 512),
                                      start=(c == 0), stop=(c == NCH - 1))
                        spe.inc(pe)
                    for d in range(NCH):
                        pr = [psA, psB][d % 2]
                        pv = [psC, psD][d % 2]
                        pe.wait_ge(h_dmf[2 + d % 2], fb(2 + d % 2, t) + 32 * (d // 2 + 1))
                        if d >= 2:
                            pe.wait_ge(h_sact, ab(t) + 16 + (d - 2) + 1)
                        else:
                            pe.wait_ge(h_sact, ab(t) + 2 * (6 + d) + 2)
                        for c in range(NCH):
                            if d == 0:
                                pe.wait_ge(h_sdve, vb(t) + 16 + c + 1)
                            pe.matmul(pr[:, 0:512], wr(d % 2, c),
                                      scb(FF_XR + c * 1024, 0, 128, 512),
                                      start=(c == 0), stop=(c == NCH - 1))
                        spe.inc(pe)
                        pe.wait_ge(h_sact, ab(t) + 16)
                        if d >= 2:
                            pe.wait_ge(h_sdve, vb(t) + 24 + (d - 2) * 2 + 1)
                        elif t > 0:
                            pe.wait_ge(h_sdve, vb(t - 1) + 24 + (d + 6) * 2 + 1)
                        for c in range(NCH):
                            pe.matmul(pv[:, 0:512], wv(d % 2, c),
                                      scb(FF_KACT + c * 1024, 0, 128, 512),
                                      start=(c == 0), stop=(c == NCH - 1))
                        spe.inc(pe)

            @blk.scalar
            def _(act):
                for t in range(NT):
                    for d in range(NCH):
                        pk = [psA, psB][d % 2]
                        act.wait_ge(h_spe, pb(t) + d + 1)
                        if t > 0 and d < 2:
                            act.wait_ge(h_spe, pb(t - 1) + 8 + 2 * 7 + 2)
                        ka = scb(FF_KACT + d * 1024, 0, 128, 512)
                        act.activation(ka, pk[:, 0:512], AF.Relu)
                        sact.inc(act)
                        act.activation(ka, ka, AF.Square)
                        sact.inc(act)
                    for d in range(NCH):
                        pr = [psA, psB][d % 2]
                        act.wait_ge(h_spe, pb(t) + 8 + d * 2 + 1)
                        if d >= 2:
                            act.wait_ge(h_sdve, vb(t) + 24 + (d - 2) * 2 + 1)
                        elif t > 0:
                            act.wait_ge(h_sdve, vb(t - 1) + 24 + (d + 6) * 2 + 1)
                        act.activation(scf(FF_RSB[d % 2], 0, 128, 512), pr[:, 0:512],
                                       AF.Sigmoid)
                        sact.inc(act)

            if store:
                @blk.gpsimd
                def _(gp):
                    for t in range(NT):
                        for c in range(NCH):
                            gp.wait_ge(h_sdve, vb(t) + 24 + c * 2 + 2)
                            dmo.inc(gp.dma_start(
                                Y[s, c * 128:(c + 1) * 128, t * TW:(t + 1) * TW],
                                xch(c, t * TW, TW)))

        assert spe.v == P0 + NT * 24
        assert sact.v == A0 + NT * 24
        assert sdve.v == V0 + NT * 40

    def emit_attn_b(self, s):
        nc = self.nc
        h_dmo, h_spe, h_sact, h_sdve, h_spl = (self.h_dmo, self.h_spe,
                                               self.h_sact, self.h_sdve,
                                               self.h_spl)
        h_dmc, h_dmw2 = self.h_dmc, self.h_dmw2
        dmc, dmw2 = self.dmc, self.dmw2
        dmo, spe, sact, sdve, spl = (self.dmo, self.spe, self.sact,
                                     self.sdve, self.spl)
        scf, scb, wab, waf, xch, lnch, bgch = (self.scf, self.scb, self.wab,
                                               self.waf, self.xch, self.lnch,
                                               self.bgch)
        psA, psB, psC, psD, psW = self.psA, self.psB, self.psC, self.psD, self.psW
        s128t_bf, ones1b, bmu_sb = self.s128t_bf, self.ones1b, self.bmu_sb

        with nc.Block(f"s{s}_bp_w") as _b:
            blk = BW(_b)
            @blk.sync
            def _(sync):
                dmc.inc(sync.dma_start(
                    scf(0, 0, 128, 512).rearrange("p (c w) -> p c w", w=64),
                    self.WQA.rearrange("(c p) w -> p c w", p=128)))
                dmc.inc(sync.dma_start(scf(2048, 0, 64, 1024), self.WQB))
                dmc.inc(sync.dma_start(scf(6144, 0, 32, 512), self.WKB))
                dmc.inc(sync.dma_start(scf(8192, 0, 32, 256), self.WVB))
                dmc.inc(sync.dma_start(
                    scf(9216, 0, 128, 256).rearrange("p (c w) -> p c w", w=32),
                    self.WKVA.rearrange("(c p) w -> p c w", p=128)))

            @blk.vector
            def _(v):
                v.wait_ge(h_dmc, dmc.v)
                v.tensor_copy(wab(WB_WQA, 0, 128, 512), scf(0, 0, 128, 512))
                v.tensor_copy(wab(WB_WQB, 0, 64, 1024), scf(2048, 0, 64, 1024))
                v.tensor_copy(wab(WB_WKB, 0, 32, 512), scf(6144, 0, 32, 512))
                v.tensor_copy(wab(WB_WVB, 0, 32, 256), scf(8192, 0, 32, 256))
                v.tensor_copy(wab(WB_WKVA, 0, 128, 256), scf(9216, 0, 128, 256))
                sdve.inc(v)

        for NAME, DST in ((self.COSM, BP_COS), (self.SINM, BP_SIN)):
            with nc.Block(f"s{s}_bp_rope{DST}") as _b:
                blk = BW(_b)
                @blk.sync
                def _(sync):
                    sync.wait_ge(h_sdve, sdve.v)
                    dmc.inc(sync.dma_start(scf(BP_STAGE, 0, 128, 2048), NAME))

                @blk.vector
                def _(v):
                    v.wait_ge(h_dmc, dmc.v)
                    v.tensor_copy(scb(DST, 0, 128, 2048), scf(BP_STAGE, 0, 128, 2048))
                    sdve.inc(v)

        a0, p0 = sact.v, spe.v
        with nc.Block(f"s{s}_bp_lat") as _b:
            blk = BW(_b)
            @blk.tensor
            def _(pe):
                for t in range(NT):
                    pq = [psA, psB][t % 2]
                    pk = [psC, psD][t % 2]
                    if t >= 2:
                        pe.wait_ge(h_sact, a0 + (t - 2) * 2 + 2)
                    for c in range(NCH):
                        pe.matmul(pq[0:64, 0:512],
                                  wab(WB_WQA, 0, 128, 512)
                                  .rearrange("p (c w) -> p c w", w=64)[:, c, :],
                                  lnch(c, t * TW, TW), start=(c == 0), stop=(c == NCH - 1))
                    spe.inc(pe)
                    for c in range(NCH):
                        pe.matmul(pk[0:32, 0:512],
                                  wab(WB_WKVA, 0, 128, 256)
                                  .rearrange("p (c w) -> p c w", w=32)[:, c, :],
                                  lnch(c, t * TW, TW), start=(c == 0), stop=(c == NCH - 1))
                    spe.inc(pe)

            @blk.scalar
            def _(act):
                for t in range(NT):
                    pq = [psA, psB][t % 2]
                    pk = [psC, psD][t % 2]
                    act.wait_ge(h_spe, p0 + t * 2 + 1)
                    act.copy(scb(BP_LQ, 0, 64, 2048)[:, t * TW:(t + 1) * TW],
                             pq[0:64, 0:512])
                    sact.inc(act)
                    act.wait_ge(h_spe, p0 + t * 2 + 2)
                    act.copy(scb(BP_LKV, 0, 32, 2048)[:, t * TW:(t + 1) * TW],
                             pk[0:32, 0:512])
                    sact.inc(act)

        assert spe.v == p0 + 8 and sact.v == a0 + 8

        for phase, nchunks in (("q", NCH), ("k", 4)):
            a0, v0, p0 = sact.v, sdve.v, spe.v
            with nc.Block(f"s{s}_bp_{phase}") as _b:
                blk = BW(_b)
                @blk.tensor
                def _(pe):
                    n_i = nchunks * NT

                    def qs_mm(i):
                        ps_ = [psC, psD][i % 2]
                        pe.wait_ge(h_sact, a0 + i + 1)
                        if i >= 2:
                            pe.wait_ge(h_sdve, v0 + (i - 2) * 2 + 2)
                        pe.matmul(ps_[:, 0:512], s128t_bf[:, :],
                                  wab(WB_QSBT[i % 2], 0, 128, 512),
                                  start=True, stop=True)
                        spe.inc(pe)
                    for c in range(nchunks):
                        for t in range(NT):
                            i = c * NT + t
                            pq = [psA, psB][i % 2]
                            if i >= 2:
                                pe.wait_ge(h_sact, a0 + (i - 2) + 1)
                            if phase == "q":
                                pe.matmul(pq[:, 0:512],
                                          wab(WB_WQB, 0, 64, 1024)[:, c * 128:(c + 1) * 128],
                                          scb(BP_LQ, 0, 64, 2048)[:, t * TW:(t + 1) * TW],
                                          start=True, stop=True)
                            else:
                                pe.matmul(pq[:, 0:512],
                                          wab(WB_WKB, 0, 32, 512)[:, c * 128:(c + 1) * 128],
                                          scb(BP_LKV, 0, 32, 2048)[:, t * TW:(t + 1) * TW],
                                          start=True, stop=True)
                            spe.inc(pe)
                            if i >= 1:
                                qs_mm(i - 1)
                    qs_mm(n_i - 1)

                @blk.scalar
                def _(act):
                    for c in range(nchunks):
                        for t in range(NT):
                            i = c * NT + t
                            pq = [psA, psB][i % 2]
                            act.wait_ge(h_spe, p0 + (1 if i == 0 else 2 * i))
                            if i >= 2:
                                act.wait_ge(h_sdve, v0 + (i - 2) * 2 + 2)
                            act.copy(wab(WB_QSBT[i % 2], 0, 128, 512), pq[:, 0:512])
                            sact.inc(act)

                @blk.vector
                def _(v):
                    for c in range(nchunks):
                        for t in range(NT):
                            i = c * NT + t
                            n_i = nchunks * NT
                            ps_ = [psC, psD][i % 2]
                            v.wait_ge(h_spe, p0 + (2 * i + 3 if i < n_i - 1 else 2 * n_i))
                            t1 = wab(WB_RT1, 0, 128, 512)
                            t2 = wab(WB_RT2, 0, 128, 512)
                            v.tensor_tensor(t1, ps_[:, 0:512],
                                            scb(BP_SIN, 0, 128, 2048)[:, t * TW:(t + 1) * TW],
                                            ALU.mult)
                            v.tensor_tensor(t2, wab(WB_QSBT[i % 2], 0, 128, 512),
                                            scb(BP_COS, 0, 128, 2048)[:, t * TW:(t + 1) * TW],
                                            ALU.mult)
                            if phase == "q":
                                dst = bgch(c, t * TW, TW)
                            else:
                                dst = scb(BP_KREP + c * 4096, 0, 128, 2048)[:, t * TW:(t + 1) * TW]
                            v.tensor_tensor(dst, t2, t1, ALU.add)
                            sdve.inc(v, 2)

            assert spe.v == p0 + nchunks * NT * 2
            assert sact.v == a0 + nchunks * NT
            assert sdve.v == v0 + nchunks * NT * 2

        a0, v0, p0 = sact.v, sdve.v, spe.v
        with nc.Block(f"s{s}_bp_v") as _b:
            blk = BW(_b)
            @blk.vector
            def _(v):
                vt = scb(BS_VTOK, 0, 128, 16 * 260)
                v.memset(vt.rearrange("p (j g w) -> p j g w", g=4, w=65)[:, :, :, 64:65],
                         1.0)
                sdve.inc(v)

            @blk.tensor
            def _(pe):
                for j in range(16):
                    pv = [psA, psB][j % 2]
                    if j >= 2:
                        pe.wait_ge(h_sact, a0 + (j - 2) + 1)
                    pe.matmul(pv[:, 0:256],
                              scb(BP_LKV, 0, 32, 2048)[:, j * 128:(j + 1) * 128],
                              wab(WB_WVB, 0, 32, 256), start=True, stop=True)
                    spe.inc(pe)

            @blk.scalar
            def _(act):
                vt = scb(BS_VTOK, 0, 128, 16 * 260)
                for j in range(16):
                    pv = [psA, psB][j % 2]
                    act.wait_ge(h_spe, p0 + j + 1)
                    act.copy(vt[:, j * 260: j * 260 + 260]
                             .rearrange("p (g w) -> p g w", w=65)[:, :, 0:64],
                             pv[:, 0:256].rearrange("p (g w) -> p g w", w=64))
                    sact.inc(act)

        assert spe.v == p0 + 16 and sact.v == a0 + 16

        njs = [4 * i + 4 for i in range(NT)]
        a0, v0, p0, l0 = sact.v, sdve.v, spe.v, spl.v

        ops = []
        pend_pv = None
        pend_bc = None
        for pi in range(NPAIR):
            for i in range(NT):
                for j in range(njs[i]):
                    ops.append(("scA", pi, i, j))
                    ops.append(("scB", pi, i, j))
                    if pend_bc is not None and j == 1:
                        ops.append(("bcA",) + pend_bc)
                        ops.append(("bcB",) + pend_bc)
                        pend_bc = None
                    if pend_pv is not None:
                        ops.append(("pvA",) + pend_pv)
                        ops.append(("pvB",) + pend_pv)
                    pend_pv = (pi, i, j)
                ops.append(("pvA",) + pend_pv)
                ops.append(("pvB",) + pend_pv)
                pend_pv = None
                assert pend_bc is None
                pend_bc = (pi, i)
        ops.append(("bcA",) + pend_bc)
        ops.append(("bcB",) + pend_bc)
        val = {op: k + 1 for k, op in enumerate(ops)}

        def actb(p, i):
            return a0 + p * sum(2 * n + 4 for n in njs) + sum(2 * njs[k] + 4 for k in range(i))

        def dveb(p, i):
            return v0 + p * NT * 20 + 20 * i

        def splb(p, i):
            return l0 + 16 * (p * NT + i)

        def pv_done(pi, i, j):
            return p0 + val[("pvB", pi, i, j)]

        def prev_pv_all(pi, i):
            if i > 0:
                return pv_done(pi, i - 1, njs[i - 1] - 1)
            if pi > 0:
                return pv_done(pi - 1, NT - 1, njs[NT - 1] - 1)
            return 0

        PAb = [BS_PA0, BS_PA0 + 1024, BS_PA0 + 2048]
        PBb = [BS_PB0, BS_PB0 + 1024, BS_PB0 + 2048]
        ohsbB = self.ohsb_t[:, :]
        bw_ld = [None]

        with nc.Block(f"s{s}_bs") as _b:
            blk = BW(_b)

            @blk.sync
            def _(sync):
                dmw2.inc(sync.dma_start(
                    wab(WA_WOUT, 0, 128, 8192).rearrange("p (c w) -> p c w", w=1024),
                    self.WOUTB_S.rearrange("(c p) w -> c p w", p=128).transpose([1, 0, 2])))
                bw_ld[0] = dmw2.v

            @blk.tensor
            def _(pe):
                for op in ops:
                    kind, pi = op[0], op[1]
                    gkv = pi // 2
                    if kind in ("scA", "scB"):
                        _, pi, i, j = op
                        q0 = max(128 * j, TW * i)
                        w = TW * (i + 1) - q0
                        ps_ = [[psA, psC], [psB, psD]][kind == "scB"][j % 2]
                        if kind == "scA":
                            if j >= 2:
                                pe.wait_ge(h_sact, actb(pi, i) + 2 * (j - 2) + 2)
                            elif i > 0:
                                pe.wait_ge(h_sact, actb(pi, i - 1) + 2 * njs[i - 1])
                            elif pi > 0:
                                pe.wait_ge(h_sact, actb(pi - 1, NT - 1) + 2 * njs[NT - 1])
                        base = 0 if kind == "scA" else 64
                        pe.matmul(ps_[:, 0:w],
                                  scb(BP_KREP + gkv * 4096, 0, 128, 2048)
                                  [base:base + 64, j * 128:(j + 1) * 128],
                                  bgch(pi, q0, w, p0=base, np_=64),
                                  start=True, stop=True)
                        spe.inc(pe)
                    elif kind in ("pvA", "pvB"):
                        _, pi, i, j = op
                        if kind == "pvA":
                            pe.wait_ge(h_sact, actb(pi, i) + 2 * j + 2)
                            if j >= 4 * i:
                                jj = j - 4 * i
                                pe.wait_ge(h_sdve, dveb(pi, i) + 4 * jj + 4)
                            if j == 0:
                                if i > 0 or pi > 0:
                                    pe.wait_ge(h_sdve, dveb(pi, i))
                                    pe.wait_ge(h_spl, splb(pi, i))
                        vsl = scb(BS_VTOK, 0, 128, 16 * 260)[
                            :, j * 260 + gkv * 65: j * 260 + gkv * 65 + 65]
                        pbuf = scb((PAb if kind == "pvA" else PBb)[j % 3], 0, 128, 512)
                        col = 0 if kind == "pvA" else 512
                        pe.matmul(psW[0:65, col:col + 512], vsl, pbuf,
                                  start=(j == 0), stop=(j == njs[i] - 1))
                        spe.inc(pe)
                    elif kind == "bcA":
                        _, pi, i = op
                        pe.wait_ge(h_sact, actb(pi, i) + 2 * njs[i] + 2)
                        pe.wait_ge(h_sdve, dveb(pi, i))
                        pe.matmul(psW[0:64, 1024:1536], ones1b[:, :],
                                  self.recs_bf[:, 0:512], start=True, stop=True)
                        spe.inc(pe)
                    else:
                        _, pi, i = op
                        pe.wait_ge(h_sact, actb(pi, i) + 2 * njs[i] + 4)
                        pe.matmul(psW[64:128, 1536:2048], ones1b[:, :],
                                  self.recs_bf[:, 512:1024], start=True, stop=True,
                                  tile_position=(0, 64))
                        spe.inc(pe)

            @blk.scalar
            def _(act):
                for pi in range(NPAIR):
                    for i in range(NT):
                        for j in range(njs[i]):
                            q0 = max(128 * j, TW * i)
                            w = TW * (i + 1) - q0
                            poff = q0 - TW * i
                            psSA = [psA, psC][j % 2]
                            psSB = [psB, psD][j % 2]
                            pA = scb(PAb[j % 3], 0, 128, 512)
                            pB = scb(PBb[j % 3], 0, 128, 512)
                            act.wait_ge(h_spe, p0 + val[("scB", pi, i, j)])
                            if j >= 3:
                                act.wait_ge(h_spe, pv_done(pi, i, j - 3))
                            else:
                                pw = prev_pv_all(pi, i)
                                if pw:
                                    act.wait_ge(h_spe, pw)
                            act.activation(pA[:, poff:poff + w], psSA[:, 0:w],
                                           AF.Exp, scale=0.125)
                            act.activation(pB[:, poff:poff + w], psSB[:, 0:w],
                                           AF.Exp, scale=0.125)
                            sact.inc(act, 2)
                        act.wait_ge(h_spe, pv_done(pi, i, njs[i] - 1))
                        act.activation(self.drow_f[:, :], psW[64:65, 0:512], AF.Ln)
                        act.activation(self.recs_bf[:, 0:512], self.drow_f[:, :],
                                       AF.Exp, scale=-1.0)
                        sact.inc(act, 2)
                        act.activation(self.drow_f[:, :], psW[64:65, 512:1024], AF.Ln)
                        act.activation(self.recs_bf[:, 512:1024], self.drow_f[:, :],
                                       AF.Exp, scale=-1.0)
                        sact.inc(act, 2)

            @blk.gpsimd
            def _(gp):
                for pi in range(NPAIR):
                    for i in range(NT):
                        gp.wait_ge(h_sdve, dveb(pi, i) + 18)
                        spl.inc(gp.dma_start(
                            self.lnx[64:128,
                                     pi * (T + 1) + 1 + i * TW: pi * (T + 1) + 1 + (i + 1) * TW],
                            ohsbB), 16)

            @blk.vector
            def _(v):
                for pi in range(NPAIR):
                    for i in range(NT):
                        for jj in range(4):
                            j = 4 * i + jj
                            q0 = 128 * j
                            poff = q0 - TW * i
                            pA = scb(PAb[j % 3], 0, 128, 512)
                            pB = scb(PBb[j % 3], 0, 128, 512)
                            if poff > 0:
                                if j >= 3:
                                    v.wait_ge(h_spe, pv_done(pi, i, j - 3))
                                v.memset(pA[:, 0:poff], 0.0)
                                v.memset(pB[:, 0:poff], 0.0)
                                sdve.inc(v, 2)
                            else:
                                sdve.pad(v, 2)
                            v.wait_ge(h_sact, actb(pi, i) + 2 * j + 2)
                            v.tensor_mul(pA[:, poff:poff + 128],
                                         pA[:, poff:poff + 128],
                                         self.maskd_bf[:, :])
                            v.tensor_mul(pB[:, poff:poff + 128],
                                         pB[:, poff:poff + 128],
                                         self.maskd_bf[:, :])
                            sdve.inc(v, 2)
                        v.wait_ge(h_spe, pv_done(pi, i, njs[i] - 1))
                        v.tensor_copy(scb(BS_OHSBA, 0, 128, 512)[0:64, :],
                                      psW[0:64, 0:512])
                        sdve.inc(v)
                        v.wait_ge(h_spl, splb(pi, i))
                        v.tensor_copy(ohsbB, psW[0:64, 512:1024])
                        sdve.inc(v)
                        v.wait_ge(h_spe, p0 + val[("bcA", pi, i)])
                        v.tensor_tensor(
                            self.lnx[0:64, pi * (T + 1) + 1 + i * TW:
                                     pi * (T + 1) + 1 + (i + 1) * TW],
                            scb(BS_OHSBA, 0, 128, 512)[0:64, :],
                            psW[0:64, 1024:1536], ALU.mult)
                        sdve.inc(v)
                        v.wait_ge(h_spe, p0 + val[("bcB", pi, i)])
                        v.wait_ge(h_spl, splb(pi, i) + 16)
                        ln_b = self.lnx[64:128, pi * (T + 1) + 1 + i * TW:
                                        pi * (T + 1) + 1 + (i + 1) * TW]
                        v.tensor_tensor(ln_b, ln_b, psW[64:128, 1536:2048], ALU.mult)
                        sdve.inc(v)

        assert spe.v == p0 + len(ops)
        assert sact.v == a0 + NPAIR * sum(2 * n + 4 for n in njs)
        assert sdve.v == v0 + NPAIR * NT * 20
        assert spl.v == l0 + 16 * NPAIR * NT

        a0, v0, p0 = sact.v, sdve.v, spe.v
        with nc.Block(f"s{s}_bw") as _b:
            blk = BW(_b)

            @blk.tensor
            def _(pe):
                pe.wait_ge(h_dmw2, bw_ld[0])
                for t in range(NT):
                    for d in range(NCH):
                        i = t * NCH + d
                        pw = [psA, psB, psC, psD][i % 4]
                        if i >= 4:
                            pe.wait_ge(h_sdve, v0 + (i - 4) + 1)
                        for c in range(NCH):
                            pe.matmul(pw[:, 0:512],
                                      wab(WA_WOUT, 0, 128, 8192).rearrange(
                                          "p (c w) -> p c w", w=1024)[:, c, d * 128:(d + 1) * 128],
                                      lnch(c, t * TW, TW), start=(c == 0),
                                      stop=(c == NCH - 1))
                        spe.inc(pe)

            @blk.scalar
            def _(act):
                for t in range(NT):
                    for d in range(NCH):
                        i = t * NCH + d
                        pw = [psA, psB, psC, psD][i % 4]
                        act.wait_ge(h_spe, p0 + i + 1)
                        if i >= 2:
                            act.wait_ge(h_sdve, v0 + (i - 2) + 1)
                        act.activation(scf(BW_TSB2[i % 2], 0, 128, 512), pw[:, 0:512],
                                       AF.Identity, bias=bmu_sb[:, 8 + d: 8 + d + 1])
                        sact.inc(act)

            @blk.vector
            def _(v):
                for t in range(NT):
                    for d in range(NCH):
                        i = t * NCH + d
                        v.wait_ge(h_sact, a0 + i + 1)
                        v.tensor_tensor(xch(d, t * TW, TW), xch(d, t * TW, TW),
                                        scf(BW_TSB2[i % 2], 0, 128, 512), ALU.add)
                        sdve.inc(v)

        assert spe.v == p0 + 32 and sact.v == a0 + 32 and sdve.v == v0 + 32



_CACHE = {}


def _host_prep(inputs):
    x = np.ascontiguousarray(np.asarray(inputs["L0_attn_a_x"]).transpose(0, 2, 1))
    wqkv = np.ascontiguousarray(np.asarray(inputs["L0_attn_a_W_qkv"], np.float32))
    rf = np.asarray(inputs["L0_attn_a_RF"], np.float32)
    rfp = np.zeros((8, 128, 128), np.float32)
    for g in range(8):
        rfp[g, 0:64, 0:64] = rf[2 * g]
        rfp[g, 64:128, 64:128] = rf[2 * g + 1]
    wouta = np.ascontiguousarray(np.asarray(inputs["L0_attn_a_W_out"], np.float32))
    b_out_a = np.asarray(inputs["L0_attn_a_b_out"], np.float32)
    wqa = np.ascontiguousarray(np.asarray(inputs["L0_attn_b_Wqa"], np.float32))
    wqb = np.ascontiguousarray(np.asarray(inputs["L0_attn_b_Wqb"], np.float32))
    wkva = np.ascontiguousarray(np.asarray(inputs["L0_attn_b_Wkva"], np.float32))
    wkvb = np.asarray(inputs["L0_attn_b_Wkvb"], np.float32).reshape(32, 4, 64, 2)
    wkb = np.zeros((32, 512), np.float32)
    for g in range(4):
        wkb[:, g * 128: g * 128 + 64] = wkvb[:, g, :, 0]
        wkb[:, g * 128 + 64: g * 128 + 128] = wkvb[:, g, :, 0]
    wvb = np.ascontiguousarray(wkvb[..., 1].reshape(32, 256))
    woutb = np.ascontiguousarray(np.asarray(inputs["L0_attn_b_Wout"], np.float32))
    b_out_b = np.asarray(inputs["L0_attn_b_bout"], np.float32)
    wrt = np.ascontiguousarray(np.asarray(inputs["L0_ffn_W_r"], np.float32).T)
    wkt = np.ascontiguousarray(np.asarray(inputs["L0_ffn_W_k"], np.float32).T)
    wvt = np.ascontiguousarray(np.asarray(inputs["L0_ffn_W_v"], np.float32).T)
    mu_r = np.asarray(inputs["L0_ffn_mu_r"], np.float32)
    mu_k = np.asarray(inputs["L0_ffn_mu_k"], np.float32)
    bmu = np.zeros((128, 32), np.float32)
    bmu[:, 0:8] = b_out_a.reshape(8, 128).T
    bmu[:, 8:16] = b_out_b.reshape(8, 128).T
    bmu[:, 16:24] = mu_r.reshape(8, 128).T
    bmu[:, 24:32] = mu_k.reshape(8, 128).T

    D = 64
    inv = (1.0 / (10000.0 ** (np.arange(0, D, 2, dtype=np.float32) / np.float32(D)))).astype(np.float32)
    tpos = np.arange(T, dtype=np.float32)
    fr = np.outer(tpos, inv).astype(np.float32)
    cos64 = np.cos(fr).astype(np.float32).T
    sin64 = np.sin(fr).astype(np.float32).T
    cosm = np.repeat(cos64, 2, axis=0)
    sinm = np.repeat(sin64, 2, axis=0)
    cosm = np.tile(cosm, (2, 1))
    sinm = np.tile(sinm, (2, 1))
    s64 = np.zeros((64, 64), np.float32)
    for i2 in range(32):
        s64[2 * i2, 2 * i2 + 1] = 1.0
        s64[2 * i2 + 1, 2 * i2] = -1.0
    s128t = np.zeros((128, 128), np.float32)
    s128t[0:64, 0:64] = s64
    s128t[64:128, 64:128] = s64
    maskd = (np.arange(128)[:, None] <= np.arange(128)[None, :]).astype(np.float32)
    sel2 = np.zeros((2, 128), np.float32)
    sel2[0, 0:64] = 1.0
    sel2[1, 64:128] = 1.0

    com = dict(wqkv=wqkv, rf=rfp, wouta=wouta, wqa=wqa, wqb=wqb, wkva=wkva,
               wkb=wkb, wvb=wvb, woutb=woutb, wrt=wrt, wkt=wkt, wvt=wvt,
               bmu=bmu, cosm=cosm, sinm=sinm, s128t=s128t, maskd=maskd,
               sel2=sel2)
    return x, com


def get_nc(tap=None):
    key = ("nc", tap)
    if key not in _CACHE:
        _CACHE[key] = MK(tap=tap).build()
    return _CACHE[key]


def kernel(**inputs):
    x, com = _host_prep(inputs)
    nc = get_nc()
    in_maps = []
    for k in range(NCORES):
        m = {"x": np.ascontiguousarray(x[k * NSEQ:(k + 1) * NSEQ])}
        m.update(com)
        in_maps.append(m)
    res = run_bass_kernel_spmd(nc, in_maps, core_ids=list(range(NCORES)))
    y = np.stack([np.asarray(r["y"]) for r in res.results])
    y = y.reshape(B, C, T).transpose(0, 2, 1)
    return np.ascontiguousarray(y.astype(np.float32))


# version 14 (final)
# speedup vs baseline: 1.1997x; 1.0165x over previous
import numpy as np
import concourse.bass as bass
import concourse.mybir as mybir
from concourse.bass_utils import run_bass_kernel_spmd

F32 = mybir.dt.float32
BF16 = mybir.dt.bfloat16
AF = mybir.ActivationFunctionType
ALU = mybir.AluOpType

NCORES = 8
B, T, C = 32, 2048, 1024
NSEQ = B // NCORES
NCH = C // 128
NT = 4
TW = 512
NG = 8
NPAIR = 8
EPS = 1e-5
LN_DVE_T = 25
STAB = 1e-6

LN_XB = [c * 1024 for c in range(8)]
LN_SQ = [8192 + c * 1024 for c in range(8)]
LN_ROW0 = 16384
LN_ROW1 = 18432
LN_TS = [20480, 22528]
AA_QSC = [0, 1024, 2048, 3072]
AA_KSC = [4096, 5120, 6144, 7168]
AA_PHIK = [8192, 8704]
AA_VTK = [9216, 9728]
AA_KVBD = 24576
AA_KSUM = 26624
AB_ATT = 0
AB_NRM = 8192
AB_TSB = [12288, 14336]
AB_TSB2 = [16384, 18432]
FF_XR = 0
FF_XK = 8192
FF_KACT = 16384
FF_DD = [24576, 25600]
FF_RSB = [26624, 28672]
FF_TMP = [30720, 24576]
BP_COS = 0
BP_SIN = 4096
BP_STAGE = 8192
BP_LQ = 8192
BP_LKV = 12288
BP_KREP = 16384
BS_VTOK = 0
BS_PA0 = 8320
BS_PB0 = 11392
BS_OHSBA = 14464
BW_TSB2 = [0, 2048]
ST_F32 = [0, 4096, 8192, 12288]
ST_BF = [16384, 18432, 20480, 22528]

WA_WQKV = [0, 6144]
WA_WOUT = 0
WF_K = [0, 2048]
WF_R = [4096, 6144]
WF_V = [8192, 10240]
WB_WQA = 0
WB_WQB = 1024
WB_WKB = 3072
WB_WVB = 4096
WB_WKVA = 4608
WB_MASKD = 5120
WB_QSBT = [5632, 6656]
WB_RT1 = 7680
WB_RT2 = 9728
WB_RECA = 11776
WB_RECB = 12800
WB_DROW = 13824


class EngP:

    def __init__(self, e):
        object.__setattr__(self, "e", e)
        object.__setattr__(self, "last", None)

    def __getattr__(self, name):
        f = getattr(object.__getattribute__(self, "e"), name)
        if not callable(f):
            return f

        def wrap(*a, **k):
            r = f(*a, **k)
            if hasattr(r, "then_inc"):
                object.__setattr__(self, "last", r)
            return r
        return wrap


class BW:

    def __init__(self, blk):
        self._blk = blk

    def __getattr__(self, name):
        dec = getattr(self._blk, name)

        def mydec(f):
            return dec(lambda e: f(EngP(e)))
        return mydec


class Ctr:
    def __init__(self, h, step=1):
        self.h = h
        self.v = 0
        self.step = step

    def inc(self, x, k=None):
        k = self.step if k is None else k
        if isinstance(x, EngP):
            last = object.__getattribute__(x, "last")
            if last is not None:
                last.then_inc(self.h, k)
            else:
                object.__getattribute__(x, "e").sem_inc(self.h, k)
        elif hasattr(x, "then_inc"):
            x.then_inc(self.h, k)
        else:
            x.sem_inc(self.h, k)
        self.v += k
        return self.v

    def pad(self, engp, k=None):
        k = self.step if k is None else k
        e = object.__getattribute__(engp, "e") if isinstance(engp, EngP) else engp
        e.sem_inc(self.h, k)
        self.v += k
        return self.v


class MK:
    def __init__(self, tap=None):
        self.tap = tap
        nc = bass.Bass("TRN2", target_bir_lowering=False,
                       detect_race_conditions=False)
        self.nc = nc
        self.X = nc.dram_tensor("x", [NSEQ, C, T], F32, kind="ExternalInput").ap()
        self.Y = nc.dram_tensor("y", [NSEQ, C, T], F32, kind="ExternalOutput").ap()
        self.WQKV = nc.dram_tensor("wqkv", [C, 3 * C], F32, kind="ExternalInput").ap()
        self.RF = nc.dram_tensor("rf", [NG, 128, 128], F32, kind="ExternalInput").ap()
        self.WOUTA = nc.dram_tensor("wouta", [C, C], F32, kind="ExternalInput").ap()
        self.WQA = nc.dram_tensor("wqa", [C, 64], F32, kind="ExternalInput").ap()
        self.WQB = nc.dram_tensor("wqb", [64, C], F32, kind="ExternalInput").ap()
        self.WKVA = nc.dram_tensor("wkva", [C, 32], F32, kind="ExternalInput").ap()
        self.WKB = nc.dram_tensor("wkb", [32, 512], F32, kind="ExternalInput").ap()
        self.WVB = nc.dram_tensor("wvb", [32, 256], F32, kind="ExternalInput").ap()
        self.WOUTB = nc.dram_tensor("woutb", [C, C], F32, kind="ExternalInput").ap()
        self.WRT = nc.dram_tensor("wrt", [C, C], F32, kind="ExternalInput").ap()
        self.WKT = nc.dram_tensor("wkt", [C, C], F32, kind="ExternalInput").ap()
        self.WVT = nc.dram_tensor("wvt", [C, C], F32, kind="ExternalInput").ap()
        self.BMU = nc.dram_tensor("bmu", [128, 32], F32, kind="ExternalInput").ap()
        self.COSM = nc.dram_tensor("cosm", [128, T], F32, kind="ExternalInput").ap()
        self.SINM = nc.dram_tensor("sinm", [128, T], F32, kind="ExternalInput").ap()
        self.S128T = nc.dram_tensor("s128t", [128, 128], F32, kind="ExternalInput").ap()
        self.MASKD = nc.dram_tensor("maskd", [128, 128], F32, kind="ExternalInput").ap()
        self.SEL2 = nc.dram_tensor("sel2", [2, 128], F32, kind="ExternalInput").ap()
        self.WQKV_S = nc.dram_tensor("wqkv_s", [NG, NCH, 128, 384], BF16).ap()
        self.WOUTA_S = nc.dram_tensor("wouta_s", [C, C], BF16).ap()
        self.WOUTB_S = nc.dram_tensor("woutb_s", [C, C], BF16).ap()
        self.WRT_S = nc.dram_tensor("wrt_s", [C, C], BF16).ap()
        self.WKT_S = nc.dram_tensor("wkt_s", [C, C], BF16).ap()
        self.WVT_S = nc.dram_tensor("wvt_s", [C, C], BF16).ap()
        self.TAPX = None
        if tap is not None:
            self.TAPX = nc.dram_tensor("tapx", [C, T], F32, kind="ExternalOutput").ap()

    def scf(self, off, p0, np_, w):
        assert off % 4 == 0 and off + 0 <= 32768
        return self.scr[p0:p0 + np_, off // 4: off // 4 + w]

    def scb(self, off, p0, np_, w):
        assert off % 2 == 0
        return self.scr.ap().bitcast(BF16)[p0:p0 + np_, off // 2: off // 2 + w]

    def wab(self, off, p0, np_, w):
        assert off % 2 == 0
        return self.warena[p0:p0 + np_, off // 2: off // 2 + w]

    def waf(self, off, p0, np_, w):
        assert off % 4 == 0
        return self.warena.ap().bitcast(F32)[p0:p0 + np_, off // 4: off // 4 + w]

    def xch(self, c, t0=0, w=T):
        return self.x_sb[:, c * T + t0: c * T + t0 + w]

    def lnch(self, c, t0=0, w=T, shift=0, p0=0, np_=128):
        base = c * (T + 1) + 1 - shift + t0
        return self.lnx[p0:p0 + np_, base: base + w]

    def bgch(self, c, t0=0, w=T, p0=0, np_=128):
        return self.biga[p0:p0 + np_, c * T + t0: c * T + t0 + w]

    def build(self):
        nc = self.nc
        from contextlib import ExitStack
        with ExitStack() as st:
            x_sb = st.enter_context(nc.sbuf_tensor("x_sb", [128, NCH * T], F32))
            lnx = st.enter_context(nc.sbuf_tensor("lnx", [128, NCH * (T + 1)], BF16))
            biga = st.enter_context(nc.sbuf_tensor("biga", [128, NCH * T], BF16))
            warena = st.enter_context(nc.sbuf_tensor("warena", [128, 8192], BF16))
            scr = st.enter_context(nc.sbuf_tensor("scr", [128, 8192], F32))
            rf_bf = st.enter_context(nc.sbuf_tensor("rf_bf", [128, NG * 128], BF16))
            s128t_bf = st.enter_context(nc.sbuf_tensor("s128t_bf", [128, 128], BF16))
            sel2_f = st.enter_context(nc.sbuf_tensor("sel2_f", [2, 128], F32))
            ones1f = st.enter_context(nc.sbuf_tensor("ones1f", [1, 128], F32))
            ones1b = st.enter_context(nc.sbuf_tensor("ones1b", [1, 64], BF16))
            onesc_bf = st.enter_context(nc.sbuf_tensor("onesc_bf", [128, 8], BF16))
            bmu_sb = st.enter_context(nc.sbuf_tensor("bmu_sb", [128, 32], F32))
            cst = st.enter_context(nc.sbuf_tensor("cst", [128, 2], F32))
            maskd_bf = st.enter_context(nc.sbuf_tensor("maskd_bf", [128, 128], BF16))
            recs_bf = st.enter_context(nc.sbuf_tensor("recs_bf", [1, 1024], BF16))
            drow_f = st.enter_context(nc.sbuf_tensor("drow_f", [1, 512], F32))
            self.maskd_bf, self.recs_bf, self.drow_f = maskd_bf, recs_bf, drow_f
            ohsb_t = st.enter_context(nc.sbuf_tensor("ohsb_t", [64, 512], BF16))
            self.ohsb_t = ohsb_t
            nc.const_aps.register(F32, EPS, cst[:, 0:1])
            nc.const_aps.register(F32, STAB, cst[:, 1:2])
            self.cst = cst
            psA = st.enter_context(nc.psum_tensor("psA", [128, 512], F32))
            psB = st.enter_context(nc.psum_tensor("psB", [128, 512], F32))
            psC = st.enter_context(nc.psum_tensor("psC", [128, 512], F32))
            psD = st.enter_context(nc.psum_tensor("psD", [128, 512], F32))
            psW = st.enter_context(nc.psum_tensor("psW", [128, 2048], F32))
            h_dmx = st.enter_context(nc.semaphore("dmx"))
            h_dmk0 = st.enter_context(nc.semaphore("dmk0"))
            h_dmk1 = st.enter_context(nc.semaphore("dmk1"))
            h_dmw2 = st.enter_context(nc.semaphore("dmw2"))
            h_dmf0 = st.enter_context(nc.semaphore("dmf0"))
            h_dmf1 = st.enter_context(nc.semaphore("dmf1"))
            h_dmf2 = st.enter_context(nc.semaphore("dmf2"))
            h_dmf3 = st.enter_context(nc.semaphore("dmf3"))
            h_dms0 = st.enter_context(nc.semaphore("dms0"))
            h_dms1 = st.enter_context(nc.semaphore("dms1"))
            h_dmso0 = st.enter_context(nc.semaphore("dmso0"))
            h_dmso1 = st.enter_context(nc.semaphore("dmso1"))
            h_dmc = st.enter_context(nc.semaphore("dmc"))
            h_dmo = st.enter_context(nc.semaphore("dmo"))
            h_spe = st.enter_context(nc.semaphore("spe"))
            h_sact = st.enter_context(nc.semaphore("sact"))
            h_sdve = st.enter_context(nc.semaphore("sdve"))
            h_spl = st.enter_context(nc.semaphore("spl"))
            h_spool = st.enter_context(nc.semaphore("spool"))
            self.x_sb, self.lnx, self.biga = x_sb, lnx, biga
            self.warena, self.scr = warena, scr
            self.rf_bf, self.s128t_bf, self.sel2_f = rf_bf, s128t_bf, sel2_f
            self.ones1f, self.ones1b, self.onesc_bf = ones1f, ones1b, onesc_bf
            self.bmu_sb = bmu_sb
            self.psA, self.psB, self.psC, self.psD, self.psW = psA, psB, psC, psD, psW
            self.h_dmx, self.h_dmw2, self.h_dmc, self.h_dmo = h_dmx, h_dmw2, h_dmc, h_dmo
            self.h_dmk = [h_dmk0, h_dmk1]
            self.h_dmf = [h_dmf0, h_dmf1, h_dmf2, h_dmf3]
            self.h_dms = [h_dms0, h_dms1]
            self.h_dmso = [h_dmso0, h_dmso1]
            self.h_spe, self.h_sact, self.h_sdve, self.h_spl = h_spe, h_sact, h_sdve, h_spl
            self.dmx, self.dmw2, self.dmc, self.dmo = (Ctr(h_dmx, 16), Ctr(h_dmw2, 16),
                                                       Ctr(h_dmc, 16), Ctr(h_dmo, 16))
            self.dmk = [Ctr(h_dmk0, 16), Ctr(h_dmk1, 16)]
            self.dmf = [Ctr(h_dmf0, 16), Ctr(h_dmf1, 16), Ctr(h_dmf2, 16), Ctr(h_dmf3, 16)]
            self.dms = [Ctr(h_dms0, 16), Ctr(h_dms1, 16)]
            self.dmso = [Ctr(h_dmso0, 16), Ctr(h_dmso1, 16)]
            self.spe, self.sact, self.sdve, self.spl = (Ctr(h_spe), Ctr(h_sact),
                                                        Ctr(h_sdve), Ctr(h_spl))
            self.h_spool = h_spool
            self.spool = Ctr(h_spool)
            self.emit_staging()
            self.emit_consts()
            for s in range(NSEQ):
                self.emit_seq(s)
            with nc.Block("fin") as _b:
                blk = BW(_b)
                @blk.gpsimd
                def _(g):
                    g.wait_ge(self.h_dmo, self.dmo.v)

                @blk.sync
                def _(sync):
                    sync.wait_ge(self.h_dmo, self.dmo.v)
        nc.finalize()
        return nc

    def emit_staging(self):
        nc = self.nc
        jobs = []
        for c in range(NCH):
            for sub in range(3):
                jobs.append((self.WQKV[c * 128:(c + 1) * 128, sub * 1024:(sub + 1) * 1024],
                             self.WQKV_S[:, c, :, sub * 128:(sub + 1) * 128], "qkv"))
        for W, WS in ((self.WOUTA, self.WOUTA_S), (self.WOUTB, self.WOUTB_S),
                      (self.WRT, self.WRT_S), (self.WKT, self.WKT_S),
                      (self.WVT, self.WVT_S)):
            for c in range(NCH):
                jobs.append((W[c * 128:(c + 1) * 128, :],
                             WS[c * 128:(c + 1) * 128, :], "row"))
        v0 = self.sdve.v
        h_sdve, sdve = self.h_sdve, self.sdve
        ld_sem, ld_h = self.dmf, self.h_dmf
        st_sem = [self.dms[0], self.dms[1], self.dmso[0], self.dmso[1]]
        st_h = [self.h_dms[0], self.h_dms[1], self.h_dmso[0], self.h_dmso[1]]
        sl0 = [c.v for c in ld_sem]
        so0 = [c.v for c in st_sem]
        scf, scb = self.scf, self.scb
        with nc.Block("staging") as _b:
            blk = BW(_b)
            @blk.sync
            def _(sync):
                for i, (src, dst, kind) in enumerate(jobs):
                    if i >= 4:
                        sync.wait_ge(h_sdve, v0 + (i - 4) + 1)
                    ld_sem[i % 4].inc(sync.dma_start(scf(ST_F32[i % 4], 0, 128, 1024), src))

            @blk.vector
            def _(v):
                for i, (src, dst, kind) in enumerate(jobs):
                    v.wait_ge(ld_h[i % 4], sl0[i % 4] + 16 * (i // 4 + 1))
                    if i >= 4:
                        v.wait_ge(st_h[i % 4], so0[i % 4] + 16 * ((i - 4) // 4 + 1))
                    v.tensor_copy(scb(ST_BF[i % 4], 0, 128, 1024),
                                  scf(ST_F32[i % 4], 0, 128, 1024))
                    sdve.inc(v)

            @blk.gpsimd
            def _(g):
                for i, (src, dst, kind) in enumerate(jobs):
                    g.wait_ge(h_sdve, v0 + i + 1)
                    sb = scb(ST_BF[i % 4], 0, 128, 1024)
                    if kind == "qkv":
                        st_sem[i % 4].inc(g.dma_start(dst.transpose([1, 0, 2]),
                                                      sb.rearrange("p (g w) -> p g w", w=128)))
                    else:
                        st_sem[i % 4].inc(g.dma_start(dst, sb))

    def emit_consts(self):
        nc = self.nc
        h_dmc, dmc, sdve = self.h_dmc, self.dmc, self.sdve
        scf, scb = self.scf, self.scb
        with nc.Block("consts") as _b:
            blk = BW(_b)
            @blk.sync
            def _(sync):
                sync.wait_ge(self.h_sdve, self.sdve.v)
                dmc.inc(sync.dma_start(
                    scf(0, 0, 128, 1024).rearrange("p (g w) -> p g w", w=128),
                    self.RF.transpose([1, 0, 2])))
                dmc.inc(sync.dma_start(self.sel2_f[:, :], self.SEL2))
                dmc.inc(sync.dma_start(scf(4096, 0, 128, 128), self.S128T))
                dmc.inc(sync.dma_start(self.bmu_sb[:, :], self.BMU))
                dmc.inc(sync.dma_start(scf(5120, 0, 128, 128), self.MASKD))

            @blk.vector
            def _(v):
                v.wait_ge(h_dmc, dmc.v)
                v.wait_ge(self.h_dmso[0], self.dmso[0].v)
                v.wait_ge(self.h_dmso[1], self.dmso[1].v)
                v.memset(self.cst[:, 0:1], EPS)
                v.memset(self.cst[:, 1:2], STAB)
                v.tensor_copy(self.rf_bf[:, :], scf(0, 0, 128, 1024))
                v.tensor_copy(self.s128t_bf[:, :], scf(4096, 0, 128, 128))
                v.tensor_copy(self.maskd_bf[:, :], scf(5120, 0, 128, 128))
                v.memset(self.ones1f[:, :], 1.0)
                v.memset(self.ones1b[:, :], 1.0)
                v.memset(self.onesc_bf[:, :], 1.0)
                v.memset(self.lnx.ap().rearrange("p (c t) -> p c t", t=T + 1)[:, :, 0:1], 0.0)
                sdve.inc(v)

    def emit_seq(self, s):
        nc = self.nc
        dmx, dmo = self.dmx, self.dmo
        h_dmo = self.h_dmo
        with nc.Block(f"s{s}_loadx") as _b:
            blk = BW(_b)
            @blk.sync
            def _(sync):
                if s > 0:
                    sync.wait_ge(h_dmo, dmo.v)
                for c in range(NCH):
                    dmx.inc(sync.dma_start(self.xch(c),
                                           self.X[s, c * 128:(c + 1) * 128, :]))
                self.dmk[0].inc(sync.dma_start(
                    self.wab(WA_WQKV[0], 0, 128, 3072).rearrange("p (c w) -> p c w", w=384),
                    self.WQKV_S[0].transpose([1, 0, 2])))
        self._xload = dmx.v
        self._aa_g0 = self.dmk[0].v

        if self.tap == 0:
            self.emit_tap(s)
            return
        self.emit_ln(s, 1)
        if self.tap == 5:
            self.emit_tap_lnx(s)
            return
        self.emit_attn_a(s)
        if self.tap == 1:
            self.emit_tap(s)
            return
        self.emit_ln(s, 2)
        self.emit_ffn(s, 1)
        if self.tap == 2:
            self.emit_tap(s)
            return
        self.emit_ln(s, 3)
        self.emit_attn_b(s)
        if self.tap == 3:
            self.emit_tap(s)
            return
        self.emit_ln(s, 4)
        self.emit_ffn(s, 2, store=True)
        if self.tap == 4:
            self.emit_tap(s)

    def emit_tap(self, s):
        if s != 0:
            return
        nc, dmo, h_dmo = self.nc, self.dmo, self.h_dmo
        h_dmx = self.h_dmx
        xload = self._xload
        with nc.Block("tap") as _b:
            blk = BW(_b)
            @blk.gpsimd
            def _(g):
                g.wait_ge(h_dmx, xload)
                for c in range(NCH):
                    dmo.inc(g.dma_start(self.TAPX[c * 128:(c + 1) * 128, :],
                                        self.xch(c)))
                g.wait_ge(h_dmo, dmo.v)

    def emit_tap_lnx(self, s):
        if s != 0:
            return
        nc, dmo, h_dmo, sdve, h_sdve = (self.nc, self.dmo, self.h_dmo,
                                        self.sdve, self.h_sdve)
        v0 = sdve.v
        o0 = self.dmo.v
        with nc.Block("taplnx") as _b:
            blk = BW(_b)
            @blk.vector
            def _(v):
                for c in range(NCH):
                    if c >= 2:
                        v.wait_ge(h_dmo, o0 + 16 * (c - 1))
                    v.tensor_copy(self.scf([0, 8192][c % 2], 0, 128, 2048),
                                  self.lnch(c))
                    sdve.inc(v)

            @blk.gpsimd
            def _(g):
                for c in range(NCH):
                    g.wait_ge(h_sdve, v0 + c + 1)
                    g.wait_ge(h_dmo, o0 + 16 * c)
                    dmo.inc(g.dma_start(self.TAPX[c * 128:(c + 1) * 128, :],
                                        self.scf([0, 8192][c % 2], 0, 128, 2048)))
                g.wait_ge(h_dmo, dmo.v)

    def emit_ln(self, s, which):
        nc = self.nc
        h_spool, spool = self.h_spool, self.spool
        LN_MEV, LN_REV = 24576, 26624
        h_spe, h_sact, h_sdve = self.h_spe, self.h_sact, self.h_sdve
        spe, sact, sdve = self.spe, self.sact, self.sdve
        scf, scb, xch, lnch = self.scf, self.scb, self.xch, self.lnch
        psA, psB, psC, psD, psW = self.psA, self.psB, self.psC, self.psD, self.psW
        onesc, ones1f = self.onesc_bf, self.ones1f
        h_dmx = self.h_dmx
        xload = self._xload
        a0, v0, p0 = sact.v, sdve.v, spe.v
        g0 = spool.v
        first = (which == 1)

        def bmean(t):
            return [psC[:, 0:512], psW[:, 0:512]][t % 2]

        def brstd(t):
            return [psD[:, 0:512], psW[:, 512:1024]][t % 2]

        with nc.Block(f"s{s}_ln{which}") as _b:
            blk = BW(_b)
            @blk.scalar
            def _(act):
                for t in range(NT):
                    for c in range(NCH):
                        bi = c
                        if first and t == 0 and c == 0:
                            act.wait_ge(h_dmx, xload)
                        act.copy(scb(LN_XB[bi], 0, 128, 512), xch(c, t * TW, TW))
                        sact.inc(act)
                    act.wait_ge(h_spe, p0 + t * 18 + 16)
                    act.activation(scf(LN_ROW0, 0, 1, 512), psA[0:1, 0:512],
                                   AF.Copy, scale=1.0 / 1024.0)
                    sact.inc(act)
                    act.wait_ge(h_sdve, v0 + t * LN_DVE_T + 9)
                    act.activation(scf(LN_ROW1, 0, 1, 512), scf(LN_ROW1, 0, 1, 512),
                                   AF.Ln, bias=EPS)
                    act.activation(scf(LN_ROW1, 0, 1, 512), scf(LN_ROW1, 0, 1, 512),
                                   AF.Exp, scale=-0.5)
                    sact.inc(act, 2)

            @blk.vector
            def _(v):
                for t in range(NT):
                    for c in range(NCH):
                        bi = c
                        v.wait_ge(h_sact, a0 + t * 11 + c + 1)
                        v.tensor_mul(scb(LN_SQ[bi], 0, 128, 512),
                                     scb(LN_XB[bi], 0, 128, 512),
                                     scb(LN_XB[bi], 0, 128, 512))
                        sdve.inc(v)
                    v.wait_ge(h_sact, a0 + t * 11 + 9)
                    v.scalar_tensor_tensor(scf(LN_ROW1, 0, 1, 512),
                                           scf(LN_ROW0, 0, 1, 512), 1.0,
                                           scf(LN_ROW0, 0, 1, 512), ALU.mult, ALU.mult)
                    v.wait_ge(h_spe, p0 + t * 18 + 16)
                    v.scalar_tensor_tensor(scf(LN_ROW1, 0, 1, 512), psB[0:1, 0:512],
                                           1.0 / 1024.0, scf(LN_ROW1, 0, 1, 512),
                                           ALU.mult, ALU.subtract)
                    sdve.inc(v)
                    for c in range(NCH):
                        ts = scf(LN_TS[c % 2], 0, 128, 512)
                        if c == 0:
                            v.wait_ge(h_spe, p0 + t * 18 + 17)
                        v.tensor_tensor(ts, xch(c, t * TW, TW), bmean(t),
                                        ALU.subtract)
                        if c == 0:
                            v.wait_ge(h_spe, p0 + t * 18 + 18)
                        v.tensor_tensor(lnch(c, t * TW, TW), ts, brstd(t),
                                        ALU.mult)
                        sdve.inc(v, 2)

            @blk.tensor
            def _(pe):
                for t in range(NT):
                    for c in range(NCH):
                        bi = c
                        pe.wait_ge(h_sact, a0 + t * 11 + c + 1)
                        pe.wait_ge(h_sdve, v0 + t * LN_DVE_T + c + 1)
                        if c == 0 and t > 0:
                            pe.wait_ge(h_sact, a0 + (t - 1) * 11 + 9)
                            pe.wait_ge(h_sdve, v0 + (t - 1) * 19 + 9)
                        pe.matmul(psA[0:1, 0:512], onesc[:, 0:1],
                                  scb(LN_XB[bi], 0, 128, 512),
                                  start=(c == 0), stop=(c == NCH - 1))
                        pe.matmul(psB[0:1, 0:512], onesc[:, 0:1],
                                  scb(LN_SQ[bi], 0, 128, 512),
                                  start=(c == 0), stop=(c == NCH - 1))
                        spe.inc(pe, 2)
                    pe.wait_ge(h_sact, a0 + t * 11 + 9)
                    if t >= 2:
                        pe.wait_ge(h_sdve, v0 + (t - 2) * LN_DVE_T + LN_DVE_T)
                        pass
                    pe.matmul(bmean(t), ones1f[:, :], scf(LN_ROW0, 0, 1, 512),
                              start=True, stop=True)
                    spe.inc(pe)
                    pe.wait_ge(h_sact, a0 + t * 11 + 11)
                    pe.matmul(brstd(t), ones1f[:, :], scf(LN_ROW1, 0, 1, 512),
                              start=True, stop=True)
                    spe.inc(pe)

        assert sact.v == a0 + NT * 11
        assert sdve.v == v0 + NT * LN_DVE_T
        assert spe.v == p0 + NT * 18
        
    def emit_attn_a(self, s):
        nc = self.nc
        h_spe, h_sact, h_sdve = self.h_spe, self.h_sact, self.h_sdve
        h_dmk, h_dmw2 = self.h_dmk, self.h_dmw2
        dmk, dmw2 = self.dmk, self.dmw2
        spe, sact, sdve = self.spe, self.sact, self.sdve
        scf, scb, wab, xch, lnch, bgch = (self.scf, self.scb, self.wab, self.xch,
                                          self.lnch, self.bgch)
        psA, psB, psC, psD, psW = self.psA, self.psB, self.psC, self.psD, self.psW
        rf_bf, sel2_f, bmu_sb = self.rf_bf, self.sel2_f, self.bmu_sb
        loads = {0: self._aa_g0}
        wout_cnt = [None]

        for gi in range(NG):
            wb = WA_WQKV[gi % 2]

            def wq(c, seg, wb=wb):
                return wab(wb, 0, 128, 3072).rearrange(
                    "p (c w) -> p c w", w=384)[:, c, seg * 128:(seg + 1) * 128]

            a0, v0, p0 = sact.v, sdve.v, spe.v
            dvo = 1 if gi == 0 else 0
            load_cnt = loads[gi]
            with nc.Block(f"s{s}_aa_g{gi}") as _b:
                blk = BW(_b)
                @blk.sync
                def _(sync):
                    if gi + 1 < NG:
                        dmk[(gi + 1) % 2].inc(sync.dma_start(
                            wab(WA_WQKV[(gi + 1) % 2], 0, 128, 3072)
                            .rearrange("p (c w) -> p c w", w=384),
                            self.WQKV_S[gi + 1].transpose([1, 0, 2])))
                        loads[gi + 1] = dmk[(gi + 1) % 2].v
                    else:
                        sync.wait_ge(h_spe, p0 + 60)
                        dmw2.inc(sync.dma_start(
                            wab(WA_WOUT, 0, 128, 8192).rearrange("p (c w) -> p c w", w=1024),
                            self.WOUTA_S.rearrange("(c p) w -> c p w", p=128)
                            .transpose([1, 0, 2])))
                        wout_cnt[0] = dmw2.v

                @blk.tensor
                def _(pe):
                    pe.wait_ge(h_dmk[gi % 2], load_cnt)
                    for t in range(NT):
                        pq = [psA, psC][t % 2]
                        pk = [psB, psD][t % 2]
                        if t >= 2:
                            pe.wait_ge(h_sact, a0 + (t - 2) * 2 + 2)
                        for c in range(NCH):
                            pe.matmul(pq[:, 0:512], wq(c, 0), lnch(c, t * TW, TW),
                                      start=(c == 0), stop=(c == NCH - 1))
                        spe.inc(pe)
                        for c in range(NCH):
                            pe.matmul(pk[:, 0:512], wq(c, 1), lnch(c, t * TW, TW),
                                      start=(c == 0), stop=(c == NCH - 1))
                        spe.inc(pe)
                    for t in range(NT):
                        pq = [psA, psC][t % 2]
                        wv = (t + 2) * 2 + 1 if t + 2 < NT else t * 2 + 1
                        pe.wait_ge(h_sact, a0 + wv)
                        if t >= 2:
                            pe.wait_ge(h_sdve, v0 + dvo + (t - 2) + 1)
                        pe.matmul(pq[:, 0:512], rf_bf[:, gi * 128:(gi + 1) * 128],
                                  scb(AA_QSC[t], 0, 128, 512), start=True, stop=True)
                        spe.inc(pe)
                    def kv_mm(j):
                        pe.wait_ge(h_sact, a0 + 8 + j * 2 + 2)
                        if gi == 0 and j < 2:
                            pe.wait_ge(h_sdve, v0 + 1)
                        pe.matmul(psW[:, 0:130], scb(AA_PHIK[j % 2], 0, 128, 128),
                                  scb(AA_VTK[j % 2], 0, 128, 130),
                                  start=(j == 0), stop=(j == 15))
                        spe.inc(pe)
                    for j in range(16):
                        t = j // 4
                        jj = j % 4
                        pe.wait_ge(h_sact, a0 + t * 2 + 2)
                        pv = [psA, psB][j % 2][:, 0:128]
                        pf = [psC, psD][j % 2][:, 0:128]
                        if j >= 2:
                            pe.wait_ge(h_sact, a0 + 8 + (j - 2) * 2 + 2)
                        elif j == 0:
                            pe.wait_ge(h_sdve, v0 + dvo + 3)
                        elif j == 1:
                            pe.wait_ge(h_sact, a0 + 6)
                        for c in range(NCH):
                            pe.matmul(pv, lnch(c, j * 128, 128), wq(c, 2),
                                      start=(c == 0), stop=(c == NCH - 1))
                        spe.inc(pe)
                        if j == 0:
                            pe.wait_ge(h_sdve, v0 + dvo + 4)
                        pe.matmul(pf, scb(AA_KSC[t], 0, 128, 512)[:, jj * 128:(jj + 1) * 128],
                                  rf_bf[:, gi * 128:(gi + 1) * 128], start=True, stop=True)
                        spe.inc(pe)
                        if j >= 1:
                            kv_mm(j - 1)
                    kv_mm(15)

                @blk.scalar
                def _(act):
                    for t in range(NT):
                        pq = [psA, psC][t % 2]
                        pk = [psB, psD][t % 2]
                        act.wait_ge(h_spe, p0 + t * 2 + 1)
                        act.copy(scb(AA_QSC[t], 0, 128, 512), pq[:, 0:512])
                        sact.inc(act)
                        act.wait_ge(h_spe, p0 + t * 2 + 2)
                        act.copy(scb(AA_KSC[t], 0, 128, 512), pk[:, 0:512])
                        sact.inc(act)
                    for j in range(16):
                        pv = [psA, psB][j % 2][:, 0:128]
                        pf = [psC, psD][j % 2][:, 0:128]
                        vj = 13 if j == 0 else 3 * j + 12
                        act.wait_ge(h_spe, p0 + vj)
                        act.copy(scb(AA_VTK[j % 2], 0, 128, 130)[:, 0:128], pv)
                        sact.inc(act)
                        act.wait_ge(h_spe, p0 + vj + 1)
                        act.activation(scb(AA_PHIK[j % 2], 0, 128, 128), pf, AF.Relu)
                        sact.inc(act)
                    act.wait_ge(h_spe, p0 + 12 + 48)
                    if gi == 0:
                        act.wait_ge(h_sdve, v0 + 1)
                    kvbd = scb(AA_KVBD, 0, 128, 1024)
                    ksum = scb(AA_KSUM, 0, 128, 16)
                    act.copy(kvbd[0:64, gi * 128: gi * 128 + 64], psW[0:64, 0:64])
                    act.copy(kvbd[64:128, gi * 128 + 64: gi * 128 + 128],
                             psW[64:128, 64:128])
                    act.copy(ksum[0:64, 2 * gi: 2 * gi + 1], psW[0:64, 128:129])
                    act.copy(ksum[64:128, 2 * gi + 1: 2 * gi + 2], psW[64:128, 128:129])
                    sact.inc(act, 4)

                @blk.vector
                def _(v):
                    if gi == 0:
                        v.memset(scb(AA_KVBD, 0, 128, 1024), 0.0)
                        v.memset(scb(AA_KSUM, 0, 128, 16), 0.0)
                        v.memset(scb(AA_VTK[0], 0, 128, 130)[:, 128:129], 1.0)
                        v.memset(scb(AA_VTK[1], 0, 128, 130)[:, 128:129], 1.0)
                        sdve.inc(v)
                    for t in range(NT):
                        pq = [psA, psC][t % 2]
                        v.wait_ge(h_spe, p0 + 8 + t + 1)
                        v.tensor_scalar_max(bgch(gi, t * TW, TW), pq[:, 0:512], 0.0)
                        sdve.inc(v)

            assert spe.v == p0 + 60
            assert sact.v == a0 + 44
            assert sdve.v == v0 + 4 + (1 if gi == 0 else 0)

        a0, v0, p0 = sact.v, sdve.v, spe.v
        with nc.Block(f"s{s}_aa_B") as _b:
            blk = BW(_b)
            def peV(k):
                t, gi = divmod(k, 8)
                return p0 + t * 32 + gi * 3

            def actV(k):
                t, gi = divmod(k, 8)
                return a0 + t * 32 + gi * 3

            @blk.tensor
            def _(pe):
                kvbd = scb(AA_KVBD, 0, 128, 1024)
                ksum = scb(AA_KSUM, 0, 128, 16)
                for t in range(NT):
                    for gi in range(NG):
                        k = t * 8 + gi
                        po = [psA, psC][gi % 2]
                        if k >= 2:
                            kk = k - 2
                            pe.wait_ge(h_sdve, v0 + (kk // 8) * 16 + (kk % 8) + 1)
                        pe.matmul(po[:, 0:512], kvbd[:, gi * 128:(gi + 1) * 128],
                                  bgch(gi, t * TW, TW), start=True, stop=True)
                        spe.inc(pe)
                        if k >= 1:
                            pe.wait_ge(h_sact, actV(k - 1) + 2)
                        pe.matmul(psB[0:2, 0:512], ksum[:, 2 * gi:2 * gi + 2],
                                  bgch(gi, t * TW, TW), start=True, stop=True)
                        spe.inc(pe)
                        pe.wait_ge(h_sact, actV(k) + 2)
                        pe.matmul(psD[:, 0:512], sel2_f[:, :], scf(AB_NRM, 0, 2, 512),
                                  start=True, stop=True)
                        spe.inc(pe)
                    pe.wait_ge(h_sdve, v0 + t * 16 + 8)
                    pe.wait_ge(h_dmw2, wout_cnt[0])
                    for d in range(NCH):
                        pw = psW[:, (d % 4) * 512:(d % 4) * 512 + 512]
                        k = t * 8 + d
                        if k >= 4:
                            kk = k - 4
                            pe.wait_ge(h_sdve, v0 + (kk // 8) * 16 + 8 + (kk % 8) + 1)
                        for c in range(NCH):
                            pe.matmul(pw, wab(WA_WOUT, 0, 128, 8192).rearrange(
                                "p (c w) -> p c w", w=1024)[:, c, d * 128:(d + 1) * 128],
                                scb(AB_ATT + c * 1024, 0, 128, 512),
                                start=(c == 0), stop=(c == NCH - 1))
                        spe.inc(pe)

            @blk.scalar
            def _(act):
                for t in range(NT):
                    for gi in range(NG):
                        k = t * 8 + gi
                        act.wait_ge(h_spe, peV(k) + 2)
                        act.activation(scf(AB_NRM, 0, 2, 512), psB[0:2, 0:512],
                                       AF.Ln, bias=STAB)
                        act.activation(scf(AB_NRM, 0, 2, 512), scf(AB_NRM, 0, 2, 512),
                                       AF.Exp, scale=-1.0)
                        sact.inc(act, 2)
                        act.wait_ge(h_spe, peV(k) + 3)
                        if k >= 2:
                            kk = k - 2
                            act.wait_ge(h_sdve, v0 + (kk // 8) * 16 + (kk % 8) + 1)
                        act.copy(scf(AB_TSB[gi % 2], 0, 128, 512), psD[:, 0:512])
                        sact.inc(act)
                    for d in range(NCH):
                        k = t * 8 + d
                        pw = psW[:, (d % 4) * 512:(d % 4) * 512 + 512]
                        act.wait_ge(h_spe, p0 + t * 32 + 24 + d + 1)
                        if k >= 2:
                            kk = k - 2
                            act.wait_ge(h_sdve, v0 + (kk // 8) * 16 + 8 + (kk % 8) + 1)
                        act.activation(scf(AB_TSB2[d % 2], 0, 128, 512), pw, AF.Identity,
                                       bias=bmu_sb[:, d:d + 1])
                        sact.inc(act)

            @blk.vector
            def _(v):
                for t in range(NT):
                    for gi in range(NG):
                        k = t * 8 + gi
                        po = [psA, psC][gi % 2]
                        v.wait_ge(h_sact, actV(k) + 3)
                        if t >= 1 and gi == 0:
                            v.wait_ge(h_spe, p0 + t * 32)
                        v.tensor_tensor(scb(AB_ATT + gi * 1024, 0, 128, 512),
                                        po[:, 0:512], scf(AB_TSB[gi % 2], 0, 128, 512),
                                        ALU.mult)
                        sdve.inc(v)
                    for d in range(NCH):
                        v.wait_ge(h_sact, a0 + t * 32 + 24 + d + 1)
                        v.tensor_tensor(xch(d, t * TW, TW), xch(d, t * TW, TW),
                                        scf(AB_TSB2[d % 2], 0, 128, 512), ALU.add)
                        sdve.inc(v)

        assert spe.v == p0 + NT * 32
        assert sact.v == a0 + NT * 32
        assert sdve.v == v0 + NT * 16

    def emit_ffn(self, s, which, store=False):
        nc = self.nc
        h_spe, h_sact, h_sdve = self.h_spe, self.h_sact, self.h_sdve
        h_dmf, dmf = self.h_dmf, self.dmf
        dmo, spe, sact, sdve = self.dmo, self.spe, self.sact, self.sdve
        scf, scb, wab, xch, lnch = self.scf, self.scb, self.wab, self.xch, self.lnch
        psA, psB, psC, psD = self.psA, self.psB, self.psC, self.psD
        bmu_sb, Y = self.bmu_sb, self.Y
        h_dmo = self.h_dmo
        MU_R = bmu_sb[:, 16:24]
        MU_K = bmu_sb[:, 24:32]

        def wsl(WS, d):
            return WS.rearrange("(c p) w -> c p w", p=128)[:, :, d * 128:(d + 1) * 128]

        A0, V0, P0 = sact.v, sdve.v, spe.v
        F0 = [dmf[b].v for b in range(4)]

        def pb(t):
            return P0 + 24 * t

        def ab(t):
            return A0 + 24 * t

        _SXK = [0, 24, 64, 104]
        _SXR = [16, 56, 96, 136]
        _PB = [40, 80, 120, 144]

        def vSk(t):
            return V0 + _SXK[t]

        def vSr(t):
            return V0 + _SXR[t]

        def vP(t):
            return V0 + _PB[t]

        def fb(b, t):
            return F0[b] + (4 if b < 2 else 8) * 16 * t

        with nc.Block(f"s{s}_ffn{which}") as _b:
            blk = BW(_b)

            def wk(b, c):
                return wab(WF_K[b], 0, 128, 1024).rearrange("p (c w) -> p c w", w=128)[:, c, :]

            def wr(b, c):
                return wab(WF_R[b], 0, 128, 1024).rearrange("p (c w) -> p c w", w=128)[:, c, :]

            def wv(b, c):
                return wab(WF_V[b], 0, 128, 1024).rearrange("p (c w) -> p c w", w=128)[:, c, :]

            @blk.sync
            def _(sync):
                for t in range(NT):
                    for d in range(NCH):
                        if d >= 2:
                            sync.wait_ge(h_spe, pb(t) + (d - 2) + 1)
                        elif t > 0:
                            sync.wait_ge(h_spe, pb(t - 1) + (d + 6) + 1)
                        dmf[d % 2].inc(sync.dma_start(
                            wab(WF_K[d % 2], 0, 128, 1024)
                            .rearrange("p (c w) -> p c w", w=128),
                            wsl(self.WKT_S, d).transpose([1, 0, 2])))
                    for d in range(NCH):
                        if d >= 2:
                            sync.wait_ge(h_spe, pb(t) + 8 + (d - 2) * 2 + 2)
                        elif t > 0:
                            sync.wait_ge(h_spe, pb(t - 1) + 8 + (d + 6) * 2 + 2)
                        dmf[2 + d % 2].inc(sync.dma_start(
                            wab(WF_R[d % 2], 0, 128, 1024)
                            .rearrange("p (c w) -> p c w", w=128),
                            wsl(self.WRT_S, d).transpose([1, 0, 2])))
                        dmf[2 + d % 2].inc(sync.dma_start(
                            wab(WF_V[d % 2], 0, 128, 1024)
                            .rearrange("p (c w) -> p c w", w=128),
                            wsl(self.WVT_S, d).transpose([1, 0, 2])))

            @blk.vector
            def _(v):
                def s_xk(t):
                    for c in range(NCH):
                        dd = scb(FF_DD[c % 2], 0, 128, 512)
                        if t > 0 and c == 0:
                            v.wait_ge(h_spe, pb(t - 1) + 8)
                        v.tensor_sub(dd, lnch(c, t * TW, TW), lnch(c, t * TW, TW, shift=1))
                        v.scalar_tensor_tensor(scb(FF_XK + c * 1024, 0, 128, 512), dd,
                                               MU_K[:, c:c + 1],
                                               lnch(c, t * TW, TW, shift=1),
                                               ALU.mult, ALU.add)
                        sdve.inc(v, 2)

                def s_xr(t):
                    for c in range(NCH):
                        dd = scb(FF_DD[c % 2], 0, 128, 512)
                        if t > 0 and c == 0:
                            v.wait_ge(h_spe, pb(t - 1) + 8 + 2 * 7 + 1)
                        v.tensor_sub(dd, lnch(c, t * TW, TW), lnch(c, t * TW, TW, shift=1))
                        v.scalar_tensor_tensor(scb(FF_XR + c * 1024, 0, 128, 512), dd,
                                               MU_R[:, c:c + 1],
                                               lnch(c, t * TW, TW, shift=1),
                                               ALU.mult, ALU.add)
                        sdve.inc(v)

                def pass2(t):
                    for d in range(NCH):
                        pv = [psC, psD][d % 2]
                        v.wait_ge(h_spe, pb(t) + 8 + d * 2 + 2)
                        v.wait_ge(h_sact, ab(t) + 16 + d + 1)
                        tmp = scf(FF_TMP[d % 2], 0, 128, 512)
                        v.tensor_tensor(tmp, scf(FF_RSB[d % 2], 0, 128, 512),
                                        pv[:, 0:512], ALU.mult)
                        v.tensor_tensor(xch(d, t * TW, TW), xch(d, t * TW, TW),
                                        tmp, ALU.add)
                        sdve.inc(v, 2)
                s_xk(0); s_xr(0); s_xk(1)
                pass2(0); s_xr(1); s_xk(2)
                pass2(1); s_xr(2); s_xk(3)
                pass2(2); s_xr(3)
                pass2(3)

            @blk.tensor
            def _(pe):
                for t in range(NT):
                    for d in range(NCH):
                        pk = [psA, psB][d % 2]
                        pe.wait_ge(h_dmf[d % 2], fb(d % 2, t) + 16 * (d // 2 + 1))
                        if d >= 2:
                            pe.wait_ge(h_sact, ab(t) + 2 * (d - 2) + 1)
                        elif t > 0:
                            pe.wait_ge(h_sact, ab(t - 1) + 16 + (d + 6) + 1)
                        for c in range(NCH):
                            if d == 0:
                                pe.wait_ge(h_sdve, vSk(t) + 2 * (c + 1))
                            pe.matmul(pk[:, 0:512], wk(d % 2, c),
                                      scb(FF_XK + c * 1024, 0, 128, 512),
                                      start=(c == 0), stop=(c == NCH - 1))
                        spe.inc(pe)
                    for d in range(NCH):
                        pr = [psA, psB][d % 2]
                        pv = [psC, psD][d % 2]
                        pe.wait_ge(h_dmf[2 + d % 2], fb(2 + d % 2, t) + 32 * (d // 2 + 1))
                        if d >= 2:
                            pe.wait_ge(h_sact, ab(t) + 16 + (d - 2) + 1)
                        else:
                            pe.wait_ge(h_sact, ab(t) + 2 * (6 + d) + 2)
                        for c in range(NCH):
                            if d == 0:
                                pe.wait_ge(h_sdve, vSr(t) + c + 1)
                            pe.matmul(pr[:, 0:512], wr(d % 2, c),
                                      scb(FF_XR + c * 1024, 0, 128, 512),
                                      start=(c == 0), stop=(c == NCH - 1))
                        spe.inc(pe)
                        pe.wait_ge(h_sact, ab(t) + 16)
                        if d >= 2:
                            pe.wait_ge(h_sdve, vP(t) + (d - 2) * 2 + 1)
                        elif t > 0:
                            pe.wait_ge(h_sdve, vP(t - 1) + (d + 6) * 2 + 1)
                        for c in range(NCH):
                            pe.matmul(pv[:, 0:512], wv(d % 2, c),
                                      scb(FF_KACT + c * 1024, 0, 128, 512),
                                      start=(c == 0), stop=(c == NCH - 1))
                        spe.inc(pe)

            @blk.scalar
            def _(act):
                for t in range(NT):
                    for d in range(NCH):
                        pk = [psA, psB][d % 2]
                        act.wait_ge(h_spe, pb(t) + d + 1)
                        if t > 0 and d < 2:
                            act.wait_ge(h_spe, pb(t - 1) + 8 + 2 * 7 + 2)
                        ka = scb(FF_KACT + d * 1024, 0, 128, 512)
                        act.activation(ka, pk[:, 0:512], AF.Relu)
                        sact.inc(act)
                        act.activation(ka, ka, AF.Square)
                        sact.inc(act)
                    for d in range(NCH):
                        pr = [psA, psB][d % 2]
                        act.wait_ge(h_spe, pb(t) + 8 + d * 2 + 1)
                        if d >= 2:
                            act.wait_ge(h_sdve, vP(t) + (d - 2) * 2 + 1)
                        elif t > 0:
                            act.wait_ge(h_sdve, vP(t - 1) + (d + 6) * 2 + 1)
                        act.activation(scf(FF_RSB[d % 2], 0, 128, 512), pr[:, 0:512],
                                       AF.Sigmoid)
                        sact.inc(act)

            if store:
                @blk.gpsimd
                def _(gp):
                    for t in range(NT):
                        for c in range(NCH):
                            gp.wait_ge(h_sdve, vP(t) + c * 2 + 2)
                            dmo.inc(gp.dma_start(
                                Y[s, c * 128:(c + 1) * 128, t * TW:(t + 1) * TW],
                                xch(c, t * TW, TW)))

        assert spe.v == P0 + NT * 24
        assert sact.v == A0 + NT * 24
        assert sdve.v == V0 + NT * 40

    def emit_attn_b(self, s):
        nc = self.nc
        h_dmo, h_spe, h_sact, h_sdve, h_spl = (self.h_dmo, self.h_spe,
                                               self.h_sact, self.h_sdve,
                                               self.h_spl)
        h_dmc, h_dmw2 = self.h_dmc, self.h_dmw2
        dmc, dmw2 = self.dmc, self.dmw2
        dmo, spe, sact, sdve, spl = (self.dmo, self.spe, self.sact,
                                     self.sdve, self.spl)
        scf, scb, wab, waf, xch, lnch, bgch = (self.scf, self.scb, self.wab,
                                               self.waf, self.xch, self.lnch,
                                               self.bgch)
        psA, psB, psC, psD, psW = self.psA, self.psB, self.psC, self.psD, self.psW
        s128t_bf, ones1b, bmu_sb = self.s128t_bf, self.ones1b, self.bmu_sb

        with nc.Block(f"s{s}_bp_w") as _b:
            blk = BW(_b)
            @blk.sync
            def _(sync):
                dmc.inc(sync.dma_start(
                    scf(0, 0, 128, 512).rearrange("p (c w) -> p c w", w=64),
                    self.WQA.rearrange("(c p) w -> p c w", p=128)))
                dmc.inc(sync.dma_start(scf(2048, 0, 64, 1024), self.WQB))
                dmc.inc(sync.dma_start(scf(6144, 0, 32, 512), self.WKB))
                dmc.inc(sync.dma_start(scf(8192, 0, 32, 256), self.WVB))
                dmc.inc(sync.dma_start(
                    scf(9216, 0, 128, 256).rearrange("p (c w) -> p c w", w=32),
                    self.WKVA.rearrange("(c p) w -> p c w", p=128)))

            @blk.vector
            def _(v):
                v.wait_ge(h_dmc, dmc.v)
                v.tensor_copy(wab(WB_WQA, 0, 128, 512), scf(0, 0, 128, 512))
                v.tensor_copy(wab(WB_WQB, 0, 64, 1024), scf(2048, 0, 64, 1024))
                v.tensor_copy(wab(WB_WKB, 0, 32, 512), scf(6144, 0, 32, 512))
                v.tensor_copy(wab(WB_WVB, 0, 32, 256), scf(8192, 0, 32, 256))
                v.tensor_copy(wab(WB_WKVA, 0, 128, 256), scf(9216, 0, 128, 256))
                sdve.inc(v)

        for NAME, DST in ((self.COSM, BP_COS), (self.SINM, BP_SIN)):
            with nc.Block(f"s{s}_bp_rope{DST}") as _b:
                blk = BW(_b)
                @blk.sync
                def _(sync):
                    sync.wait_ge(h_sdve, sdve.v)
                    dmc.inc(sync.dma_start(scf(BP_STAGE, 0, 128, 2048), NAME))

                @blk.vector
                def _(v):
                    v.wait_ge(h_dmc, dmc.v)
                    v.tensor_copy(scb(DST, 0, 128, 2048), scf(BP_STAGE, 0, 128, 2048))
                    sdve.inc(v)

        a0, p0 = sact.v, spe.v
        with nc.Block(f"s{s}_bp_lat") as _b:
            blk = BW(_b)
            @blk.tensor
            def _(pe):
                for t in range(NT):
                    pq = [psA, psB][t % 2]
                    pk = [psC, psD][t % 2]
                    if t >= 2:
                        pe.wait_ge(h_sact, a0 + (t - 2) * 2 + 2)
                    for c in range(NCH):
                        pe.matmul(pq[0:64, 0:512],
                                  wab(WB_WQA, 0, 128, 512)
                                  .rearrange("p (c w) -> p c w", w=64)[:, c, :],
                                  lnch(c, t * TW, TW), start=(c == 0), stop=(c == NCH - 1))
                    spe.inc(pe)
                    for c in range(NCH):
                        pe.matmul(pk[0:32, 0:512],
                                  wab(WB_WKVA, 0, 128, 256)
                                  .rearrange("p (c w) -> p c w", w=32)[:, c, :],
                                  lnch(c, t * TW, TW), start=(c == 0), stop=(c == NCH - 1))
                    spe.inc(pe)

            @blk.scalar
            def _(act):
                for t in range(NT):
                    pq = [psA, psB][t % 2]
                    pk = [psC, psD][t % 2]
                    act.wait_ge(h_spe, p0 + t * 2 + 1)
                    act.copy(scb(BP_LQ, 0, 64, 2048)[:, t * TW:(t + 1) * TW],
                             pq[0:64, 0:512])
                    sact.inc(act)
                    act.wait_ge(h_spe, p0 + t * 2 + 2)
                    act.copy(scb(BP_LKV, 0, 32, 2048)[:, t * TW:(t + 1) * TW],
                             pk[0:32, 0:512])
                    sact.inc(act)

        assert spe.v == p0 + 8 and sact.v == a0 + 8

        for phase, nchunks in (("q", NCH), ("k", 4)):
            a0, v0, p0 = sact.v, sdve.v, spe.v
            with nc.Block(f"s{s}_bp_{phase}") as _b:
                blk = BW(_b)
                @blk.tensor
                def _(pe):
                    n_i = nchunks * NT

                    def qs_mm(i):
                        ps_ = [psC, psD][i % 2]
                        pe.wait_ge(h_sact, a0 + i + 1)
                        if i >= 2:
                            pe.wait_ge(h_sdve, v0 + (i - 2) * 2 + 2)
                        pe.matmul(ps_[:, 0:512], s128t_bf[:, :],
                                  wab(WB_QSBT[i % 2], 0, 128, 512),
                                  start=True, stop=True)
                        spe.inc(pe)
                    for c in range(nchunks):
                        for t in range(NT):
                            i = c * NT + t
                            pq = [psA, psB][i % 2]
                            if i >= 2:
                                pe.wait_ge(h_sact, a0 + (i - 2) + 1)
                            if phase == "q":
                                pe.matmul(pq[:, 0:512],
                                          wab(WB_WQB, 0, 64, 1024)[:, c * 128:(c + 1) * 128],
                                          scb(BP_LQ, 0, 64, 2048)[:, t * TW:(t + 1) * TW],
                                          start=True, stop=True)
                            else:
                                pe.matmul(pq[:, 0:512],
                                          wab(WB_WKB, 0, 32, 512)[:, c * 128:(c + 1) * 128],
                                          scb(BP_LKV, 0, 32, 2048)[:, t * TW:(t + 1) * TW],
                                          start=True, stop=True)
                            spe.inc(pe)
                            if i >= 1:
                                qs_mm(i - 1)
                    qs_mm(n_i - 1)

                @blk.scalar
                def _(act):
                    for c in range(nchunks):
                        for t in range(NT):
                            i = c * NT + t
                            pq = [psA, psB][i % 2]
                            act.wait_ge(h_spe, p0 + (1 if i == 0 else 2 * i))
                            if i >= 2:
                                act.wait_ge(h_sdve, v0 + (i - 2) * 2 + 2)
                            act.copy(wab(WB_QSBT[i % 2], 0, 128, 512), pq[:, 0:512])
                            sact.inc(act)

                @blk.vector
                def _(v):
                    for c in range(nchunks):
                        for t in range(NT):
                            i = c * NT + t
                            n_i = nchunks * NT
                            ps_ = [psC, psD][i % 2]
                            v.wait_ge(h_spe, p0 + (2 * i + 3 if i < n_i - 1 else 2 * n_i))
                            t1 = wab(WB_RT1, 0, 128, 512)
                            t2 = wab(WB_RT2, 0, 128, 512)
                            v.tensor_tensor(t1, ps_[:, 0:512],
                                            scb(BP_SIN, 0, 128, 2048)[:, t * TW:(t + 1) * TW],
                                            ALU.mult)
                            v.tensor_tensor(t2, wab(WB_QSBT[i % 2], 0, 128, 512),
                                            scb(BP_COS, 0, 128, 2048)[:, t * TW:(t + 1) * TW],
                                            ALU.mult)
                            if phase == "q":
                                dst = bgch(c, t * TW, TW)
                            else:
                                dst = scb(BP_KREP + c * 4096, 0, 128, 2048)[:, t * TW:(t + 1) * TW]
                            v.tensor_tensor(dst, t2, t1, ALU.add)
                            sdve.inc(v, 2)

            assert spe.v == p0 + nchunks * NT * 2
            assert sact.v == a0 + nchunks * NT
            assert sdve.v == v0 + nchunks * NT * 2

        a0, v0, p0 = sact.v, sdve.v, spe.v
        with nc.Block(f"s{s}_bp_v") as _b:
            blk = BW(_b)
            @blk.vector
            def _(v):
                vt = scb(BS_VTOK, 0, 128, 16 * 260)
                v.memset(vt.rearrange("p (j g w) -> p j g w", g=4, w=65)[:, :, :, 64:65],
                         1.0)
                sdve.inc(v)

            @blk.tensor
            def _(pe):
                for j in range(16):
                    pv = [psA, psB][j % 2]
                    if j >= 2:
                        pe.wait_ge(h_sact, a0 + (j - 2) + 1)
                    pe.matmul(pv[:, 0:256],
                              scb(BP_LKV, 0, 32, 2048)[:, j * 128:(j + 1) * 128],
                              wab(WB_WVB, 0, 32, 256), start=True, stop=True)
                    spe.inc(pe)

            @blk.scalar
            def _(act):
                vt = scb(BS_VTOK, 0, 128, 16 * 260)
                for j in range(16):
                    pv = [psA, psB][j % 2]
                    act.wait_ge(h_spe, p0 + j + 1)
                    act.copy(vt[:, j * 260: j * 260 + 260]
                             .rearrange("p (g w) -> p g w", w=65)[:, :, 0:64],
                             pv[:, 0:256].rearrange("p (g w) -> p g w", w=64))
                    sact.inc(act)

        assert spe.v == p0 + 16 and sact.v == a0 + 16

        njs = [4 * i + 4 for i in range(NT)]
        a0, v0, p0, l0 = sact.v, sdve.v, spe.v, spl.v

        ops = []
        pend_pv = None
        pend_bc = None
        for pi in range(NPAIR):
            for i in range(NT):
                for j in range(njs[i]):
                    ops.append(("scA", pi, i, j))
                    ops.append(("scB", pi, i, j))
                    if pend_bc is not None and j == 1:
                        ops.append(("bcA",) + pend_bc)
                        ops.append(("bcB",) + pend_bc)
                        pend_bc = None
                    if pend_pv is not None:
                        ops.append(("pvA",) + pend_pv)
                        ops.append(("pvB",) + pend_pv)
                    pend_pv = (pi, i, j)
                ops.append(("pvA",) + pend_pv)
                ops.append(("pvB",) + pend_pv)
                pend_pv = None
                assert pend_bc is None
                pend_bc = (pi, i)
        ops.append(("bcA",) + pend_bc)
        ops.append(("bcB",) + pend_bc)
        val = {op: k + 1 for k, op in enumerate(ops)}

        def actb(p, i):
            return a0 + p * sum(2 * n + 4 for n in njs) + sum(2 * njs[k] + 4 for k in range(i))

        def dveb(p, i):
            return v0 + p * NT * 20 + 20 * i

        def splb(p, i):
            return l0 + 16 * (p * NT + i)

        def pv_done(pi, i, j):
            return p0 + val[("pvB", pi, i, j)]

        def prev_pv_all(pi, i):
            if i > 0:
                return pv_done(pi, i - 1, njs[i - 1] - 1)
            if pi > 0:
                return pv_done(pi - 1, NT - 1, njs[NT - 1] - 1)
            return 0

        PAb = [BS_PA0, BS_PA0 + 1024, BS_PA0 + 2048]
        PBb = [BS_PB0, BS_PB0 + 1024, BS_PB0 + 2048]
        ohsbB = self.ohsb_t[:, :]
        bw_ld = [None]

        with nc.Block(f"s{s}_bs") as _b:
            blk = BW(_b)

            @blk.sync
            def _(sync):
                dmw2.inc(sync.dma_start(
                    wab(WA_WOUT, 0, 128, 8192).rearrange("p (c w) -> p c w", w=1024),
                    self.WOUTB_S.rearrange("(c p) w -> c p w", p=128).transpose([1, 0, 2])))
                bw_ld[0] = dmw2.v

            @blk.tensor
            def _(pe):
                for op in ops:
                    kind, pi = op[0], op[1]
                    gkv = pi // 2
                    if kind in ("scA", "scB"):
                        _, pi, i, j = op
                        q0 = max(128 * j, TW * i)
                        w = TW * (i + 1) - q0
                        ps_ = [[psA, psC], [psB, psD]][kind == "scB"][j % 2]
                        if kind == "scA":
                            if j >= 2:
                                pe.wait_ge(h_sact, actb(pi, i) + 2 * (j - 2) + 2)
                            elif i > 0:
                                pe.wait_ge(h_sact, actb(pi, i - 1) + 2 * njs[i - 1])
                            elif pi > 0:
                                pe.wait_ge(h_sact, actb(pi - 1, NT - 1) + 2 * njs[NT - 1])
                        base = 0 if kind == "scA" else 64
                        pe.matmul(ps_[:, 0:w],
                                  scb(BP_KREP + gkv * 4096, 0, 128, 2048)
                                  [base:base + 64, j * 128:(j + 1) * 128],
                                  bgch(pi, q0, w, p0=base, np_=64),
                                  start=True, stop=True)
                        spe.inc(pe)
                    elif kind in ("pvA", "pvB"):
                        _, pi, i, j = op
                        if kind == "pvA":
                            pe.wait_ge(h_sact, actb(pi, i) + 2 * j + 2)
                            if j >= 4 * i:
                                jj = j - 4 * i
                                pe.wait_ge(h_sdve, dveb(pi, i) + 4 * jj + 4)
                            if j == 0:
                                if i > 0 or pi > 0:
                                    pe.wait_ge(h_sdve, dveb(pi, i))
                                    pe.wait_ge(h_spl, splb(pi, i))
                        vsl = scb(BS_VTOK, 0, 128, 16 * 260)[
                            :, j * 260 + gkv * 65: j * 260 + gkv * 65 + 65]
                        pbuf = scb((PAb if kind == "pvA" else PBb)[j % 3], 0, 128, 512)
                        col = 0 if kind == "pvA" else 512
                        pe.matmul(psW[0:65, col:col + 512], vsl, pbuf,
                                  start=(j == 0), stop=(j == njs[i] - 1))
                        spe.inc(pe)
                    elif kind == "bcA":
                        _, pi, i = op
                        pe.wait_ge(h_sact, actb(pi, i) + 2 * njs[i] + 2)
                        pe.wait_ge(h_sdve, dveb(pi, i))
                        pe.matmul(psW[0:64, 1024:1536], ones1b[:, :],
                                  self.recs_bf[:, 0:512], start=True, stop=True)
                        spe.inc(pe)
                    else:
                        _, pi, i = op
                        pe.wait_ge(h_sact, actb(pi, i) + 2 * njs[i] + 4)
                        pe.matmul(psW[64:128, 1536:2048], ones1b[:, :],
                                  self.recs_bf[:, 512:1024], start=True, stop=True,
                                  tile_position=(0, 64))
                        spe.inc(pe)

            @blk.scalar
            def _(act):
                for pi in range(NPAIR):
                    for i in range(NT):
                        for j in range(njs[i]):
                            q0 = max(128 * j, TW * i)
                            w = TW * (i + 1) - q0
                            poff = q0 - TW * i
                            psSA = [psA, psC][j % 2]
                            psSB = [psB, psD][j % 2]
                            pA = scb(PAb[j % 3], 0, 128, 512)
                            pB = scb(PBb[j % 3], 0, 128, 512)
                            act.wait_ge(h_spe, p0 + val[("scB", pi, i, j)])
                            if j >= 3:
                                act.wait_ge(h_spe, pv_done(pi, i, j - 3))
                            else:
                                pw = prev_pv_all(pi, i)
                                if pw:
                                    act.wait_ge(h_spe, pw)
                            act.activation(pA[:, poff:poff + w], psSA[:, 0:w],
                                           AF.Exp, scale=0.125)
                            act.activation(pB[:, poff:poff + w], psSB[:, 0:w],
                                           AF.Exp, scale=0.125)
                            sact.inc(act, 2)
                        act.wait_ge(h_spe, pv_done(pi, i, njs[i] - 1))
                        act.activation(self.drow_f[:, :], psW[64:65, 0:512], AF.Ln)
                        act.activation(self.recs_bf[:, 0:512], self.drow_f[:, :],
                                       AF.Exp, scale=-1.0)
                        sact.inc(act, 2)
                        act.activation(self.drow_f[:, :], psW[64:65, 512:1024], AF.Ln)
                        act.activation(self.recs_bf[:, 512:1024], self.drow_f[:, :],
                                       AF.Exp, scale=-1.0)
                        sact.inc(act, 2)

            @blk.gpsimd
            def _(gp):
                for pi in range(NPAIR):
                    for i in range(NT):
                        gp.wait_ge(h_sdve, dveb(pi, i) + 18)
                        spl.inc(gp.dma_start(
                            self.lnx[64:128,
                                     pi * (T + 1) + 1 + i * TW: pi * (T + 1) + 1 + (i + 1) * TW],
                            ohsbB), 16)

            @blk.vector
            def _(v):
                for pi in range(NPAIR):
                    for i in range(NT):
                        for jj in range(4):
                            j = 4 * i + jj
                            q0 = 128 * j
                            poff = q0 - TW * i
                            pA = scb(PAb[j % 3], 0, 128, 512)
                            pB = scb(PBb[j % 3], 0, 128, 512)
                            if poff > 0:
                                if j >= 3:
                                    v.wait_ge(h_spe, pv_done(pi, i, j - 3))
                                v.memset(pA[:, 0:poff], 0.0)
                                v.memset(pB[:, 0:poff], 0.0)
                                sdve.inc(v, 2)
                            else:
                                sdve.pad(v, 2)
                            v.wait_ge(h_sact, actb(pi, i) + 2 * j + 2)
                            v.tensor_mul(pA[:, poff:poff + 128],
                                         pA[:, poff:poff + 128],
                                         self.maskd_bf[:, :])
                            v.tensor_mul(pB[:, poff:poff + 128],
                                         pB[:, poff:poff + 128],
                                         self.maskd_bf[:, :])
                            sdve.inc(v, 2)
                        v.wait_ge(h_spe, pv_done(pi, i, njs[i] - 1))
                        v.tensor_copy(scb(BS_OHSBA, 0, 128, 512)[0:64, :],
                                      psW[0:64, 0:512])
                        sdve.inc(v)
                        v.wait_ge(h_spl, splb(pi, i))
                        v.tensor_copy(ohsbB, psW[0:64, 512:1024])
                        sdve.inc(v)
                        v.wait_ge(h_spe, p0 + val[("bcA", pi, i)])
                        v.tensor_tensor(
                            self.lnx[0:64, pi * (T + 1) + 1 + i * TW:
                                     pi * (T + 1) + 1 + (i + 1) * TW],
                            scb(BS_OHSBA, 0, 128, 512)[0:64, :],
                            psW[0:64, 1024:1536], ALU.mult)
                        sdve.inc(v)
                        v.wait_ge(h_spe, p0 + val[("bcB", pi, i)])
                        v.wait_ge(h_spl, splb(pi, i) + 16)
                        ln_b = self.lnx[64:128, pi * (T + 1) + 1 + i * TW:
                                        pi * (T + 1) + 1 + (i + 1) * TW]
                        v.tensor_tensor(ln_b, ln_b, psW[64:128, 1536:2048], ALU.mult)
                        sdve.inc(v)

        assert spe.v == p0 + len(ops)
        assert sact.v == a0 + NPAIR * sum(2 * n + 4 for n in njs)
        assert sdve.v == v0 + NPAIR * NT * 20
        assert spl.v == l0 + 16 * NPAIR * NT

        a0, v0, p0 = sact.v, sdve.v, spe.v
        with nc.Block(f"s{s}_bw") as _b:
            blk = BW(_b)

            @blk.tensor
            def _(pe):
                pe.wait_ge(h_dmw2, bw_ld[0])
                for t in range(NT):
                    for d in range(NCH):
                        i = t * NCH + d
                        pw = [psA, psB, psC, psD][i % 4]
                        if i >= 4:
                            pe.wait_ge(h_sdve, v0 + (i - 4) + 1)
                        for c in range(NCH):
                            pe.matmul(pw[:, 0:512],
                                      wab(WA_WOUT, 0, 128, 8192).rearrange(
                                          "p (c w) -> p c w", w=1024)[:, c, d * 128:(d + 1) * 128],
                                      lnch(c, t * TW, TW), start=(c == 0),
                                      stop=(c == NCH - 1))
                        spe.inc(pe)

            @blk.scalar
            def _(act):
                for t in range(NT):
                    for d in range(NCH):
                        i = t * NCH + d
                        pw = [psA, psB, psC, psD][i % 4]
                        act.wait_ge(h_spe, p0 + i + 1)
                        if i >= 2:
                            act.wait_ge(h_sdve, v0 + (i - 2) + 1)
                        act.activation(scf(BW_TSB2[i % 2], 0, 128, 512), pw[:, 0:512],
                                       AF.Identity, bias=bmu_sb[:, 8 + d: 8 + d + 1])
                        sact.inc(act)

            @blk.vector
            def _(v):
                for t in range(NT):
                    for d in range(NCH):
                        i = t * NCH + d
                        v.wait_ge(h_sact, a0 + i + 1)
                        v.tensor_tensor(xch(d, t * TW, TW), xch(d, t * TW, TW),
                                        scf(BW_TSB2[i % 2], 0, 128, 512), ALU.add)
                        sdve.inc(v)

        assert spe.v == p0 + 32 and sact.v == a0 + 32 and sdve.v == v0 + 32



_CACHE = {}


def _host_prep(inputs):
    x = np.ascontiguousarray(np.asarray(inputs["L0_attn_a_x"]).transpose(0, 2, 1))
    wqkv = np.ascontiguousarray(np.asarray(inputs["L0_attn_a_W_qkv"], np.float32))
    rf = np.asarray(inputs["L0_attn_a_RF"], np.float32)
    rfp = np.zeros((8, 128, 128), np.float32)
    for g in range(8):
        rfp[g, 0:64, 0:64] = rf[2 * g]
        rfp[g, 64:128, 64:128] = rf[2 * g + 1]
    wouta = np.ascontiguousarray(np.asarray(inputs["L0_attn_a_W_out"], np.float32))
    b_out_a = np.asarray(inputs["L0_attn_a_b_out"], np.float32)
    wqa = np.ascontiguousarray(np.asarray(inputs["L0_attn_b_Wqa"], np.float32))
    wqb = np.ascontiguousarray(np.asarray(inputs["L0_attn_b_Wqb"], np.float32))
    wkva = np.ascontiguousarray(np.asarray(inputs["L0_attn_b_Wkva"], np.float32))
    wkvb = np.asarray(inputs["L0_attn_b_Wkvb"], np.float32).reshape(32, 4, 64, 2)
    wkb = np.zeros((32, 512), np.float32)
    for g in range(4):
        wkb[:, g * 128: g * 128 + 64] = wkvb[:, g, :, 0]
        wkb[:, g * 128 + 64: g * 128 + 128] = wkvb[:, g, :, 0]
    wvb = np.ascontiguousarray(wkvb[..., 1].reshape(32, 256))
    woutb = np.ascontiguousarray(np.asarray(inputs["L0_attn_b_Wout"], np.float32))
    b_out_b = np.asarray(inputs["L0_attn_b_bout"], np.float32)
    wrt = np.ascontiguousarray(np.asarray(inputs["L0_ffn_W_r"], np.float32).T)
    wkt = np.ascontiguousarray(np.asarray(inputs["L0_ffn_W_k"], np.float32).T)
    wvt = np.ascontiguousarray(np.asarray(inputs["L0_ffn_W_v"], np.float32).T)
    mu_r = np.asarray(inputs["L0_ffn_mu_r"], np.float32)
    mu_k = np.asarray(inputs["L0_ffn_mu_k"], np.float32)
    bmu = np.zeros((128, 32), np.float32)
    bmu[:, 0:8] = b_out_a.reshape(8, 128).T
    bmu[:, 8:16] = b_out_b.reshape(8, 128).T
    bmu[:, 16:24] = mu_r.reshape(8, 128).T
    bmu[:, 24:32] = mu_k.reshape(8, 128).T

    D = 64
    inv = (1.0 / (10000.0 ** (np.arange(0, D, 2, dtype=np.float32) / np.float32(D)))).astype(np.float32)
    tpos = np.arange(T, dtype=np.float32)
    fr = np.outer(tpos, inv).astype(np.float32)
    cos64 = np.cos(fr).astype(np.float32).T
    sin64 = np.sin(fr).astype(np.float32).T
    cosm = np.repeat(cos64, 2, axis=0)
    sinm = np.repeat(sin64, 2, axis=0)
    cosm = np.tile(cosm, (2, 1))
    sinm = np.tile(sinm, (2, 1))
    s64 = np.zeros((64, 64), np.float32)
    for i2 in range(32):
        s64[2 * i2, 2 * i2 + 1] = 1.0
        s64[2 * i2 + 1, 2 * i2] = -1.0
    s128t = np.zeros((128, 128), np.float32)
    s128t[0:64, 0:64] = s64
    s128t[64:128, 64:128] = s64
    maskd = (np.arange(128)[:, None] <= np.arange(128)[None, :]).astype(np.float32)
    sel2 = np.zeros((2, 128), np.float32)
    sel2[0, 0:64] = 1.0
    sel2[1, 64:128] = 1.0

    com = dict(wqkv=wqkv, rf=rfp, wouta=wouta, wqa=wqa, wqb=wqb, wkva=wkva,
               wkb=wkb, wvb=wvb, woutb=woutb, wrt=wrt, wkt=wkt, wvt=wvt,
               bmu=bmu, cosm=cosm, sinm=sinm, s128t=s128t, maskd=maskd,
               sel2=sel2)
    return x, com


def get_nc(tap=None):
    key = ("nc", tap)
    if key not in _CACHE:
        _CACHE[key] = MK(tap=tap).build()
    return _CACHE[key]


def kernel(**inputs):
    x, com = _host_prep(inputs)
    nc = get_nc()
    in_maps = []
    for k in range(NCORES):
        m = {"x": np.ascontiguousarray(x[k * NSEQ:(k + 1) * NSEQ])}
        m.update(com)
        in_maps.append(m)
    res = run_bass_kernel_spmd(nc, in_maps, core_ids=list(range(NCORES)))
    y = np.stack([np.asarray(r["y"]) for r in res.results])
    y = y.reshape(B, C, T).transpose(0, 2, 1)
    return np.ascontiguousarray(y.astype(np.float32))


# version 16
# speedup vs baseline: 1.2283x; 1.0239x over previous
import numpy as np
import concourse.bass as bass
import concourse.mybir as mybir
from concourse.bass_utils import run_bass_kernel_spmd

F32 = mybir.dt.float32
BF16 = mybir.dt.bfloat16
AF = mybir.ActivationFunctionType
ALU = mybir.AluOpType

NCORES = 8
B, T, C = 32, 2048, 1024
NSEQ = B // NCORES
NCH = C // 128
NT = 4
TW = 512
NG = 8
NPAIR = 8
EPS = 1e-5
LN_DVE_T = 25
STAB = 1e-6

LN_XB = [c * 1024 for c in range(8)]
LN_SQ = [8192 + c * 1024 for c in range(8)]
LN_ROW0 = 16384
LN_ROW1 = 18432
LN_TS = [20480, 22528]
AA_QSC = [0, 1024, 2048, 3072]
AA_KSC = [4096, 5120, 6144, 7168]
AA_PHIK = [8192, 8704]
AA_VTK = [9216, 9728]
AA_KVBD = 24576
AA_KSUM = 26624
AB_ATT = 0
AB_NRM = 8192
AB_TSB = [12288, 14336]
AB_TSB2 = [16384, 18432]
FF_XR = 0
FF_XK = 8192
FF_KACT = 16384
FF_DD = [24576, 25600]
FF_RSB = [26624, 28672]
FF_TMP = [30720, 24576]
BP_COS = 0
BP_SIN = 4096
BP_STAGE = 8192
BP_LQ = 8192
BP_LKV = 12288
BP_KREP = 16384
BS_VTOK = 0
BS_PA0 = 8320
BS_PB0 = 11392
BS_OHSBA = 14464
BW_TSB2 = [0, 2048]
ST_F32 = [0, 4096, 8192, 12288]
ST_BF = [16384, 18432, 20480, 22528]

WA_WQKV = [0, 6144]
WA_WOUT = 0
WF_K = [0, 2048, 4096, 6144]
WF_R = [8192, 10240]
WF_V = [12288, 14336]
WB_WQA = 0
WB_WQB = 1024
WB_WKB = 3072
WB_WVB = 4096
WB_WKVA = 4608
WB_MASKD = 5120
WB_QSBT = [5632, 6656]
WB_RT1 = 7680
WB_RT2 = 9728
WB_RECA = 11776
WB_RECB = 12800
WB_DROW = 13824


class EngP:

    def __init__(self, e):
        object.__setattr__(self, "e", e)
        object.__setattr__(self, "last", None)

    def __getattr__(self, name):
        f = getattr(object.__getattribute__(self, "e"), name)
        if not callable(f):
            return f

        def wrap(*a, **k):
            r = f(*a, **k)
            if hasattr(r, "then_inc"):
                object.__setattr__(self, "last", r)
            return r
        return wrap


class BW:

    def __init__(self, blk):
        self._blk = blk

    def __getattr__(self, name):
        dec = getattr(self._blk, name)

        def mydec(f):
            return dec(lambda e: f(EngP(e)))
        return mydec


class Ctr:
    def __init__(self, h, step=1):
        self.h = h
        self.v = 0
        self.step = step

    def inc(self, x, k=None):
        k = self.step if k is None else k
        if isinstance(x, EngP):
            last = object.__getattribute__(x, "last")
            if last is not None:
                last.then_inc(self.h, k)
            else:
                object.__getattribute__(x, "e").sem_inc(self.h, k)
        elif hasattr(x, "then_inc"):
            x.then_inc(self.h, k)
        else:
            x.sem_inc(self.h, k)
        self.v += k
        return self.v

    def pad(self, engp, k=None):
        k = self.step if k is None else k
        e = object.__getattribute__(engp, "e") if isinstance(engp, EngP) else engp
        e.sem_inc(self.h, k)
        self.v += k
        return self.v


class MK:
    def __init__(self, tap=None):
        self.tap = tap
        nc = bass.Bass("TRN2", target_bir_lowering=False,
                       detect_race_conditions=False)
        self.nc = nc
        self.X = nc.dram_tensor("x", [NSEQ, C, T], F32, kind="ExternalInput").ap()
        self.Y = nc.dram_tensor("y", [NSEQ, C, T], F32, kind="ExternalOutput").ap()
        self.WQKV = nc.dram_tensor("wqkv", [C, 3 * C], F32, kind="ExternalInput").ap()
        self.RF = nc.dram_tensor("rf", [NG, 128, 128], F32, kind="ExternalInput").ap()
        self.WOUTA = nc.dram_tensor("wouta", [C, C], F32, kind="ExternalInput").ap()
        self.WQA = nc.dram_tensor("wqa", [C, 64], F32, kind="ExternalInput").ap()
        self.WQB = nc.dram_tensor("wqb", [64, C], F32, kind="ExternalInput").ap()
        self.WKVA = nc.dram_tensor("wkva", [C, 32], F32, kind="ExternalInput").ap()
        self.WKB = nc.dram_tensor("wkb", [32, 512], F32, kind="ExternalInput").ap()
        self.WVB = nc.dram_tensor("wvb", [32, 256], F32, kind="ExternalInput").ap()
        self.WOUTB = nc.dram_tensor("woutb", [C, C], F32, kind="ExternalInput").ap()
        self.WRT = nc.dram_tensor("wrt", [C, C], F32, kind="ExternalInput").ap()
        self.WKT = nc.dram_tensor("wkt", [C, C], F32, kind="ExternalInput").ap()
        self.WVT = nc.dram_tensor("wvt", [C, C], F32, kind="ExternalInput").ap()
        self.BMU = nc.dram_tensor("bmu", [128, 32], F32, kind="ExternalInput").ap()
        self.COSM = nc.dram_tensor("cosm", [128, T], F32, kind="ExternalInput").ap()
        self.SINM = nc.dram_tensor("sinm", [128, T], F32, kind="ExternalInput").ap()
        self.S128T = nc.dram_tensor("s128t", [128, 128], F32, kind="ExternalInput").ap()
        self.MASKD = nc.dram_tensor("maskd", [128, 128], F32, kind="ExternalInput").ap()
        self.SEL2 = nc.dram_tensor("sel2", [2, 128], F32, kind="ExternalInput").ap()
        self.WQKV_S = nc.dram_tensor("wqkv_s", [NG, NCH, 128, 384], BF16).ap()
        self.WOUTA_S = nc.dram_tensor("wouta_s", [C, C], BF16).ap()
        self.WOUTB_S = nc.dram_tensor("woutb_s", [C, C], BF16).ap()
        self.WRT_S = nc.dram_tensor("wrt_s", [C, C], BF16).ap()
        self.WKT_S = nc.dram_tensor("wkt_s", [C, C], BF16).ap()
        self.WVT_S = nc.dram_tensor("wvt_s", [C, C], BF16).ap()
        self.TAPX = None
        if tap is not None:
            self.TAPX = nc.dram_tensor("tapx", [C, T], F32, kind="ExternalOutput").ap()

    def scf(self, off, p0, np_, w):
        assert off % 4 == 0 and off + 0 <= 32768
        return self.scr[p0:p0 + np_, off // 4: off // 4 + w]

    def scb(self, off, p0, np_, w):
        assert off % 2 == 0
        return self.scr.ap().bitcast(BF16)[p0:p0 + np_, off // 2: off // 2 + w]

    def wab(self, off, p0, np_, w):
        assert off % 2 == 0
        return self.warena[p0:p0 + np_, off // 2: off // 2 + w]

    def waf(self, off, p0, np_, w):
        assert off % 4 == 0
        return self.warena.ap().bitcast(F32)[p0:p0 + np_, off // 4: off // 4 + w]

    def xch(self, c, t0=0, w=T):
        return self.x_sb[:, c * T + t0: c * T + t0 + w]

    def lnch(self, c, t0=0, w=T, shift=0, p0=0, np_=128):
        base = c * (T + 1) + 1 - shift + t0
        return self.lnx[p0:p0 + np_, base: base + w]

    def bgch(self, c, t0=0, w=T, p0=0, np_=128):
        return self.biga[p0:p0 + np_, c * T + t0: c * T + t0 + w]

    def build(self):
        nc = self.nc
        from contextlib import ExitStack
        with ExitStack() as st:
            x_sb = st.enter_context(nc.sbuf_tensor("x_sb", [128, NCH * T], F32))
            lnx = st.enter_context(nc.sbuf_tensor("lnx", [128, NCH * (T + 1)], BF16))
            biga = st.enter_context(nc.sbuf_tensor("biga", [128, NCH * T], BF16))
            warena = st.enter_context(nc.sbuf_tensor("warena", [128, 8192], BF16))
            scr = st.enter_context(nc.sbuf_tensor("scr", [128, 8192], F32))
            rf_bf = st.enter_context(nc.sbuf_tensor("rf_bf", [128, NG * 128], BF16))
            s128t_bf = st.enter_context(nc.sbuf_tensor("s128t_bf", [128, 128], BF16))
            sel2_f = st.enter_context(nc.sbuf_tensor("sel2_f", [2, 128], F32))
            ones1f = st.enter_context(nc.sbuf_tensor("ones1f", [1, 128], F32))
            ones1b = st.enter_context(nc.sbuf_tensor("ones1b", [1, 64], BF16))
            onesc_bf = st.enter_context(nc.sbuf_tensor("onesc_bf", [128, 8], BF16))
            bmu_sb = st.enter_context(nc.sbuf_tensor("bmu_sb", [128, 32], F32))
            cst = st.enter_context(nc.sbuf_tensor("cst", [128, 2], F32))
            maskd_bf = st.enter_context(nc.sbuf_tensor("maskd_bf", [128, 128], BF16))
            recs_bf = st.enter_context(nc.sbuf_tensor("recs_bf", [1, 1024], BF16))
            drow_f = st.enter_context(nc.sbuf_tensor("drow_f", [1, 512], F32))
            self.maskd_bf, self.recs_bf, self.drow_f = maskd_bf, recs_bf, drow_f
            ohsb_t = st.enter_context(nc.sbuf_tensor("ohsb_t", [64, 512], BF16))
            self.ohsb_t = ohsb_t
            nc.const_aps.register(F32, EPS, cst[:, 0:1])
            nc.const_aps.register(F32, STAB, cst[:, 1:2])
            self.cst = cst
            psA = st.enter_context(nc.psum_tensor("psA", [128, 512], F32))
            psB = st.enter_context(nc.psum_tensor("psB", [128, 512], F32))
            psC = st.enter_context(nc.psum_tensor("psC", [128, 512], F32))
            psD = st.enter_context(nc.psum_tensor("psD", [128, 512], F32))
            psW = st.enter_context(nc.psum_tensor("psW", [128, 2048], F32))
            h_dmx = st.enter_context(nc.semaphore("dmx"))
            h_dmk0 = st.enter_context(nc.semaphore("dmk0"))
            h_dmk1 = st.enter_context(nc.semaphore("dmk1"))
            h_dmw2 = st.enter_context(nc.semaphore("dmw2"))
            h_dmf0 = st.enter_context(nc.semaphore("dmf0"))
            h_dmf1 = st.enter_context(nc.semaphore("dmf1"))
            h_dmf2 = st.enter_context(nc.semaphore("dmf2"))
            h_dmf3 = st.enter_context(nc.semaphore("dmf3"))
            h_dms0 = st.enter_context(nc.semaphore("dms0"))
            h_dms1 = st.enter_context(nc.semaphore("dms1"))
            h_dmso0 = st.enter_context(nc.semaphore("dmso0"))
            h_dmso1 = st.enter_context(nc.semaphore("dmso1"))
            h_dmc = st.enter_context(nc.semaphore("dmc"))
            h_dmo = st.enter_context(nc.semaphore("dmo"))
            h_spe = st.enter_context(nc.semaphore("spe"))
            h_sact = st.enter_context(nc.semaphore("sact"))
            h_sdve = st.enter_context(nc.semaphore("sdve"))
            h_spl = st.enter_context(nc.semaphore("spl"))
            h_spool = st.enter_context(nc.semaphore("spool"))
            self.x_sb, self.lnx, self.biga = x_sb, lnx, biga
            self.warena, self.scr = warena, scr
            self.rf_bf, self.s128t_bf, self.sel2_f = rf_bf, s128t_bf, sel2_f
            self.ones1f, self.ones1b, self.onesc_bf = ones1f, ones1b, onesc_bf
            self.bmu_sb = bmu_sb
            self.psA, self.psB, self.psC, self.psD, self.psW = psA, psB, psC, psD, psW
            self.h_dmx, self.h_dmw2, self.h_dmc, self.h_dmo = h_dmx, h_dmw2, h_dmc, h_dmo
            self.h_dmk = [h_dmk0, h_dmk1]
            self.h_dmf = [h_dmf0, h_dmf1, h_dmf2, h_dmf3]
            self.h_dms = [h_dms0, h_dms1]
            self.h_dmso = [h_dmso0, h_dmso1]
            self.h_spe, self.h_sact, self.h_sdve, self.h_spl = h_spe, h_sact, h_sdve, h_spl
            self.dmx, self.dmw2, self.dmc, self.dmo = (Ctr(h_dmx, 16), Ctr(h_dmw2, 16),
                                                       Ctr(h_dmc, 16), Ctr(h_dmo, 16))
            self.dmk = [Ctr(h_dmk0, 16), Ctr(h_dmk1, 16)]
            self.dmf = [Ctr(h_dmf0, 16), Ctr(h_dmf1, 16), Ctr(h_dmf2, 16), Ctr(h_dmf3, 16)]
            self.dms = [Ctr(h_dms0, 16), Ctr(h_dms1, 16)]
            self.dmso = [Ctr(h_dmso0, 16), Ctr(h_dmso1, 16)]
            self.spe, self.sact, self.sdve, self.spl = (Ctr(h_spe), Ctr(h_sact),
                                                        Ctr(h_sdve), Ctr(h_spl))
            self.h_spool = h_spool
            self.spool = Ctr(h_spool)
            self.emit_staging()
            self.emit_consts()
            for s in range(NSEQ):
                self.emit_seq(s)
            with nc.Block("fin") as _b:
                blk = BW(_b)
                @blk.gpsimd
                def _(g):
                    g.wait_ge(self.h_dmo, self.dmo.v)

                @blk.sync
                def _(sync):
                    sync.wait_ge(self.h_dmo, self.dmo.v)
        nc.finalize()
        return nc

    def emit_staging(self):
        nc = self.nc
        jobs = []
        for c in range(NCH):
            for sub in range(3):
                jobs.append((self.WQKV[c * 128:(c + 1) * 128, sub * 1024:(sub + 1) * 1024],
                             self.WQKV_S[:, c, :, sub * 128:(sub + 1) * 128], "qkv"))
        for W, WS in ((self.WOUTA, self.WOUTA_S), (self.WOUTB, self.WOUTB_S),
                      (self.WRT, self.WRT_S), (self.WKT, self.WKT_S),
                      (self.WVT, self.WVT_S)):
            for c in range(NCH):
                jobs.append((W[c * 128:(c + 1) * 128, :],
                             WS[c * 128:(c + 1) * 128, :], "row"))
        v0 = self.sdve.v
        h_sdve, sdve = self.h_sdve, self.sdve
        ld_sem, ld_h = self.dmf, self.h_dmf
        st_sem = [self.dms[0], self.dms[1], self.dmso[0], self.dmso[1]]
        st_h = [self.h_dms[0], self.h_dms[1], self.h_dmso[0], self.h_dmso[1]]
        sl0 = [c.v for c in ld_sem]
        so0 = [c.v for c in st_sem]
        scf, scb = self.scf, self.scb
        with nc.Block("staging") as _b:
            blk = BW(_b)
            @blk.sync
            def _(sync):
                for i, (src, dst, kind) in enumerate(jobs):
                    if i >= 4:
                        sync.wait_ge(h_sdve, v0 + (i - 4) + 1)
                    ld_sem[i % 4].inc(sync.dma_start(scf(ST_F32[i % 4], 0, 128, 1024), src))

            @blk.vector
            def _(v):
                for i, (src, dst, kind) in enumerate(jobs):
                    v.wait_ge(ld_h[i % 4], sl0[i % 4] + 16 * (i // 4 + 1))
                    if i >= 4:
                        v.wait_ge(st_h[i % 4], so0[i % 4] + 16 * ((i - 4) // 4 + 1))
                    v.tensor_copy(scb(ST_BF[i % 4], 0, 128, 1024),
                                  scf(ST_F32[i % 4], 0, 128, 1024))
                    sdve.inc(v)

            @blk.gpsimd
            def _(g):
                for i, (src, dst, kind) in enumerate(jobs):
                    g.wait_ge(h_sdve, v0 + i + 1)
                    sb = scb(ST_BF[i % 4], 0, 128, 1024)
                    if kind == "qkv":
                        st_sem[i % 4].inc(g.dma_start(dst.transpose([1, 0, 2]),
                                                      sb.rearrange("p (g w) -> p g w", w=128)))
                    else:
                        st_sem[i % 4].inc(g.dma_start(dst, sb))

    def emit_consts(self):
        nc = self.nc
        h_dmc, dmc, sdve = self.h_dmc, self.dmc, self.sdve
        scf, scb = self.scf, self.scb
        with nc.Block("consts") as _b:
            blk = BW(_b)
            @blk.sync
            def _(sync):
                sync.wait_ge(self.h_sdve, self.sdve.v)
                dmc.inc(sync.dma_start(
                    scf(0, 0, 128, 1024).rearrange("p (g w) -> p g w", w=128),
                    self.RF.transpose([1, 0, 2])))
                dmc.inc(sync.dma_start(self.sel2_f[:, :], self.SEL2))
                dmc.inc(sync.dma_start(scf(4096, 0, 128, 128), self.S128T))
                dmc.inc(sync.dma_start(self.bmu_sb[:, :], self.BMU))
                dmc.inc(sync.dma_start(scf(5120, 0, 128, 128), self.MASKD))

            @blk.vector
            def _(v):
                v.wait_ge(h_dmc, dmc.v)
                v.wait_ge(self.h_dmso[0], self.dmso[0].v)
                v.wait_ge(self.h_dmso[1], self.dmso[1].v)
                v.memset(self.cst[:, 0:1], EPS)
                v.memset(self.cst[:, 1:2], STAB)
                v.tensor_copy(self.rf_bf[:, :], scf(0, 0, 128, 1024))
                v.tensor_copy(self.s128t_bf[:, :], scf(4096, 0, 128, 128))
                v.tensor_copy(self.maskd_bf[:, :], scf(5120, 0, 128, 128))
                v.memset(self.ones1f[:, :], 1.0)
                v.memset(self.ones1b[:, :], 1.0)
                v.memset(self.onesc_bf[:, :], 1.0)
                v.memset(self.lnx.ap().rearrange("p (c t) -> p c t", t=T + 1)[:, :, 0:1], 0.0)
                sdve.inc(v)

    def emit_seq(self, s):
        nc = self.nc
        dmx, dmo = self.dmx, self.dmo
        h_dmo = self.h_dmo
        with nc.Block(f"s{s}_loadx") as _b:
            blk = BW(_b)
            @blk.sync
            def _(sync):
                if s > 0:
                    sync.wait_ge(h_dmo, dmo.v)
                for c in range(NCH):
                    dmx.inc(sync.dma_start(self.xch(c),
                                           self.X[s, c * 128:(c + 1) * 128, :]))
                self.dmk[0].inc(sync.dma_start(
                    self.wab(WA_WQKV[0], 0, 128, 3072).rearrange("p (c w) -> p c w", w=384),
                    self.WQKV_S[0].transpose([1, 0, 2])))
        self._xload = dmx.v
        self._aa_g0 = self.dmk[0].v

        if self.tap == 0:
            self.emit_tap(s)
            return
        self.emit_ln(s, 1)
        if self.tap == 5:
            self.emit_tap_lnx(s)
            return
        self.emit_attn_a(s)
        if self.tap == 1:
            self.emit_tap(s)
            return
        self.emit_ln(s, 2)
        self.emit_ffn(s, 1)
        if self.tap == 2:
            self.emit_tap(s)
            return
        self.emit_ln(s, 3)
        self.emit_attn_b(s)
        if self.tap == 3:
            self.emit_tap(s)
            return
        self.emit_ln(s, 4)
        self.emit_ffn(s, 2, store=True)
        if self.tap == 4:
            self.emit_tap(s)

    def emit_tap(self, s):
        if s != 0:
            return
        nc, dmo, h_dmo = self.nc, self.dmo, self.h_dmo
        h_dmx = self.h_dmx
        xload = self._xload
        with nc.Block("tap") as _b:
            blk = BW(_b)
            @blk.gpsimd
            def _(g):
                g.wait_ge(h_dmx, xload)
                for c in range(NCH):
                    dmo.inc(g.dma_start(self.TAPX[c * 128:(c + 1) * 128, :],
                                        self.xch(c)))
                g.wait_ge(h_dmo, dmo.v)

    def emit_tap_lnx(self, s):
        if s != 0:
            return
        nc, dmo, h_dmo, sdve, h_sdve = (self.nc, self.dmo, self.h_dmo,
                                        self.sdve, self.h_sdve)
        v0 = sdve.v
        o0 = self.dmo.v
        with nc.Block("taplnx") as _b:
            blk = BW(_b)
            @blk.vector
            def _(v):
                for c in range(NCH):
                    if c >= 2:
                        v.wait_ge(h_dmo, o0 + 16 * (c - 1))
                    v.tensor_copy(self.scf([0, 8192][c % 2], 0, 128, 2048),
                                  self.lnch(c))
                    sdve.inc(v)

            @blk.gpsimd
            def _(g):
                for c in range(NCH):
                    g.wait_ge(h_sdve, v0 + c + 1)
                    g.wait_ge(h_dmo, o0 + 16 * c)
                    dmo.inc(g.dma_start(self.TAPX[c * 128:(c + 1) * 128, :],
                                        self.scf([0, 8192][c % 2], 0, 128, 2048)))
                g.wait_ge(h_dmo, dmo.v)

    def emit_ln(self, s, which):
        nc = self.nc
        h_spool, spool = self.h_spool, self.spool
        LN_MEV, LN_REV = 24576, 26624
        h_spe, h_sact, h_sdve = self.h_spe, self.h_sact, self.h_sdve
        spe, sact, sdve = self.spe, self.sact, self.sdve
        scf, scb, xch, lnch = self.scf, self.scb, self.xch, self.lnch
        psA, psB, psC, psD, psW = self.psA, self.psB, self.psC, self.psD, self.psW
        onesc, ones1f = self.onesc_bf, self.ones1f
        h_dmx = self.h_dmx
        xload = self._xload
        a0, v0, p0 = sact.v, sdve.v, spe.v
        g0 = spool.v
        first = (which == 1)

        def bmean(t):
            return [psC[:, 0:512], psW[:, 0:512]][t % 2]

        def brstd(t):
            return [psD[:, 0:512], psW[:, 512:1024]][t % 2]

        with nc.Block(f"s{s}_ln{which}") as _b:
            blk = BW(_b)
            @blk.scalar
            def _(act):
                for t in range(NT):
                    for c in range(NCH):
                        bi = c
                        if first and t == 0 and c == 0:
                            act.wait_ge(h_dmx, xload)
                        act.copy(scb(LN_XB[bi], 0, 128, 512), xch(c, t * TW, TW))
                        sact.inc(act)
                    act.wait_ge(h_spe, p0 + t * 18 + 16)
                    act.activation(scf(LN_ROW0, 0, 1, 512), psA[0:1, 0:512],
                                   AF.Copy, scale=1.0 / 1024.0)
                    sact.inc(act)
                    act.wait_ge(h_sdve, v0 + t * LN_DVE_T + 9)
                    act.activation(scf(LN_ROW1, 0, 1, 512), scf(LN_ROW1, 0, 1, 512),
                                   AF.Ln, bias=EPS)
                    act.activation(scf(LN_ROW1, 0, 1, 512), scf(LN_ROW1, 0, 1, 512),
                                   AF.Exp, scale=-0.5)
                    sact.inc(act, 2)

            @blk.vector
            def _(v):
                for t in range(NT):
                    for c in range(NCH):
                        bi = c
                        v.wait_ge(h_sact, a0 + t * 11 + c + 1)
                        v.tensor_mul(scb(LN_SQ[bi], 0, 128, 512),
                                     scb(LN_XB[bi], 0, 128, 512),
                                     scb(LN_XB[bi], 0, 128, 512))
                        sdve.inc(v)
                    v.wait_ge(h_sact, a0 + t * 11 + 9)
                    v.scalar_tensor_tensor(scf(LN_ROW1, 0, 1, 512),
                                           scf(LN_ROW0, 0, 1, 512), 1.0,
                                           scf(LN_ROW0, 0, 1, 512), ALU.mult, ALU.mult)
                    v.wait_ge(h_spe, p0 + t * 18 + 16)
                    v.scalar_tensor_tensor(scf(LN_ROW1, 0, 1, 512), psB[0:1, 0:512],
                                           1.0 / 1024.0, scf(LN_ROW1, 0, 1, 512),
                                           ALU.mult, ALU.subtract)
                    sdve.inc(v)
                    for c in range(NCH):
                        ts = scf(LN_TS[c % 2], 0, 128, 512)
                        if c == 0:
                            v.wait_ge(h_spe, p0 + t * 18 + 17)
                        v.tensor_tensor(ts, xch(c, t * TW, TW), bmean(t),
                                        ALU.subtract)
                        if c == 0:
                            v.wait_ge(h_spe, p0 + t * 18 + 18)
                        v.tensor_tensor(lnch(c, t * TW, TW), ts, brstd(t),
                                        ALU.mult)
                        sdve.inc(v, 2)

            @blk.tensor
            def _(pe):
                for t in range(NT):
                    for c in range(NCH):
                        bi = c
                        pe.wait_ge(h_sact, a0 + t * 11 + c + 1)
                        pe.wait_ge(h_sdve, v0 + t * LN_DVE_T + c + 1)
                        if c == 0 and t > 0:
                            pe.wait_ge(h_sact, a0 + (t - 1) * 11 + 9)
                            pe.wait_ge(h_sdve, v0 + (t - 1) * 19 + 9)
                        pe.matmul(psA[0:1, 0:512], onesc[:, 0:1],
                                  scb(LN_XB[bi], 0, 128, 512),
                                  start=(c == 0), stop=(c == NCH - 1))
                        pe.matmul(psB[0:1, 0:512], onesc[:, 0:1],
                                  scb(LN_SQ[bi], 0, 128, 512),
                                  start=(c == 0), stop=(c == NCH - 1))
                        spe.inc(pe, 2)
                    pe.wait_ge(h_sact, a0 + t * 11 + 9)
                    if t >= 2:
                        pe.wait_ge(h_sdve, v0 + (t - 2) * LN_DVE_T + LN_DVE_T)
                        pass
                    pe.matmul(bmean(t), ones1f[:, :], scf(LN_ROW0, 0, 1, 512),
                              start=True, stop=True)
                    spe.inc(pe)
                    pe.wait_ge(h_sact, a0 + t * 11 + 11)
                    pe.matmul(brstd(t), ones1f[:, :], scf(LN_ROW1, 0, 1, 512),
                              start=True, stop=True)
                    spe.inc(pe)

        assert sact.v == a0 + NT * 11
        assert sdve.v == v0 + NT * LN_DVE_T
        assert spe.v == p0 + NT * 18
        
    def emit_attn_a(self, s):
        nc = self.nc
        h_spe, h_sact, h_sdve = self.h_spe, self.h_sact, self.h_sdve
        h_dmk, h_dmw2 = self.h_dmk, self.h_dmw2
        dmk, dmw2 = self.dmk, self.dmw2
        spe, sact, sdve = self.spe, self.sact, self.sdve
        scf, scb, wab, xch, lnch, bgch = (self.scf, self.scb, self.wab, self.xch,
                                          self.lnch, self.bgch)
        psA, psB, psC, psD, psW = self.psA, self.psB, self.psC, self.psD, self.psW
        rf_bf, sel2_f, bmu_sb = self.rf_bf, self.sel2_f, self.bmu_sb
        loads = {0: self._aa_g0}
        wout_cnt = [None]

        for gi in range(NG):
            wb = WA_WQKV[gi % 2]

            def wq(c, seg, wb=wb):
                return wab(wb, 0, 128, 3072).rearrange(
                    "p (c w) -> p c w", w=384)[:, c, seg * 128:(seg + 1) * 128]

            a0, v0, p0 = sact.v, sdve.v, spe.v
            dvo = 1 if gi == 0 else 0
            load_cnt = loads[gi]
            with nc.Block(f"s{s}_aa_g{gi}") as _b:
                blk = BW(_b)
                @blk.sync
                def _(sync):
                    if gi + 1 < NG:
                        dmk[(gi + 1) % 2].inc(sync.dma_start(
                            wab(WA_WQKV[(gi + 1) % 2], 0, 128, 3072)
                            .rearrange("p (c w) -> p c w", w=384),
                            self.WQKV_S[gi + 1].transpose([1, 0, 2])))
                        loads[gi + 1] = dmk[(gi + 1) % 2].v
                    else:
                        sync.wait_ge(h_spe, p0 + 60)
                        dmw2.inc(sync.dma_start(
                            wab(WA_WOUT, 0, 128, 8192).rearrange("p (c w) -> p c w", w=1024),
                            self.WOUTA_S.rearrange("(c p) w -> c p w", p=128)
                            .transpose([1, 0, 2])))
                        wout_cnt[0] = dmw2.v

                @blk.tensor
                def _(pe):
                    pe.wait_ge(h_dmk[gi % 2], load_cnt)
                    for t in range(NT):
                        pq = [psA, psC][t % 2]
                        pk = [psB, psD][t % 2]
                        if t >= 2:
                            pe.wait_ge(h_sact, a0 + (t - 2) * 2 + 2)
                        for c in range(NCH):
                            pe.matmul(pq[:, 0:512], wq(c, 0), lnch(c, t * TW, TW),
                                      start=(c == 0), stop=(c == NCH - 1))
                        spe.inc(pe)
                        for c in range(NCH):
                            pe.matmul(pk[:, 0:512], wq(c, 1), lnch(c, t * TW, TW),
                                      start=(c == 0), stop=(c == NCH - 1))
                        spe.inc(pe)
                    for t in range(NT):
                        pq = [psA, psC][t % 2]
                        wv = (t + 2) * 2 + 1 if t + 2 < NT else t * 2 + 1
                        pe.wait_ge(h_sact, a0 + wv)
                        if t >= 2:
                            pe.wait_ge(h_sdve, v0 + dvo + (t - 2) + 1)
                        pe.matmul(pq[:, 0:512], rf_bf[:, gi * 128:(gi + 1) * 128],
                                  scb(AA_QSC[t], 0, 128, 512), start=True, stop=True)
                        spe.inc(pe)
                    def kv_mm(j):
                        pe.wait_ge(h_sact, a0 + 8 + j * 2 + 2)
                        if gi == 0 and j < 2:
                            pe.wait_ge(h_sdve, v0 + 1)
                        pe.matmul(psW[:, 0:130], scb(AA_PHIK[j % 2], 0, 128, 128),
                                  scb(AA_VTK[j % 2], 0, 128, 130),
                                  start=(j == 0), stop=(j == 15))
                        spe.inc(pe)
                    for j in range(16):
                        t = j // 4
                        jj = j % 4
                        pe.wait_ge(h_sact, a0 + t * 2 + 2)
                        pv = [psA, psB][j % 2][:, 0:128]
                        pf = [psC, psD][j % 2][:, 0:128]
                        if j >= 2:
                            pe.wait_ge(h_sact, a0 + 8 + (j - 2) * 2 + 2)
                        elif j == 0:
                            pe.wait_ge(h_sdve, v0 + dvo + 3)
                        elif j == 1:
                            pe.wait_ge(h_sact, a0 + 6)
                        for c in range(NCH):
                            pe.matmul(pv, lnch(c, j * 128, 128), wq(c, 2),
                                      start=(c == 0), stop=(c == NCH - 1))
                        spe.inc(pe)
                        if j == 0:
                            pe.wait_ge(h_sdve, v0 + dvo + 4)
                        pe.matmul(pf, scb(AA_KSC[t], 0, 128, 512)[:, jj * 128:(jj + 1) * 128],
                                  rf_bf[:, gi * 128:(gi + 1) * 128], start=True, stop=True)
                        spe.inc(pe)
                        if j >= 1:
                            kv_mm(j - 1)
                    kv_mm(15)

                @blk.scalar
                def _(act):
                    for t in range(NT):
                        pq = [psA, psC][t % 2]
                        pk = [psB, psD][t % 2]
                        act.wait_ge(h_spe, p0 + t * 2 + 1)
                        act.copy(scb(AA_QSC[t], 0, 128, 512), pq[:, 0:512])
                        sact.inc(act)
                        act.wait_ge(h_spe, p0 + t * 2 + 2)
                        act.copy(scb(AA_KSC[t], 0, 128, 512), pk[:, 0:512])
                        sact.inc(act)
                    for j in range(16):
                        pv = [psA, psB][j % 2][:, 0:128]
                        pf = [psC, psD][j % 2][:, 0:128]
                        vj = 13 if j == 0 else 3 * j + 12
                        act.wait_ge(h_spe, p0 + vj)
                        act.copy(scb(AA_VTK[j % 2], 0, 128, 130)[:, 0:128], pv)
                        sact.inc(act)
                        act.wait_ge(h_spe, p0 + vj + 1)
                        act.activation(scb(AA_PHIK[j % 2], 0, 128, 128), pf, AF.Relu)
                        sact.inc(act)
                    act.wait_ge(h_spe, p0 + 12 + 48)
                    if gi == 0:
                        act.wait_ge(h_sdve, v0 + 1)
                    kvbd = scb(AA_KVBD, 0, 128, 1024)
                    ksum = scb(AA_KSUM, 0, 128, 16)
                    act.copy(kvbd[0:64, gi * 128: gi * 128 + 64], psW[0:64, 0:64])
                    act.copy(kvbd[64:128, gi * 128 + 64: gi * 128 + 128],
                             psW[64:128, 64:128])
                    act.copy(ksum[0:64, 2 * gi: 2 * gi + 1], psW[0:64, 128:129])
                    act.copy(ksum[64:128, 2 * gi + 1: 2 * gi + 2], psW[64:128, 128:129])
                    sact.inc(act, 4)

                @blk.vector
                def _(v):
                    if gi == 0:
                        v.memset(scb(AA_KVBD, 0, 128, 1024), 0.0)
                        v.memset(scb(AA_KSUM, 0, 128, 16), 0.0)
                        v.memset(scb(AA_VTK[0], 0, 128, 130)[:, 128:129], 1.0)
                        v.memset(scb(AA_VTK[1], 0, 128, 130)[:, 128:129], 1.0)
                        sdve.inc(v)
                    for t in range(NT):
                        pq = [psA, psC][t % 2]
                        v.wait_ge(h_spe, p0 + 8 + t + 1)
                        v.tensor_scalar_max(bgch(gi, t * TW, TW), pq[:, 0:512], 0.0)
                        sdve.inc(v)

            assert spe.v == p0 + 60
            assert sact.v == a0 + 44
            assert sdve.v == v0 + 4 + (1 if gi == 0 else 0)

        a0, v0, p0 = sact.v, sdve.v, spe.v
        with nc.Block(f"s{s}_aa_B") as _b:
            blk = BW(_b)
            def peV(k):
                t, gi = divmod(k, 8)
                return p0 + t * 32 + gi * 3

            def actV(k):
                t, gi = divmod(k, 8)
                return a0 + t * 32 + gi * 3

            @blk.tensor
            def _(pe):
                kvbd = scb(AA_KVBD, 0, 128, 1024)
                ksum = scb(AA_KSUM, 0, 128, 16)
                for t in range(NT):
                    for gi in range(NG):
                        k = t * 8 + gi
                        po = [psA, psC][gi % 2]
                        if k >= 2:
                            kk = k - 2
                            pe.wait_ge(h_sdve, v0 + (kk // 8) * 16 + (kk % 8) + 1)
                        pe.matmul(po[:, 0:512], kvbd[:, gi * 128:(gi + 1) * 128],
                                  bgch(gi, t * TW, TW), start=True, stop=True)
                        spe.inc(pe)
                        if k >= 1:
                            pe.wait_ge(h_sact, actV(k - 1) + 2)
                        pe.matmul(psB[0:2, 0:512], ksum[:, 2 * gi:2 * gi + 2],
                                  bgch(gi, t * TW, TW), start=True, stop=True)
                        spe.inc(pe)
                        pe.wait_ge(h_sact, actV(k) + 2)
                        pe.matmul(psD[:, 0:512], sel2_f[:, :], scf(AB_NRM, 0, 2, 512),
                                  start=True, stop=True)
                        spe.inc(pe)
                    pe.wait_ge(h_sdve, v0 + t * 16 + 8)
                    pe.wait_ge(h_dmw2, wout_cnt[0])
                    for d in range(NCH):
                        pw = psW[:, (d % 4) * 512:(d % 4) * 512 + 512]
                        k = t * 8 + d
                        if k >= 4:
                            kk = k - 4
                            pe.wait_ge(h_sdve, v0 + (kk // 8) * 16 + 8 + (kk % 8) + 1)
                        for c in range(NCH):
                            pe.matmul(pw, wab(WA_WOUT, 0, 128, 8192).rearrange(
                                "p (c w) -> p c w", w=1024)[:, c, d * 128:(d + 1) * 128],
                                scb(AB_ATT + c * 1024, 0, 128, 512),
                                start=(c == 0), stop=(c == NCH - 1))
                        spe.inc(pe)

            @blk.scalar
            def _(act):
                for t in range(NT):
                    for gi in range(NG):
                        k = t * 8 + gi
                        act.wait_ge(h_spe, peV(k) + 2)
                        act.activation(scf(AB_NRM, 0, 2, 512), psB[0:2, 0:512],
                                       AF.Ln, bias=STAB)
                        act.activation(scf(AB_NRM, 0, 2, 512), scf(AB_NRM, 0, 2, 512),
                                       AF.Exp, scale=-1.0)
                        sact.inc(act, 2)
                        act.wait_ge(h_spe, peV(k) + 3)
                        if k >= 2:
                            kk = k - 2
                            act.wait_ge(h_sdve, v0 + (kk // 8) * 16 + (kk % 8) + 1)
                        act.copy(scf(AB_TSB[gi % 2], 0, 128, 512), psD[:, 0:512])
                        sact.inc(act)
                    for d in range(NCH):
                        k = t * 8 + d
                        pw = psW[:, (d % 4) * 512:(d % 4) * 512 + 512]
                        act.wait_ge(h_spe, p0 + t * 32 + 24 + d + 1)
                        if k >= 2:
                            kk = k - 2
                            act.wait_ge(h_sdve, v0 + (kk // 8) * 16 + 8 + (kk % 8) + 1)
                        act.activation(scf(AB_TSB2[d % 2], 0, 128, 512), pw, AF.Identity,
                                       bias=bmu_sb[:, d:d + 1])
                        sact.inc(act)

            @blk.vector
            def _(v):
                for t in range(NT):
                    for gi in range(NG):
                        k = t * 8 + gi
                        po = [psA, psC][gi % 2]
                        v.wait_ge(h_sact, actV(k) + 3)
                        if t >= 1 and gi == 0:
                            v.wait_ge(h_spe, p0 + t * 32)
                        v.tensor_tensor(scb(AB_ATT + gi * 1024, 0, 128, 512),
                                        po[:, 0:512], scf(AB_TSB[gi % 2], 0, 128, 512),
                                        ALU.mult)
                        sdve.inc(v)
                    for d in range(NCH):
                        v.wait_ge(h_sact, a0 + t * 32 + 24 + d + 1)
                        v.tensor_tensor(xch(d, t * TW, TW), xch(d, t * TW, TW),
                                        scf(AB_TSB2[d % 2], 0, 128, 512), ALU.add)
                        sdve.inc(v)

        assert spe.v == p0 + NT * 32
        assert sact.v == a0 + NT * 32
        assert sdve.v == v0 + NT * 16

    def emit_ffn(self, s, which, store=False):
        nc = self.nc
        h_spe, h_sact, h_sdve = self.h_spe, self.h_sact, self.h_sdve
        h_dmf, dmf = self.h_dmf, self.dmf
        dmo, spe, sact, sdve = self.dmo, self.spe, self.sact, self.sdve
        scf, scb, wab, xch, lnch = self.scf, self.scb, self.wab, self.xch, self.lnch
        psA, psB, psC, psD = self.psA, self.psB, self.psC, self.psD
        bmu_sb, Y = self.bmu_sb, self.Y
        h_dmo = self.h_dmo
        MU_R = bmu_sb[:, 16:24]
        MU_K = bmu_sb[:, 24:32]

        def wsl(WS, d):
            return WS.rearrange("(c p) w -> c p w", p=128)[:, :, d * 128:(d + 1) * 128]

        A0, V0, P0 = sact.v, sdve.v, spe.v
        F0 = [dmf[b].v for b in range(4)]
        wkq = [dmf[0], dmf[1], self.dmk[0], self.dmk[1]]
        h_wkq = [h_dmf[0], h_dmf[1], self.h_dmk[0], self.h_dmk[1]]
        WK0 = [q.v for q in wkq]

        def pb(t):
            return P0 + 24 * t

        def ab(t):
            return A0 + 24 * t

        _SXK = [0, 24, 64, 104]
        _SXR = [16, 56, 96, 136]
        _PB = [40, 80, 120, 144]

        def vSk(t):
            return V0 + _SXK[t]

        def vSr(t):
            return V0 + _SXR[t]

        def vP(t):
            return V0 + _PB[t]

        def fb(b, t):
            return F0[b] + 8 * 16 * t

        def fbk(b, t):
            return WK0[b] + 2 * 16 * t

        with nc.Block(f"s{s}_ffn{which}") as _b:
            blk = BW(_b)

            def wk(b, c):
                return wab(WF_K[b], 0, 128, 1024).rearrange("p (c w) -> p c w", w=128)[:, c, :]

            def wr(b, c):
                return wab(WF_R[b], 0, 128, 1024).rearrange("p (c w) -> p c w", w=128)[:, c, :]

            def wv(b, c):
                return wab(WF_V[b], 0, 128, 1024).rearrange("p (c w) -> p c w", w=128)[:, c, :]

            @blk.sync
            def _(sync):
                for t in range(NT):
                    for d in range(NCH):
                        if d >= 4:
                            sync.wait_ge(h_spe, pb(t) + (d - 4) + 1)
                        elif t > 0:
                            sync.wait_ge(h_spe, pb(t - 1) + (d + 4) + 1)
                        wkq[d % 4].inc(sync.dma_start(
                            wab(WF_K[d % 4], 0, 128, 1024)
                            .rearrange("p (c w) -> p c w", w=128),
                            wsl(self.WKT_S, d).transpose([1, 0, 2])))
                    for d in range(NCH):
                        if d >= 2:
                            sync.wait_ge(h_spe, pb(t) + 8 + (d - 2) * 2 + 2)
                        elif t > 0:
                            sync.wait_ge(h_spe, pb(t - 1) + 8 + (d + 6) * 2 + 2)
                        dmf[2 + d % 2].inc(sync.dma_start(
                            wab(WF_R[d % 2], 0, 128, 1024)
                            .rearrange("p (c w) -> p c w", w=128),
                            wsl(self.WRT_S, d).transpose([1, 0, 2])))
                        dmf[2 + d % 2].inc(sync.dma_start(
                            wab(WF_V[d % 2], 0, 128, 1024)
                            .rearrange("p (c w) -> p c w", w=128),
                            wsl(self.WVT_S, d).transpose([1, 0, 2])))

            @blk.vector
            def _(v):
                def s_xk(t):
                    for c in range(NCH):
                        dd = scb(FF_DD[c % 2], 0, 128, 512)
                        if t > 0 and c == 0:
                            v.wait_ge(h_spe, pb(t - 1) + 8)
                        v.tensor_sub(dd, lnch(c, t * TW, TW), lnch(c, t * TW, TW, shift=1))
                        v.scalar_tensor_tensor(scb(FF_XK + c * 1024, 0, 128, 512), dd,
                                               MU_K[:, c:c + 1],
                                               lnch(c, t * TW, TW, shift=1),
                                               ALU.mult, ALU.add)
                        sdve.inc(v, 2)

                def s_xr(t):
                    for c in range(NCH):
                        dd = scb(FF_DD[c % 2], 0, 128, 512)
                        if t > 0 and c == 0:
                            v.wait_ge(h_spe, pb(t - 1) + 8 + 2 * 7 + 1)
                        v.tensor_sub(dd, lnch(c, t * TW, TW), lnch(c, t * TW, TW, shift=1))
                        v.scalar_tensor_tensor(scb(FF_XR + c * 1024, 0, 128, 512), dd,
                                               MU_R[:, c:c + 1],
                                               lnch(c, t * TW, TW, shift=1),
                                               ALU.mult, ALU.add)
                        sdve.inc(v)

                def pass2(t):
                    for d in range(NCH):
                        pv = [psC, psD][d % 2]
                        v.wait_ge(h_spe, pb(t) + 8 + d * 2 + 2)
                        v.wait_ge(h_sact, ab(t) + 16 + d + 1)
                        tmp = scf(FF_TMP[d % 2], 0, 128, 512)
                        v.tensor_tensor(tmp, scf(FF_RSB[d % 2], 0, 128, 512),
                                        pv[:, 0:512], ALU.mult)
                        v.tensor_tensor(xch(d, t * TW, TW), xch(d, t * TW, TW),
                                        tmp, ALU.add)
                        sdve.inc(v, 2)
                s_xk(0); s_xr(0); s_xk(1)
                pass2(0); s_xr(1); s_xk(2)
                pass2(1); s_xr(2); s_xk(3)
                pass2(2); s_xr(3)
                pass2(3)

            @blk.tensor
            def _(pe):
                for t in range(NT):
                    for d in range(NCH):
                        pk = [psA, psB][d % 2]
                        pe.wait_ge(h_wkq[d % 4], fbk(d % 4, t) + 16 * (d // 4 + 1))
                        if d >= 2:
                            pe.wait_ge(h_sact, ab(t) + 2 * (d - 2) + 1)
                        elif t > 0:
                            pe.wait_ge(h_sact, ab(t - 1) + 16 + (d + 6) + 1)
                        for c in range(NCH):
                            if d == 0:
                                pe.wait_ge(h_sdve, vSk(t) + 2 * (c + 1))
                            pe.matmul(pk[:, 0:512], wk(d % 4, c),
                                      scb(FF_XK + c * 1024, 0, 128, 512),
                                      start=(c == 0), stop=(c == NCH - 1))
                        spe.inc(pe)
                    for d in range(NCH):
                        pr = [psA, psB][d % 2]
                        pv = [psC, psD][d % 2]
                        pe.wait_ge(h_dmf[2 + d % 2], fb(2 + d % 2, t) + 32 * (d // 2 + 1))
                        if d >= 2:
                            pe.wait_ge(h_sact, ab(t) + 16 + (d - 2) + 1)
                        else:
                            pe.wait_ge(h_sact, ab(t) + 2 * (6 + d) + 2)
                        for c in range(NCH):
                            if d == 0:
                                pe.wait_ge(h_sdve, vSr(t) + c + 1)
                            pe.matmul(pr[:, 0:512], wr(d % 2, c),
                                      scb(FF_XR + c * 1024, 0, 128, 512),
                                      start=(c == 0), stop=(c == NCH - 1))
                        spe.inc(pe)
                        pe.wait_ge(h_sact, ab(t) + 16)
                        if d >= 2:
                            pe.wait_ge(h_sdve, vP(t) + (d - 2) * 2 + 1)
                        elif t > 0:
                            pe.wait_ge(h_sdve, vP(t - 1) + (d + 6) * 2 + 1)
                        for c in range(NCH):
                            pe.matmul(pv[:, 0:512], wv(d % 2, c),
                                      scb(FF_KACT + c * 1024, 0, 128, 512),
                                      start=(c == 0), stop=(c == NCH - 1))
                        spe.inc(pe)

            @blk.scalar
            def _(act):
                for t in range(NT):
                    for d in range(NCH):
                        pk = [psA, psB][d % 2]
                        act.wait_ge(h_spe, pb(t) + d + 1)
                        if t > 0 and d < 2:
                            act.wait_ge(h_spe, pb(t - 1) + 8 + 2 * 7 + 2)
                        ka = scb(FF_KACT + d * 1024, 0, 128, 512)
                        act.activation(ka, pk[:, 0:512], AF.Relu)
                        sact.inc(act)
                        act.activation(ka, ka, AF.Square)
                        sact.inc(act)
                    for d in range(NCH):
                        pr = [psA, psB][d % 2]
                        act.wait_ge(h_spe, pb(t) + 8 + d * 2 + 1)
                        if d >= 2:
                            act.wait_ge(h_sdve, vP(t) + (d - 2) * 2 + 1)
                        elif t > 0:
                            act.wait_ge(h_sdve, vP(t - 1) + (d + 6) * 2 + 1)
                        act.activation(scf(FF_RSB[d % 2], 0, 128, 512), pr[:, 0:512],
                                       AF.Sigmoid)
                        sact.inc(act)

            if store:
                @blk.gpsimd
                def _(gp):
                    for t in range(NT):
                        for c in range(NCH):
                            gp.wait_ge(h_sdve, vP(t) + c * 2 + 2)
                            dmo.inc(gp.dma_start(
                                Y[s, c * 128:(c + 1) * 128, t * TW:(t + 1) * TW],
                                xch(c, t * TW, TW)))

        assert spe.v == P0 + NT * 24
        assert sact.v == A0 + NT * 24
        assert sdve.v == V0 + NT * 40

    def emit_attn_b(self, s):
        nc = self.nc
        h_dmo, h_spe, h_sact, h_sdve, h_spl = (self.h_dmo, self.h_spe,
                                               self.h_sact, self.h_sdve,
                                               self.h_spl)
        h_dmc, h_dmw2 = self.h_dmc, self.h_dmw2
        dmc, dmw2 = self.dmc, self.dmw2
        dmo, spe, sact, sdve, spl = (self.dmo, self.spe, self.sact,
                                     self.sdve, self.spl)
        scf, scb, wab, waf, xch, lnch, bgch = (self.scf, self.scb, self.wab,
                                               self.waf, self.xch, self.lnch,
                                               self.bgch)
        psA, psB, psC, psD, psW = self.psA, self.psB, self.psC, self.psD, self.psW
        s128t_bf, ones1b, bmu_sb = self.s128t_bf, self.ones1b, self.bmu_sb

        with nc.Block(f"s{s}_bp_w") as _b:
            blk = BW(_b)
            @blk.sync
            def _(sync):
                dmc.inc(sync.dma_start(
                    scf(0, 0, 128, 512).rearrange("p (c w) -> p c w", w=64),
                    self.WQA.rearrange("(c p) w -> p c w", p=128)))
                dmc.inc(sync.dma_start(scf(2048, 0, 64, 1024), self.WQB))
                dmc.inc(sync.dma_start(scf(6144, 0, 32, 512), self.WKB))
                dmc.inc(sync.dma_start(scf(8192, 0, 32, 256), self.WVB))
                dmc.inc(sync.dma_start(
                    scf(9216, 0, 128, 256).rearrange("p (c w) -> p c w", w=32),
                    self.WKVA.rearrange("(c p) w -> p c w", p=128)))

            @blk.vector
            def _(v):
                v.wait_ge(h_dmc, dmc.v)
                v.tensor_copy(wab(WB_WQA, 0, 128, 512), scf(0, 0, 128, 512))
                v.tensor_copy(wab(WB_WQB, 0, 64, 1024), scf(2048, 0, 64, 1024))
                v.tensor_copy(wab(WB_WKB, 0, 32, 512), scf(6144, 0, 32, 512))
                v.tensor_copy(wab(WB_WVB, 0, 32, 256), scf(8192, 0, 32, 256))
                v.tensor_copy(wab(WB_WKVA, 0, 128, 256), scf(9216, 0, 128, 256))
                sdve.inc(v)

        for NAME, DST in ((self.COSM, BP_COS), (self.SINM, BP_SIN)):
            with nc.Block(f"s{s}_bp_rope{DST}") as _b:
                blk = BW(_b)
                @blk.sync
                def _(sync):
                    sync.wait_ge(h_sdve, sdve.v)
                    dmc.inc(sync.dma_start(scf(BP_STAGE, 0, 128, 2048), NAME))

                @blk.vector
                def _(v):
                    v.wait_ge(h_dmc, dmc.v)
                    v.tensor_copy(scb(DST, 0, 128, 2048), scf(BP_STAGE, 0, 128, 2048))
                    sdve.inc(v)

        a0, p0 = sact.v, spe.v
        with nc.Block(f"s{s}_bp_lat") as _b:
            blk = BW(_b)
            @blk.tensor
            def _(pe):
                for t in range(NT):
                    pq = [psA, psB][t % 2]
                    pk = [psC, psD][t % 2]
                    if t >= 2:
                        pe.wait_ge(h_sact, a0 + (t - 2) * 2 + 2)
                    for c in range(NCH):
                        pe.matmul(pq[0:64, 0:512],
                                  wab(WB_WQA, 0, 128, 512)
                                  .rearrange("p (c w) -> p c w", w=64)[:, c, :],
                                  lnch(c, t * TW, TW), start=(c == 0), stop=(c == NCH - 1))
                    spe.inc(pe)
                    for c in range(NCH):
                        pe.matmul(pk[0:32, 0:512],
                                  wab(WB_WKVA, 0, 128, 256)
                                  .rearrange("p (c w) -> p c w", w=32)[:, c, :],
                                  lnch(c, t * TW, TW), start=(c == 0), stop=(c == NCH - 1))
                    spe.inc(pe)

            @blk.scalar
            def _(act):
                for t in range(NT):
                    pq = [psA, psB][t % 2]
                    pk = [psC, psD][t % 2]
                    act.wait_ge(h_spe, p0 + t * 2 + 1)
                    act.copy(scb(BP_LQ, 0, 64, 2048)[:, t * TW:(t + 1) * TW],
                             pq[0:64, 0:512])
                    sact.inc(act)
                    act.wait_ge(h_spe, p0 + t * 2 + 2)
                    act.copy(scb(BP_LKV, 0, 32, 2048)[:, t * TW:(t + 1) * TW],
                             pk[0:32, 0:512])
                    sact.inc(act)

        assert spe.v == p0 + 8 and sact.v == a0 + 8

        for phase, nchunks in (("q", NCH), ("k", 4)):
            a0, v0, p0 = sact.v, sdve.v, spe.v
            with nc.Block(f"s{s}_bp_{phase}") as _b:
                blk = BW(_b)
                @blk.tensor
                def _(pe):
                    n_i = nchunks * NT

                    def qs_mm(i):
                        ps_ = [psC, psD][i % 2]
                        pe.wait_ge(h_sact, a0 + i + 1)
                        if i >= 2:
                            pe.wait_ge(h_sdve, v0 + (i - 2) * 2 + 2)
                        pe.matmul(ps_[:, 0:512], s128t_bf[:, :],
                                  wab(WB_QSBT[i % 2], 0, 128, 512),
                                  start=True, stop=True)
                        spe.inc(pe)
                    for c in range(nchunks):
                        for t in range(NT):
                            i = c * NT + t
                            pq = [psA, psB][i % 2]
                            if i >= 2:
                                pe.wait_ge(h_sact, a0 + (i - 2) + 1)
                            if phase == "q":
                                pe.matmul(pq[:, 0:512],
                                          wab(WB_WQB, 0, 64, 1024)[:, c * 128:(c + 1) * 128],
                                          scb(BP_LQ, 0, 64, 2048)[:, t * TW:(t + 1) * TW],
                                          start=True, stop=True)
                            else:
                                pe.matmul(pq[:, 0:512],
                                          wab(WB_WKB, 0, 32, 512)[:, c * 128:(c + 1) * 128],
                                          scb(BP_LKV, 0, 32, 2048)[:, t * TW:(t + 1) * TW],
                                          start=True, stop=True)
                            spe.inc(pe)
                            if i >= 1:
                                qs_mm(i - 1)
                    qs_mm(n_i - 1)

                @blk.scalar
                def _(act):
                    for c in range(nchunks):
                        for t in range(NT):
                            i = c * NT + t
                            pq = [psA, psB][i % 2]
                            act.wait_ge(h_spe, p0 + (1 if i == 0 else 2 * i))
                            if i >= 2:
                                act.wait_ge(h_sdve, v0 + (i - 2) * 2 + 2)
                            act.copy(wab(WB_QSBT[i % 2], 0, 128, 512), pq[:, 0:512])
                            sact.inc(act)

                @blk.vector
                def _(v):
                    for c in range(nchunks):
                        for t in range(NT):
                            i = c * NT + t
                            n_i = nchunks * NT
                            ps_ = [psC, psD][i % 2]
                            v.wait_ge(h_spe, p0 + (2 * i + 3 if i < n_i - 1 else 2 * n_i))
                            t1 = wab(WB_RT1, 0, 128, 512)
                            t2 = wab(WB_RT2, 0, 128, 512)
                            v.tensor_tensor(t1, ps_[:, 0:512],
                                            scb(BP_SIN, 0, 128, 2048)[:, t * TW:(t + 1) * TW],
                                            ALU.mult)
                            v.tensor_tensor(t2, wab(WB_QSBT[i % 2], 0, 128, 512),
                                            scb(BP_COS, 0, 128, 2048)[:, t * TW:(t + 1) * TW],
                                            ALU.mult)
                            if phase == "q":
                                dst = bgch(c, t * TW, TW)
                            else:
                                dst = scb(BP_KREP + c * 4096, 0, 128, 2048)[:, t * TW:(t + 1) * TW]
                            v.tensor_tensor(dst, t2, t1, ALU.add)
                            sdve.inc(v, 2)

            assert spe.v == p0 + nchunks * NT * 2
            assert sact.v == a0 + nchunks * NT
            assert sdve.v == v0 + nchunks * NT * 2

        a0, v0, p0 = sact.v, sdve.v, spe.v
        with nc.Block(f"s{s}_bp_v") as _b:
            blk = BW(_b)
            @blk.vector
            def _(v):
                vt = scb(BS_VTOK, 0, 128, 16 * 260)
                v.memset(vt.rearrange("p (j g w) -> p j g w", g=4, w=65)[:, :, :, 64:65],
                         1.0)
                sdve.inc(v)

            @blk.tensor
            def _(pe):
                for j in range(16):
                    pv = [psA, psB][j % 2]
                    if j >= 2:
                        pe.wait_ge(h_sact, a0 + (j - 2) + 1)
                    pe.matmul(pv[:, 0:256],
                              scb(BP_LKV, 0, 32, 2048)[:, j * 128:(j + 1) * 128],
                              wab(WB_WVB, 0, 32, 256), start=True, stop=True)
                    spe.inc(pe)

            @blk.scalar
            def _(act):
                vt = scb(BS_VTOK, 0, 128, 16 * 260)
                for j in range(16):
                    pv = [psA, psB][j % 2]
                    act.wait_ge(h_spe, p0 + j + 1)
                    act.copy(vt[:, j * 260: j * 260 + 260]
                             .rearrange("p (g w) -> p g w", w=65)[:, :, 0:64],
                             pv[:, 0:256].rearrange("p (g w) -> p g w", w=64))
                    sact.inc(act)

        assert spe.v == p0 + 16 and sact.v == a0 + 16

        njs = [4 * i + 4 for i in range(NT)]
        a0, v0, p0, l0 = sact.v, sdve.v, spe.v, spl.v

        ops = []
        pend_pv = None
        pend_bc = None
        for pi in range(NPAIR):
            for i in range(NT):
                for j in range(njs[i]):
                    ops.append(("scA", pi, i, j))
                    ops.append(("scB", pi, i, j))
                    if pend_bc is not None and j == 1:
                        ops.append(("bcA",) + pend_bc)
                        ops.append(("bcB",) + pend_bc)
                        pend_bc = None
                    if pend_pv is not None:
                        ops.append(("pvA",) + pend_pv)
                        ops.append(("pvB",) + pend_pv)
                    pend_pv = (pi, i, j)
                ops.append(("pvA",) + pend_pv)
                ops.append(("pvB",) + pend_pv)
                pend_pv = None
                assert pend_bc is None
                pend_bc = (pi, i)
        ops.append(("bcA",) + pend_bc)
        ops.append(("bcB",) + pend_bc)
        val = {op: k + 1 for k, op in enumerate(ops)}

        def actb(p, i):
            return a0 + p * sum(2 * n + 4 for n in njs) + sum(2 * njs[k] + 4 for k in range(i))

        def dveb(p, i):
            return v0 + p * NT * 20 + 20 * i

        def splb(p, i):
            return l0 + 16 * (p * NT + i)

        def pv_done(pi, i, j):
            return p0 + val[("pvB", pi, i, j)]

        def prev_pv_all(pi, i):
            if i > 0:
                return pv_done(pi, i - 1, njs[i - 1] - 1)
            if pi > 0:
                return pv_done(pi - 1, NT - 1, njs[NT - 1] - 1)
            return 0

        PAb = [BS_PA0, BS_PA0 + 1024, BS_PA0 + 2048]
        PBb = [BS_PB0, BS_PB0 + 1024, BS_PB0 + 2048]
        ohsbB = self.ohsb_t[:, :]
        bw_ld = [None]

        with nc.Block(f"s{s}_bs") as _b:
            blk = BW(_b)

            @blk.sync
            def _(sync):
                dmw2.inc(sync.dma_start(
                    wab(WA_WOUT, 0, 128, 8192).rearrange("p (c w) -> p c w", w=1024),
                    self.WOUTB_S.rearrange("(c p) w -> c p w", p=128).transpose([1, 0, 2])))
                bw_ld[0] = dmw2.v

            @blk.tensor
            def _(pe):
                for op in ops:
                    kind, pi = op[0], op[1]
                    gkv = pi // 2
                    if kind in ("scA", "scB"):
                        _, pi, i, j = op
                        q0 = max(128 * j, TW * i)
                        w = TW * (i + 1) - q0
                        ps_ = [[psA, psC], [psB, psD]][kind == "scB"][j % 2]
                        if kind == "scA":
                            if j >= 2:
                                pe.wait_ge(h_sact, actb(pi, i) + 2 * (j - 2) + 2)
                            elif i > 0:
                                pe.wait_ge(h_sact, actb(pi, i - 1) + 2 * njs[i - 1])
                            elif pi > 0:
                                pe.wait_ge(h_sact, actb(pi - 1, NT - 1) + 2 * njs[NT - 1])
                        base = 0 if kind == "scA" else 64
                        pe.matmul(ps_[:, 0:w],
                                  scb(BP_KREP + gkv * 4096, 0, 128, 2048)
                                  [base:base + 64, j * 128:(j + 1) * 128],
                                  bgch(pi, q0, w, p0=base, np_=64),
                                  start=True, stop=True)
                        spe.inc(pe)
                    elif kind in ("pvA", "pvB"):
                        _, pi, i, j = op
                        if kind == "pvA":
                            pe.wait_ge(h_sact, actb(pi, i) + 2 * j + 2)
                            if j >= 4 * i:
                                jj = j - 4 * i
                                pe.wait_ge(h_sdve, dveb(pi, i) + 4 * jj + 4)
                            if j == 0:
                                if i > 0 or pi > 0:
                                    pe.wait_ge(h_sdve, dveb(pi, i))
                                    pe.wait_ge(h_spl, splb(pi, i))
                        vsl = scb(BS_VTOK, 0, 128, 16 * 260)[
                            :, j * 260 + gkv * 65: j * 260 + gkv * 65 + 65]
                        pbuf = scb((PAb if kind == "pvA" else PBb)[j % 3], 0, 128, 512)
                        col = 0 if kind == "pvA" else 512
                        pe.matmul(psW[0:65, col:col + 512], vsl, pbuf,
                                  start=(j == 0), stop=(j == njs[i] - 1))
                        spe.inc(pe)
                    elif kind == "bcA":
                        _, pi, i = op
                        pe.wait_ge(h_sact, actb(pi, i) + 2 * njs[i] + 2)
                        pe.wait_ge(h_sdve, dveb(pi, i))
                        pe.matmul(psW[0:64, 1024:1536], ones1b[:, :],
                                  self.recs_bf[:, 0:512], start=True, stop=True)
                        spe.inc(pe)
                    else:
                        _, pi, i = op
                        pe.wait_ge(h_sact, actb(pi, i) + 2 * njs[i] + 4)
                        pe.matmul(psW[64:128, 1536:2048], ones1b[:, :],
                                  self.recs_bf[:, 512:1024], start=True, stop=True,
                                  tile_position=(0, 64))
                        spe.inc(pe)

            @blk.scalar
            def _(act):
                for pi in range(NPAIR):
                    for i in range(NT):
                        for j in range(njs[i]):
                            q0 = max(128 * j, TW * i)
                            w = TW * (i + 1) - q0
                            poff = q0 - TW * i
                            psSA = [psA, psC][j % 2]
                            psSB = [psB, psD][j % 2]
                            pA = scb(PAb[j % 3], 0, 128, 512)
                            pB = scb(PBb[j % 3], 0, 128, 512)
                            act.wait_ge(h_spe, p0 + val[("scB", pi, i, j)])
                            if j >= 3:
                                act.wait_ge(h_spe, pv_done(pi, i, j - 3))
                            else:
                                pw = prev_pv_all(pi, i)
                                if pw:
                                    act.wait_ge(h_spe, pw)
                            act.activation(pA[:, poff:poff + w], psSA[:, 0:w],
                                           AF.Exp, scale=0.125)
                            act.activation(pB[:, poff:poff + w], psSB[:, 0:w],
                                           AF.Exp, scale=0.125)
                            sact.inc(act, 2)
                        act.wait_ge(h_spe, pv_done(pi, i, njs[i] - 1))
                        act.activation(self.drow_f[:, :], psW[64:65, 0:512], AF.Ln)
                        act.activation(self.recs_bf[:, 0:512], self.drow_f[:, :],
                                       AF.Exp, scale=-1.0)
                        sact.inc(act, 2)
                        act.activation(self.drow_f[:, :], psW[64:65, 512:1024], AF.Ln)
                        act.activation(self.recs_bf[:, 512:1024], self.drow_f[:, :],
                                       AF.Exp, scale=-1.0)
                        sact.inc(act, 2)

            @blk.gpsimd
            def _(gp):
                for pi in range(NPAIR):
                    for i in range(NT):
                        gp.wait_ge(h_sdve, dveb(pi, i) + 18)
                        spl.inc(gp.dma_start(
                            self.lnx[64:128,
                                     pi * (T + 1) + 1 + i * TW: pi * (T + 1) + 1 + (i + 1) * TW],
                            ohsbB), 16)

            @blk.vector
            def _(v):
                for pi in range(NPAIR):
                    for i in range(NT):
                        for jj in range(4):
                            j = 4 * i + jj
                            q0 = 128 * j
                            poff = q0 - TW * i
                            pA = scb(PAb[j % 3], 0, 128, 512)
                            pB = scb(PBb[j % 3], 0, 128, 512)
                            if poff > 0:
                                if j >= 3:
                                    v.wait_ge(h_spe, pv_done(pi, i, j - 3))
                                v.memset(pA[:, 0:poff], 0.0)
                                v.memset(pB[:, 0:poff], 0.0)
                                sdve.inc(v, 2)
                            else:
                                sdve.pad(v, 2)
                            v.wait_ge(h_sact, actb(pi, i) + 2 * j + 2)
                            v.tensor_mul(pA[:, poff:poff + 128],
                                         pA[:, poff:poff + 128],
                                         self.maskd_bf[:, :])
                            v.tensor_mul(pB[:, poff:poff + 128],
                                         pB[:, poff:poff + 128],
                                         self.maskd_bf[:, :])
                            sdve.inc(v, 2)
                        v.wait_ge(h_spe, pv_done(pi, i, njs[i] - 1))
                        v.tensor_copy(scb(BS_OHSBA, 0, 128, 512)[0:64, :],
                                      psW[0:64, 0:512])
                        sdve.inc(v)
                        v.wait_ge(h_spl, splb(pi, i))
                        v.tensor_copy(ohsbB, psW[0:64, 512:1024])
                        sdve.inc(v)
                        v.wait_ge(h_spe, p0 + val[("bcA", pi, i)])
                        v.tensor_tensor(
                            self.lnx[0:64, pi * (T + 1) + 1 + i * TW:
                                     pi * (T + 1) + 1 + (i + 1) * TW],
                            scb(BS_OHSBA, 0, 128, 512)[0:64, :],
                            psW[0:64, 1024:1536], ALU.mult)
                        sdve.inc(v)
                        v.wait_ge(h_spe, p0 + val[("bcB", pi, i)])
                        v.wait_ge(h_spl, splb(pi, i) + 16)
                        ln_b = self.lnx[64:128, pi * (T + 1) + 1 + i * TW:
                                        pi * (T + 1) + 1 + (i + 1) * TW]
                        v.tensor_tensor(ln_b, ln_b, psW[64:128, 1536:2048], ALU.mult)
                        sdve.inc(v)

        assert spe.v == p0 + len(ops)
        assert sact.v == a0 + NPAIR * sum(2 * n + 4 for n in njs)
        assert sdve.v == v0 + NPAIR * NT * 20
        assert spl.v == l0 + 16 * NPAIR * NT

        a0, v0, p0 = sact.v, sdve.v, spe.v
        with nc.Block(f"s{s}_bw") as _b:
            blk = BW(_b)

            @blk.tensor
            def _(pe):
                pe.wait_ge(h_dmw2, bw_ld[0])
                for t in range(NT):
                    for d in range(NCH):
                        i = t * NCH + d
                        pw = [psA, psB, psC, psD][i % 4]
                        if i >= 4:
                            pe.wait_ge(h_sdve, v0 + (i - 4) + 1)
                        for c in range(NCH):
                            pe.matmul(pw[:, 0:512],
                                      wab(WA_WOUT, 0, 128, 8192).rearrange(
                                          "p (c w) -> p c w", w=1024)[:, c, d * 128:(d + 1) * 128],
                                      lnch(c, t * TW, TW), start=(c == 0),
                                      stop=(c == NCH - 1))
                        spe.inc(pe)

            @blk.scalar
            def _(act):
                for t in range(NT):
                    for d in range(NCH):
                        i = t * NCH + d
                        pw = [psA, psB, psC, psD][i % 4]
                        act.wait_ge(h_spe, p0 + i + 1)
                        if i >= 2:
                            act.wait_ge(h_sdve, v0 + (i - 2) + 1)
                        act.activation(scf(BW_TSB2[i % 2], 0, 128, 512), pw[:, 0:512],
                                       AF.Identity, bias=bmu_sb[:, 8 + d: 8 + d + 1])
                        sact.inc(act)

            @blk.vector
            def _(v):
                for t in range(NT):
                    for d in range(NCH):
                        i = t * NCH + d
                        v.wait_ge(h_sact, a0 + i + 1)
                        v.tensor_tensor(xch(d, t * TW, TW), xch(d, t * TW, TW),
                                        scf(BW_TSB2[i % 2], 0, 128, 512), ALU.add)
                        sdve.inc(v)

        assert spe.v == p0 + 32 and sact.v == a0 + 32 and sdve.v == v0 + 32



_CACHE = {}


def _host_prep(inputs):
    x = np.ascontiguousarray(np.asarray(inputs["L0_attn_a_x"]).transpose(0, 2, 1))
    wqkv = np.ascontiguousarray(np.asarray(inputs["L0_attn_a_W_qkv"], np.float32))
    rf = np.asarray(inputs["L0_attn_a_RF"], np.float32)
    rfp = np.zeros((8, 128, 128), np.float32)
    for g in range(8):
        rfp[g, 0:64, 0:64] = rf[2 * g]
        rfp[g, 64:128, 64:128] = rf[2 * g + 1]
    wouta = np.ascontiguousarray(np.asarray(inputs["L0_attn_a_W_out"], np.float32))
    b_out_a = np.asarray(inputs["L0_attn_a_b_out"], np.float32)
    wqa = np.ascontiguousarray(np.asarray(inputs["L0_attn_b_Wqa"], np.float32))
    wqb = np.ascontiguousarray(np.asarray(inputs["L0_attn_b_Wqb"], np.float32))
    wkva = np.ascontiguousarray(np.asarray(inputs["L0_attn_b_Wkva"], np.float32))
    wkvb = np.asarray(inputs["L0_attn_b_Wkvb"], np.float32).reshape(32, 4, 64, 2)
    wkb = np.zeros((32, 512), np.float32)
    for g in range(4):
        wkb[:, g * 128: g * 128 + 64] = wkvb[:, g, :, 0]
        wkb[:, g * 128 + 64: g * 128 + 128] = wkvb[:, g, :, 0]
    wvb = np.ascontiguousarray(wkvb[..., 1].reshape(32, 256))
    woutb = np.ascontiguousarray(np.asarray(inputs["L0_attn_b_Wout"], np.float32))
    b_out_b = np.asarray(inputs["L0_attn_b_bout"], np.float32)
    wrt = np.ascontiguousarray(np.asarray(inputs["L0_ffn_W_r"], np.float32).T)
    wkt = np.ascontiguousarray(np.asarray(inputs["L0_ffn_W_k"], np.float32).T)
    wvt = np.ascontiguousarray(np.asarray(inputs["L0_ffn_W_v"], np.float32).T)
    mu_r = np.asarray(inputs["L0_ffn_mu_r"], np.float32)
    mu_k = np.asarray(inputs["L0_ffn_mu_k"], np.float32)
    bmu = np.zeros((128, 32), np.float32)
    bmu[:, 0:8] = b_out_a.reshape(8, 128).T
    bmu[:, 8:16] = b_out_b.reshape(8, 128).T
    bmu[:, 16:24] = mu_r.reshape(8, 128).T
    bmu[:, 24:32] = mu_k.reshape(8, 128).T

    D = 64
    inv = (1.0 / (10000.0 ** (np.arange(0, D, 2, dtype=np.float32) / np.float32(D)))).astype(np.float32)
    tpos = np.arange(T, dtype=np.float32)
    fr = np.outer(tpos, inv).astype(np.float32)
    cos64 = np.cos(fr).astype(np.float32).T
    sin64 = np.sin(fr).astype(np.float32).T
    cosm = np.repeat(cos64, 2, axis=0)
    sinm = np.repeat(sin64, 2, axis=0)
    cosm = np.tile(cosm, (2, 1))
    sinm = np.tile(sinm, (2, 1))
    s64 = np.zeros((64, 64), np.float32)
    for i2 in range(32):
        s64[2 * i2, 2 * i2 + 1] = 1.0
        s64[2 * i2 + 1, 2 * i2] = -1.0
    s128t = np.zeros((128, 128), np.float32)
    s128t[0:64, 0:64] = s64
    s128t[64:128, 64:128] = s64
    maskd = (np.arange(128)[:, None] <= np.arange(128)[None, :]).astype(np.float32)
    sel2 = np.zeros((2, 128), np.float32)
    sel2[0, 0:64] = 1.0
    sel2[1, 64:128] = 1.0

    com = dict(wqkv=wqkv, rf=rfp, wouta=wouta, wqa=wqa, wqb=wqb, wkva=wkva,
               wkb=wkb, wvb=wvb, woutb=woutb, wrt=wrt, wkt=wkt, wvt=wvt,
               bmu=bmu, cosm=cosm, sinm=sinm, s128t=s128t, maskd=maskd,
               sel2=sel2)
    return x, com


def get_nc(tap=None):
    key = ("nc", tap)
    if key not in _CACHE:
        _CACHE[key] = MK(tap=tap).build()
    return _CACHE[key]


def kernel(**inputs):
    x, com = _host_prep(inputs)
    nc = get_nc()
    in_maps = []
    for k in range(NCORES):
        m = {"x": np.ascontiguousarray(x[k * NSEQ:(k + 1) * NSEQ])}
        m.update(com)
        in_maps.append(m)
    res = run_bass_kernel_spmd(nc, in_maps, core_ids=list(range(NCORES)))
    y = np.stack([np.asarray(r["y"]) for r in res.results])
    y = y.reshape(B, C, T).transpose(0, 2, 1)
    return np.ascontiguousarray(y.astype(np.float32))
